# version 06
# baseline (speedup 1.0000x reference)
import numpy as np
from contextlib import ExitStack

import concourse.bass as bass
import concourse.mybir as mybir
from concourse.bass_utils import run_bass_kernel_spmd

F32 = mybir.dt.float32
BF16 = mybir.dt.bfloat16
AF = mybir.ActivationFunctionType
ALU = mybir.AluOpType
AX = mybir.AxisListType

P = 128
D = 2048
HT = D // P
SEQ = 16384
TC = 4096
NT = SEQ // P
NB = SEQ // 512
I5 = 5632
ITI = I5 // P
CH = 512
NCH = SEQ // CH
KCH = 1024
NKC = SEQ // KCH
QB = 512
NQB = TC // QB
NU = NQB * NCH * 4

ENGINE_DMA_SEMS = {
    "gpsimd": ["dgp_misc", "x0", "x1", "x2", "x3", "xw0", "xw1", "xr0", "xr1", "xb",
               "xo", "pw", "qt60", "qt61", "pt6", "ot6"],
    "sync": ["wl0", "wl1", "dw0", "dw1", "dw2", "dw3", "sg0", "sg1",
             "su0", "su1", "dts0", "dts1", "k60", "k61", "v60", "v61"],
    "scalar": ["ks0", "ks1", "vs0", "vs1", "qs0", "qs1", "st0", "st1",
               "dta0", "dta1"],
}


class Phase:
    def __init__(self, sems):
        self.sems = sems
        self.marks = {}
        self.tot = {k: 0 for k in sems}
        self.bar_prev = 0


CHAIN_SEM = {"vector": "vec", "scalar": "act", "gpsimd": "gp"}


class Gen:
    __slots__ = ("ph", "eng", "cnt", "chain", "last", "last_waited")

    def __init__(self, ph, eng, cnt, engine_name):
        self.ph = ph
        self.eng = eng
        self.cnt = cnt
        self.chain = CHAIN_SEM.get(engine_name)
        self.last = None
        self.last_waited = cnt.get(self.chain, 0) if self.chain else 0

    def do(self, f, dma=False):
        c = self.chain
        if c is not None and self.cnt[c] > self.last_waited:
            if self.eng is not None:
                self.eng.wait_ge(self.ph.sems[c], self.cnt[c])
            self.last_waited = self.cnt[c]
        if self.eng is not None:
            self.last = f(self.eng)
        if c is not None and not dma:
            self.cnt[c] += 1
            if self.eng is not None:
                self.last.then_inc(self.ph.sems[c], 1)
        return self.last

    def dma(self, f):
        return self.do(f, dma=True)

    def inc(self, sem, n=1, tag=None):
        self.cnt[sem] += n
        if self.eng is None:
            if tag is not None:
                key = (sem,) + tag
                assert key not in self.ph.marks, key
                self.ph.marks[key] = self.cnt[sem]
        else:
            self.last.then_inc(self.ph.sems[sem], n)

    def mark(self, sem, tag):
        if self.eng is None:
            key = (sem,) + tag
            assert key not in self.ph.marks, key
            self.ph.marks[key] = self.cnt[sem]

    def wait(self, sem, tag):
        if self.eng is not None:
            self.eng.wait_ge(self.ph.sems[sem], self.ph.marks[(sem,) + tag])


def run_phase(block, ph, emitters):
    base = dict(ph.tot)
    for name in ("tensor", "vector", "scalar", "sync", "gpsimd"):
        fn = emitters.get(name)
        if fn:
            fn(Gen(ph, None, ph.tot, name))
    ph.tot["bar"] += 5
    bar_target = ph.tot["bar"]
    prev = ph.bar_prev

    def mk(name, fn):
        def body(eng):
            if prev > 0:
                eng.wait_ge(ph.sems["bar"], prev)
            if fn:
                g2 = Gen(ph, eng, dict(base), name)
                fn(g2)
            for s in ENGINE_DMA_SEMS.get(name, []):
                if ph.tot[s] > 0:
                    eng.wait_ge(ph.sems[s], ph.tot[s])
            cs = CHAIN_SEM.get(name)
            if cs is not None and ph.tot[cs] > 0:
                eng.wait_ge(ph.sems[cs], ph.tot[cs])
            eng.nop().then_inc(ph.sems["bar"], 1)
        return body

    block.tensor(mk("tensor", emitters.get("tensor")))
    block.vector(mk("vector", emitters.get("vector")))
    block.scalar(mk("scalar", emitters.get("scalar")))
    block.sync(mk("sync", emitters.get("sync")))
    block.gpsimd(mk("gpsimd", emitters.get("gpsimd")))
    ph.bar_prev = bar_target


def tpos(tt8):
    return (tt8 // 4) * 512 + (tt8 % 4) * 128


def build_program(debug=False):
    nc = bass.Bass("TRN2", target_bir_lowering=False)

    x_full = nc.dram_tensor("x_full", [SEQ, D], F32, kind="ExternalInput").ap()
    wq = nc.dram_tensor("wq", [D, D], F32, kind="ExternalInput").ap()
    wk = nc.dram_tensor("wk", [D, D], F32, kind="ExternalInput").ap()
    wv = nc.dram_tensor("wv", [D, D], F32, kind="ExternalInput").ap()
    bq = nc.dram_tensor("bq", [D], F32, kind="ExternalInput").ap()
    bk = nc.dram_tensor("bk", [D], F32, kind="ExternalInput").ap()
    bv = nc.dram_tensor("bv", [D], F32, kind="ExternalInput").ap()
    gw = nc.dram_tensor("gw", [I5, D], F32, kind="ExternalInput").ap()
    uw = nc.dram_tensor("uw", [I5, D], F32, kind="ExternalInput").ap()
    dw = nc.dram_tensor("dw", [D, I5], F32, kind="ExternalInput").ap()
    gs = nc.dram_tensor("gs", [I5], F32, kind="ExternalInput").ap()
    us = nc.dram_tensor("us", [I5], F32, kind="ExternalInput").ap()
    ds = nc.dram_tensor("ds", [D], F32, kind="ExternalInput").ap()
    out_e = nc.dram_tensor("out", [TC, D], F32, kind="ExternalOutput").ap()

    xnt_d = nc.dram_tensor("xnt_d", [NB, P, 16 * 512], BF16).ap()
    k_d = nc.dram_tensor("k_d", [NKC, P, HT * KCH], BF16).ap()
    v_d = nc.dram_tensor("v_d", [NKC, P, 8 * D], BF16).ap()
    qt_d = nc.dram_tensor("qt_d", [NQB, P, HT * 512], BF16).ap()
    gt_d = nc.dram_tensor("gt_d", [ITI, P, D], BF16).ap()
    ut_d = nc.dram_tensor("ut_d", [ITI, P, D], BF16).ap()
    dt_d = nc.dram_tensor("dt_d", [ITI, P, D], BF16).ap()
    part_d = nc.dram_tensor("part_d", [TC // P, P, D], F32).ap()

    dbg = {}
    if debug:
        for nm, src in (("dbg_xnt", xnt_d), ("dbg_k", k_d), ("dbg_v", v_d),
                        ("dbg_qt", qt_d), ("dbg_gt", gt_d), ("dbg_dt", dt_d),
                        ("dbg_part", part_d)):
            dbg[nm] = (nc.dram_tensor(nm, list(src.shape), src.dtype,
                                      kind="ExternalOutput").ap(), src)

    with ExitStack() as G:
        E = G.enter_context
        sems = {}
        sem_names = (["bar", "gp", "pe", "vec", "act", "fin", "dgp_misc",
                      "x0", "x1", "x2", "x3", "xw0", "xw1", "xr0", "xr1", "xb", "xo",
                      "pw", "qt60", "qt61", "pt6", "ot6",
                      "wl0", "wl1", "dw0", "dw1", "dw2", "dw3",
                      "sg0", "sg1", "su0", "su1", "dts0", "dts1",
                      "k60", "k61", "v60", "v61",
                      "ks0", "ks1", "vs0", "vs1", "qs0", "qs1",
                      "st0", "st1", "dta0", "dta1"])
        for s in sem_names:
            sems[s] = E(nc.semaphore(s))
        ph = Phase(sems)

        ident = E(nc.sbuf_tensor("ident", [P, P], BF16))
        identf = E(nc.sbuf_tensor("identf", [P, P], F32))
        ones = E(nc.sbuf_tensor("ones", [1, P], F32))
        bqc = E(nc.sbuf_tensor("bqc", [P, HT], F32))
        bkc = E(nc.sbuf_tensor("bkc", [P, HT], F32))
        gsc = E(nc.sbuf_tensor("gsc", [P, ITI], F32))
        usc = E(nc.sbuf_tensor("usc", [P, ITI], F32))
        dsc_sb = E(nc.sbuf_tensor("dsc_sb", [P, HT], F32))

        block = E(nc.Block())

        with ExitStack() as C:
            st44 = C.enter_context(nc.sbuf_tensor("st44", [44, P], F32))
            ps0 = C.enter_context(nc.psum_tensor("ps0", [P, 256], F32))

            loads = [
                (bq.rearrange("(a f) -> a f", a=16), 16, bqc),
                (bk.rearrange("(a f) -> a f", a=16), 16, bkc),
                (ds.rearrange("(a f) -> a f", a=16), 16, dsc_sb),
                (gs.rearrange("(a f) -> a f", a=44), 44, gsc),
                (us.rearrange("(a f) -> a f", a=44), 44, usc),
            ]

            def p0_gp(g):
                g.do(lambda e: e.memset(ident[:], 0.0))
                g.do(lambda e: e.memset(identf[:], 0.0))
                g.do(lambda e: e.memset(ones[:], 1.0))
                g.do(lambda e: e.affine_select(
                    out=ident[:], in_=ident[:], compare_op=ALU.not_equal,
                    fill=1.0, base=0, pattern=[[-1, P]], channel_multiplier=1))
                g.do(lambda e: e.affine_select(
                    out=identf[:], in_=identf[:], compare_op=ALU.not_equal,
                    fill=1.0, base=0, pattern=[[-1, P]], channel_multiplier=1))
                g.mark("gp", ("p0", "const"))
                for j, (src, rows, _) in enumerate(loads):
                    if j > 0:
                        g.wait("pe", ("p0", "tr", j - 1))
                    g.dma(lambda e, src=src, rows=rows:
                         e.dma_start(out=st44[0:rows, :], in_=src))
                    g.inc("dgp_misc", 16, ("p0", "ld", j))

            def p0_pe(g):
                g.wait("gp", ("p0", "const"))
                off = 0
                for j, (_, rows, _) in enumerate(loads):
                    g.wait("dgp_misc", ("p0", "ld", j))
                    if j > 0:
                        g.wait("vec", ("p0", "ev", j - 1))
                    g.do(lambda e, off=off, rows=rows: e.transpose(
                        ps0[:, off:off + rows], st44[0:rows, :],
                        identf[0:rows, 0:rows]))
                    g.inc("pe", 1, ("p0", "tr", j))
                    off += rows

            def p0_vec(g):
                off = 0
                for j, (_, rows, dest) in enumerate(loads):
                    g.wait("pe", ("p0", "tr", j))
                    g.do(lambda e, off=off, rows=rows, dest=dest:
                         e.tensor_copy(dest[:, 0:rows], ps0[:, off:off + rows]))
                    g.mark("vec", ("p0", "ev", j))
                    off += rows

            run_phase(block, ph, {"gpsimd": p0_gp, "tensor": p0_pe,
                                  "vector": p0_vec})

        with ExitStack() as C1:
            E1 = C1.enter_context
            WT = E1(nc.sbuf_tensor("WT", [P, HT * D], BF16))
            wrow = [E1(nc.sbuf_tensor(f"wrow{i}", [P, D], F32)) for i in range(2)]
            wbf = [E1(nc.sbuf_tensor(f"wbf{i}", [P, D], BF16)) for i in range(2)]
            xs = [E1(nc.sbuf_tensor(f"xs{i}", [P, D], F32)) for i in range(4)]
            sqs = E1(nc.sbuf_tensor("sqs", [P, D], F32))
            xnbf = [E1(nc.sbuf_tensor(f"xnbf{i}", [P, D], BF16)) for i in range(2)]
            stage = [E1(nc.sbuf_tensor(f"stage{i}", [P, 16 * 512], BF16)) for i in range(2)]
            ktbf = [E1(nc.sbuf_tensor(f"ktbf{i}", [P, 512], BF16)) for i in range(2)]
            vbf = [E1(nc.sbuf_tensor(f"vbf{i}", [P, D], BF16)) for i in range(2)]
            qtbf = [E1(nc.sbuf_tensor(f"qtbf{i}", [P, 512], BF16)) for i in range(2)]
            bvb = E1(nc.sbuf_tensor("bvb", [P, D], F32))
            bvrow = E1(nc.sbuf_tensor("bvrow", [1, D], F32))
            stats = {nm: [E1(nc.sbuf_tensor(f"{nm}{i}", [P, 1], F32))
                          for i in range(4)]
                     for nm in ("sumv", "ssq", "mb", "t2", "t3", "stdv", "rstd")}

            def wt_build(pn, w_ap, extra_pe=None, extra_vec=None, extra_gp=None):
                def f_sync(g):
                    for ot in range(HT):
                        if ot >= 2:
                            g.wait("vec", (pn, "cast", ot - 2))
                        g.dma(lambda e, ot=ot: e.dma_start(
                            out=wrow[ot % 2][:], in_=w_ap[ot * P:(ot + 1) * P, :]))
                        g.inc(f"wl{ot % 2}", 16, (pn, "wl", ot))

                def f_vec(g):
                    for ot in range(HT):
                        g.wait(f"wl{ot % 2}", (pn, "wl", ot))
                        if ot >= 2:
                            g.wait("pe", (pn, "tr", ot - 2))
                        g.do(lambda e, ot=ot: e.tensor_copy(
                            wbf[ot % 2][:], wrow[ot % 2][:]))
                        g.mark("vec", (pn, "cast", ot))
                        g.wait("pe", (pn, "tr", ot))
                        dst = WT.ap().rearrange("p (ht o) -> p ht o", ht=HT)
                        g.do(lambda e, ot=ot, dst=dst: e.tensor_copy(
                            dst[:, :, ot * P:(ot + 1) * P],
                            wt_ps.ap().rearrange("p (ht o) -> p ht o", ht=HT)))
                        g.mark("vec", (pn, "wtev", ot))
                    if extra_vec:
                        extra_vec(g)

                def f_pe(g):
                    g.wait("gp", ("p0", "const"))
                    g.wait("vec", ("p0", "ev", 4))
                    for ot in range(HT):
                        g.wait("vec", (pn, "cast", ot))
                        if ot >= 1:
                            g.wait("vec", (pn, "wtev", ot - 1))
                        for ht in range(HT):
                            g.do(lambda e, ot=ot, ht=ht: e.transpose(
                                wt_ps[:, ht * P:(ht + 1) * P],
                                wbf[ot % 2][:, ht * P:(ht + 1) * P], ident[:]))
                        g.inc("pe", 1, (pn, "tr", ot))
                    if extra_pe:
                        extra_pe(g)

                return {"sync": f_sync, "vector": f_vec, "tensor": f_pe,
                        "gpsimd": extra_gp}

            with ExitStack() as CP:
                wt_ps = CP.enter_context(nc.psum_tensor("wt_ps1", [P, D], BF16))
                run_phase(block, ph, wt_build("p1a", wk))

            with ExitStack() as CP:
                xnT_ps = CP.enter_context(nc.psum_tensor("xnT_ps", [P, D], BF16))
                kt_ps = [CP.enter_context(nc.psum_tensor(f"kt_ps{i}", [P, 512], F32))
                         for i in range(2)]

                def p1b_gp(g):
                    for tb in range(NB):
                        for tt in range(4):
                            i = 4 * tb + tt
                            if i >= 4:
                                g.wait("vec", ("p1b", "xn", i - 4))
                            g.dma(lambda e, i=i: e.dma_start(
                                out=xs[i % 4][:], in_=x_full[i * P:(i + 1) * P, :]))
                            g.inc(f"x{i % 4}", 16, ("p1b", "x", i))
                        g.wait("vec", ("p1b", "sev", 4 * tb + 3))
                        g.dma(lambda e, tb=tb: e.dma_start(
                            out=xnt_d[tb], in_=stage[tb % 2][:]))
                        g.inc(f"xw{tb % 2}", 16, ("p1b", "xw", tb))

                def p1b_vec(g):
                    for tb in range(NB):
                        for tt in range(4):
                            i = 4 * tb + tt
                            p = i % 4
                            g.wait(f"x{i % 4}", ("p1b", "x", i))
                            if i >= 4:
                                g.wait("act", ("p1b", "sqrt", i - 4))
                            g.do(lambda e, p=p: e.reduce_sum(
                                stats["sumv"][p][:], xs[p][:], axis=AX.X))
                            g.do(lambda e, p=p: e.scalar_tensor_tensor(
                                sqs[:], xs[p][:], 1.0, xs[p][:],
                                ALU.mult, ALU.mult,
                                accum_out=stats["ssq"][p][:]))
                            g.do(lambda e, p=p: e.tensor_scalar_mul(
                                stats["mb"][p][:], stats["sumv"][p][:], 1.0 / D))
                            g.do(lambda e, p=p: e.tensor_mul(
                                stats["t2"][p][:], stats["mb"][p][:],
                                stats["sumv"][p][:]))
                            g.do(lambda e, p=p: e.tensor_sub(
                                stats["t3"][p][:], stats["ssq"][p][:],
                                stats["t2"][p][:]))
                            g.do(lambda e, p=p: e.tensor_scalar(
                                stats["t3"][p][:], stats["t3"][p][:],
                                1.0 / D, 1e-5, ALU.mult, ALU.add))
                            g.mark("vec", ("p1b", "t3", i))
                        for tt in range(4):
                            i = 4 * tb + tt
                            p = i % 4
                            g.wait("act", ("p1b", "sqrt", i))
                            g.do(lambda e, p=p: e.reciprocal(
                                stats["rstd"][p][:], stats["stdv"][p][:]))
                            g.do(lambda e, p=p: e.tensor_scalar(
                                xnbf[i % 2][:], xs[p][:], stats["mb"][p][:, 0:1],
                                stats["rstd"][p][:, 0:1],
                                ALU.subtract, ALU.mult))
                            g.mark("vec", ("p1b", "xn", i))
                            g.wait("pe", ("p1b", "tr", i))
                            if tt == 0 and tb >= 2:
                                g.wait("pe", ("p1b", "k", (tb - 2) * HT + HT - 1))
                                g.wait(f"xw{tb % 2}", ("p1b", "xw", tb - 2))
                            dst = stage[tb % 2].ap().rearrange(
                                "p (ht t) -> p ht t", ht=HT)
                            g.do(lambda e, tt=tt, dst=dst: e.tensor_copy(
                                dst[:, :, tt * P:(tt + 1) * P],
                                xnT_ps.ap().rearrange("p (ht t) -> p ht t", ht=HT)))
                            g.mark("vec", ("p1b", "sev", i))

                def p1b_act(g):
                    g.wait("vec", ("p0", "ev", 1))
                    for tb in range(NB):
                        for tt in range(4):
                            i = 4 * tb + tt
                            p = i % 4
                            g.wait("vec", ("p1b", "t3", i))
                            if i >= 4:
                                g.wait("vec", ("p1b", "xn", i - 4))
                            g.do(lambda e, p=p: e.activation(
                                stats["stdv"][p][:], stats["t3"][p][:], AF.Sqrt))
                            g.mark("act", ("p1b", "sqrt", i))
                            if tb >= 1:
                                for e4 in range(4):
                                    ot = 4 * tt + e4
                                    kg = (tb - 1) * HT + ot
                                    g.wait("pe", ("p1b", "k", kg))
                                    if kg >= 2:
                                        g.wait(f"ks{kg % 2}", ("p1b", "ks", kg - 2))
                                    g.do(lambda e, kg=kg, ot=ot: e.activation(
                                        ktbf[kg % 2][:], kt_ps[ot % 2][:],
                                        AF.Identity, bias=bkc[:, ot:ot + 1]))
                                    g.mark("act", ("p1b", "ktev", kg))
                                    tb1 = tb - 1
                                    g.dma(lambda e, kg=kg, tb1=tb1, ot=ot:
                                         e.dma_start(
                                             out=k_d[tb1 // 2, :,
                                                     ot * KCH + (tb1 % 2) * 512:
                                                     ot * KCH + (tb1 % 2) * 512 + 512],
                                             in_=ktbf[kg % 2][:]))
                                    g.inc(f"ks{kg % 2}", 16, ("p1b", "ks", kg))
                    tb1 = NB - 1
                    for ot in range(HT):
                        kg = tb1 * HT + ot
                        g.wait("pe", ("p1b", "k", kg))
                        if kg >= 2:
                            g.wait(f"ks{kg % 2}", ("p1b", "ks", kg - 2))
                        g.do(lambda e, kg=kg, ot=ot: e.activation(
                            ktbf[kg % 2][:], kt_ps[ot % 2][:], AF.Identity,
                            bias=bkc[:, ot:ot + 1]))
                        g.mark("act", ("p1b", "ktev", kg))
                        g.dma(lambda e, kg=kg, tb1=tb1, ot=ot: e.dma_start(
                            out=k_d[tb1 // 2, :,
                                    ot * KCH + (tb1 % 2) * 512:
                                    ot * KCH + (tb1 % 2) * 512 + 512],
                            in_=ktbf[kg % 2][:]))
                        g.inc(f"ks{kg % 2}", 16, ("p1b", "ks", kg))

                def p1b_pe(g):
                    g.wait("vec", ("p1a", "wtev", HT - 1))
                    for tb in range(NB):
                        for tt in range(4):
                            i = 4 * tb + tt
                            g.wait("vec", ("p1b", "xn", i))
                            if i >= 1:
                                g.wait("vec", ("p1b", "sev", i - 1))
                            for ht in range(HT):
                                g.do(lambda e, i=i, ht=ht: e.transpose(
                                    xnT_ps[:, ht * P:(ht + 1) * P],
                                    xnbf[i % 2][:, ht * P:(ht + 1) * P],
                                    ident[:]))
                            g.inc("pe", 1, ("p1b", "tr", i))
                        g.wait("vec", ("p1b", "sev", 4 * tb + 3))
                        for ot in range(HT):
                            kg = tb * HT + ot
                            if kg >= 2:
                                g.wait("act", ("p1b", "ktev", kg - 2))
                            for ht in range(HT):
                                g.do(lambda e, ot=ot, ht=ht, tb=tb: e.matmul(
                                    kt_ps[ot % 2][:],
                                    WT[:, ht * D + ot * P: ht * D + (ot + 1) * P],
                                    stage[tb % 2][:, ht * 512:(ht + 1) * 512],
                                    start=(ht == 0), stop=(ht == HT - 1)))
                            g.inc("pe", 1, ("p1b", "k", kg))

                run_phase(block, ph, {"gpsimd": p1b_gp, "vector": p1b_vec,
                                      "scalar": p1b_act, "tensor": p1b_pe})

            with ExitStack() as CP:
                wt_ps = CP.enter_context(nc.psum_tensor("wt_ps2", [P, D], BF16))
                pbv = CP.enter_context(nc.psum_tensor("pbv", [P, D], F32))

                def p2a_gp(g):
                    g.dma(lambda e: e.dma_start(
                        out=bvrow[:], in_=bv.rearrange("(a f) -> a f", a=1)))
                    g.inc("dgp_misc", 16, ("p2a", "bv"))

                def p2a_pe_extra(g):
                    g.wait("dgp_misc", ("p2a", "bv"))
                    for k in range(4):
                        g.do(lambda e, k=k: e.matmul(
                            pbv[:, k * 512:(k + 1) * 512], ones[0:1, :],
                            bvrow[0:1, k * 512:(k + 1) * 512],
                            start=True, stop=True))
                    g.inc("pe", 1, ("p2a", "bvmm"))

                def p2a_vec_extra(g):
                    g.wait("pe", ("p2a", "bvmm"))
                    g.do(lambda e: e.tensor_copy(bvb[:], pbv[:]))
                    g.mark("vec", ("p2a", "bvev"))

                run_phase(block, ph, wt_build("p2a", wv, extra_pe=p2a_pe_extra,
                                              extra_vec=p2a_vec_extra,
                                              extra_gp=p2a_gp))

            with ExitStack() as CP:
                v_ps = [CP.enter_context(nc.psum_tensor(f"v_ps{i}", [P, D], F32))
                        for i in range(2)]

                def p2b_gp(g):
                    for tb in range(NB):
                        if tb >= 2:
                            g.wait("pe", ("p2b", "v", 4 * (tb - 2) + 3))
                        g.dma(lambda e, tb=tb: e.dma_start(
                            out=stage[tb % 2][:], in_=xnt_d[tb]))
                        g.inc(f"xr{tb % 2}", 16, ("p2b", "xr", tb))

                def p2b_pe(g):
                    g.wait("vec", ("p2a", "wtev", HT - 1))
                    g.wait("vec", ("p2a", "bvev"))
                    for tb in range(NB):
                        g.wait(f"xr{tb % 2}", ("p2b", "xr", tb))
                        for tt in range(4):
                            vg = 4 * tb + tt
                            if vg >= 2:
                                g.wait("vec", ("p2b", "vev", vg - 2))
                            for ht in range(HT):
                                for oc in range(4):
                                    g.do(lambda e, vg=vg, tb=tb, tt=tt, ht=ht,
                                         oc=oc: e.matmul(
                                             v_ps[vg % 2][:, oc * 512:(oc + 1) * 512],
                                             stage[tb % 2][:, ht * 512 + tt * P:
                                                           ht * 512 + (tt + 1) * P],
                                             WT[:, ht * D + oc * 512:
                                                ht * D + (oc + 1) * 512],
                                             start=(ht == 0), stop=(ht == HT - 1)))
                            g.inc("pe", 1, ("p2b", "v", vg))

                def p2b_vec(g):
                    for vg in range(NT):
                        g.wait("pe", ("p2b", "v", vg))
                        if vg >= 2:
                            g.wait(f"vs{vg % 2}", ("p2b", "vs", vg - 2))
                        g.do(lambda e, vg=vg: e.tensor_add(
                            vbf[vg % 2][:], v_ps[vg % 2][:], bvb[:]))
                        g.mark("vec", ("p2b", "vev", vg))

                def p2b_act(g):
                    for vg in range(NT):
                        g.wait("vec", ("p2b", "vev", vg))
                        g.dma(lambda e, vg=vg: e.dma_start(
                            out=v_d[vg // 8, :, (vg % 8) * D:(vg % 8 + 1) * D],
                            in_=vbf[vg % 2][:]))
                        g.inc(f"vs{vg % 2}", 16, ("p2b", "vs", vg))

                run_phase(block, ph, {"gpsimd": p2b_gp, "tensor": p2b_pe,
                                      "vector": p2b_vec, "scalar": p2b_act})

            with ExitStack() as CP:
                wt_ps = CP.enter_context(nc.psum_tensor("wt_ps3", [P, D], BF16))
                run_phase(block, ph, wt_build("p3a", wq))

            with ExitStack() as CP:
                qt_ps = [CP.enter_context(nc.psum_tensor(f"qt_ps{i}", [P, 512], F32))
                         for i in range(2)]

                def p3b_gp(g):
                    for qc in range(NQB):
                        if qc >= 2:
                            g.wait("pe", ("p3b", "q", (qc - 2) * HT + HT - 1))
                        g.dma(lambda e, qc=qc: e.dma_start(
                            out=stage[qc % 2][:], in_=xnt_d[qc]))
                        g.inc(f"xr{qc % 2}", 16, ("p3b", "xr", qc))

                def p3b_pe(g):
                    g.wait("vec", ("p3a", "wtev", HT - 1))
                    for qc in range(NQB):
                        g.wait(f"xr{qc % 2}", ("p3b", "xr", qc))
                        for ot in range(HT):
                            k = qc * HT + ot
                            if k >= 2:
                                g.wait("act", ("p3b", "qev", k - 2))
                            for ht in range(HT):
                                g.do(lambda e, k=k, qc=qc, ot=ot, ht=ht:
                                     e.matmul(
                                         qt_ps[k % 2][:],
                                         WT[:, ht * D + ot * P:
                                            ht * D + (ot + 1) * P],
                                         stage[qc % 2][:, ht * 512:(ht + 1) * 512],
                                         start=(ht == 0), stop=(ht == HT - 1)))
                            g.inc("pe", 1, ("p3b", "q", k))

                def p3b_act(g):
                    g.wait("vec", ("p0", "ev", 0))
                    for qc in range(NQB):
                        for ot in range(HT):
                            k = qc * HT + ot
                            g.wait("pe", ("p3b", "q", k))
                            if k >= 2:
                                g.wait(f"qs{k % 2}", ("p3b", "qs", k - 2))
                            g.do(lambda e, k=k, ot=ot: e.activation(
                                qtbf[k % 2][:], qt_ps[k % 2][:], AF.Identity,
                                bias=bqc[:, ot:ot + 1]))
                            g.mark("act", ("p3b", "qev", k))
                            g.dma(lambda e, k=k, qc=qc, ot=ot: e.dma_start(
                                out=qt_d[qc, :, ot * 512:(ot + 1) * 512],
                                in_=qtbf[k % 2][:]))
                            g.inc(f"qs{k % 2}", 16, ("p3b", "qs", k))

                run_phase(block, ph, {"gpsimd": p3b_gp, "tensor": p3b_pe,
                                      "scalar": p3b_act})

        with ExitStack() as C2:
            E2 = C2.enter_context
            gw_st = [E2(nc.sbuf_tensor(f"gw_st{i}", [P, D], F32)) for i in range(2)]
            gubf = [E2(nc.sbuf_tensor(f"gubf{i}", [P, D], BF16)) for i in range(2)]
            gu_stage = [E2(nc.sbuf_tensor(f"gu_stage{i}", [P, D], BF16))
                        for i in range(2)]
            with ExitStack() as CP:
                gt_ps = [CP.enter_context(nc.psum_tensor(f"gt_ps{i}", [P, D], BF16))
                         for i in range(2)]
                NK = 2 * ITI

                def p4a_sync(g):
                    for k in range(NK):
                        it, isup = k // 2, k % 2
                        if k >= 2:
                            g.wait("vec", ("p4a", "cast", k - 2))
                        src = uw if isup else gw
                        g.dma(lambda e, k=k, it=it, src=src: e.dma_start(
                            out=gw_st[k % 2][:], in_=src[it * P:(it + 1) * P, :]))
                        g.inc(f"wl{k % 2}", 16, ("p4a", "wl", k))

                def p4a_vec(g):
                    for k in range(NK):
                        it, isup = k // 2, k % 2
                        g.wait(f"wl{k % 2}", ("p4a", "wl", k))
                        if k >= 2:
                            g.wait("pe", ("p4a", "tr", k - 2))
                        sc = usc if isup else gsc
                        g.do(lambda e, k=k, it=it, sc=sc: e.tensor_scalar_mul(
                            gubf[k % 2][:], gw_st[k % 2][:], sc[:, it:it + 1]))
                        g.mark("vec", ("p4a", "cast", k))
                        g.wait("pe", ("p4a", "tr", k))
                        if k >= 2:
                            g.wait(f"st{k % 2}", ("p4a", "st", k - 2))
                        g.do(lambda e, k=k: e.tensor_copy(
                            gu_stage[k % 2][:], gt_ps[k % 2][:]))
                        g.mark("vec", ("p4a", "ev", k))

                def p4a_pe(g):
                    g.wait("gp", ("p0", "const"))
                    g.wait("act", ("p3b", "qev", NQB * HT - 1))
                    for k in range(NK):
                        g.wait("vec", ("p4a", "cast", k))
                        if k >= 2:
                            g.wait("vec", ("p4a", "ev", k - 2))
                        for ht in range(HT):
                            g.do(lambda e, k=k, ht=ht: e.transpose(
                                gt_ps[k % 2][:, ht * P:(ht + 1) * P],
                                gubf[k % 2][:, ht * P:(ht + 1) * P], ident[:]))
                        g.inc("pe", 1, ("p4a", "tr", k))

                def p4a_act(g):
                    for k in range(NK):
                        it, isup = k // 2, k % 2
                        g.wait("vec", ("p4a", "ev", k))
                        dst = ut_d if isup else gt_d
                        g.dma(lambda e, k=k, it=it, dst=dst: e.dma_start(
                            out=dst[it], in_=gu_stage[k % 2][:]))
                        g.inc(f"st{k % 2}", 16, ("p4a", "st", k))

                run_phase(block, ph, {"sync": p4a_sync, "vector": p4a_vec,
                                      "tensor": p4a_pe, "scalar": p4a_act})

        with ExitStack() as C2:
            E2 = C2.enter_context
            dw_st = [E2(nc.sbuf_tensor(f"dw_st{i}", [P, I5], F32)) for i in range(4)]
            dwbf = [E2(nc.sbuf_tensor(f"dwbf{i}", [P, I5], BF16)) for i in range(4)]
            dt_stage = [E2(nc.sbuf_tensor(f"dt_stage{i}", [P, 512], BF16))
                        for i in range(2)]
            with ExitStack() as CP:
                dt_ps = [CP.enter_context(nc.psum_tensor(f"dt_ps{i}", [P, 512], BF16))
                         for i in range(2)]

                def p4b_sync(g):
                    for h4 in range(4):
                        for kk in range(4):
                            if h4 >= 1:
                                g.wait("pe", ("p4b", "tr", h4 - 1, ITI - 1))
                            g.dma(lambda e, h4=h4, kk=kk: e.dma_start(
                                out=dw_st[kk][:],
                                in_=dw[(h4 * 4 + kk) * P:(h4 * 4 + kk + 1) * P, :]))
                            g.inc(f"dw{kk}", 16, ("p4b", "wl", h4, kk))

                def p4b_vec(g):
                    for h4 in range(4):
                        for kk in range(4):
                            g.wait(f"dw{kk}", ("p4b", "wl", h4, kk))
                            if h4 >= 1:
                                g.wait("pe", ("p4b", "tr", h4 - 1, ITI - 1))
                            g.do(lambda e, h4=h4, kk=kk: e.tensor_scalar_mul(
                                dwbf[kk][:], dw_st[kk][:],
                                dsc_sb[:, h4 * 4 + kk:h4 * 4 + kk + 1]))
                            g.mark("vec", ("p4b", "cast", h4, kk))
                        for it in range(ITI):
                            f = h4 * ITI + it
                            g.wait("pe", ("p4b", "tr", h4, it))
                            if f >= 2:
                                g.wait(f"st{f % 2}", ("p4b", "st", f - 2))
                            g.do(lambda e, it=it: e.tensor_copy(
                                dt_stage[it % 2][:], dt_ps[it % 2][:]))
                            g.mark("vec", ("p4b", "ev", f))

                def p4b_pe(g):
                    g.wait("gp", ("p0", "const"))
                    g.wait("vec", ("p4a", "ev", 2 * ITI - 1))
                    for h4 in range(4):
                        for kk in range(4):
                            g.wait("vec", ("p4b", "cast", h4, kk))
                        for it in range(ITI):
                            f = h4 * ITI + it
                            if f >= 2:
                                g.wait("vec", ("p4b", "ev", f - 2))
                            for kk in range(4):
                                g.do(lambda e, it=it, kk=kk: e.transpose(
                                    dt_ps[it % 2][:, kk * P:(kk + 1) * P],
                                    dwbf[kk][:, it * P:(it + 1) * P], ident[:]))
                            g.inc("pe", 1, ("p4b", "tr", h4, it))

                def p4b_act(g):
                    for h4 in range(4):
                        for it in range(ITI):
                            f = h4 * ITI + it
                            g.wait("vec", ("p4b", "ev", f))
                            g.dma(lambda e, h4=h4, it=it: e.dma_start(
                                out=dt_d[it, :, h4 * 512:(h4 + 1) * 512],
                                in_=dt_stage[it % 2][:]))
                            g.inc(f"st{f % 2}", 16, ("p4b", "st", f))

                run_phase(block, ph, {"sync": p4b_sync, "vector": p4b_vec,
                                      "tensor": p4b_pe, "scalar": p4b_act})

        with ExitStack() as C3:
            E3 = C3.enter_context
            xnT_b = E3(nc.sbuf_tensor("xnT_b", [P, 2 * 16 * 512], BF16))
            hdn = E3(nc.sbuf_tensor("hdn", [P, ITI * 1024], BF16))
            gt_t = [E3(nc.sbuf_tensor(f"gt_t{i}", [P, D], BF16)) for i in range(2)]
            ut_t = [E3(nc.sbuf_tensor(f"ut_t{i}", [P, D], BF16)) for i in range(2)]
            silu_g = [E3(nc.sbuf_tensor(f"silu_g{i}", [P, 1024], F32))
                      for i in range(2)]
            dt_t = [E3(nc.sbuf_tensor(f"dt_t{i}", [P, D], BF16)) for i in range(4)]
            xo = [E3(nc.sbuf_tensor(f"xo{i}", [P, D], F32)) for i in range(2)]
            part_sb = [E3(nc.sbuf_tensor(f"part_sb{i}", [P, D], F32))
                       for i in range(2)]
            with ExitStack() as CP:
                pgu = CP.enter_context(nc.psum_tensor("pgu", [P, D], F32))
                pd = CP.enter_context(nc.psum_tensor("pd", [P, D], F32))
                NBLK = TC // 1024

                def p5_gp(g):
                    for blk in range(NBLK):
                        if blk >= 1:
                            g.wait("pe", ("p5", "gu", blk - 1, 2 * ITI - 1))
                        g.dma(lambda e, blk=blk: e.dma_start(
                            out=xnT_b[:, 0:8192], in_=xnt_d[2 * blk]))
                        g.inc("xb", 16)
                        g.dma(lambda e, blk=blk: e.dma_start(
                            out=xnT_b[:, 8192:16384], in_=xnt_d[2 * blk + 1]))
                        g.inc("xb", 16, ("p5", "xb", blk))
                        for pr in range(4):
                            if (blk, pr) != (0, 0):
                                pblk, ppr = (blk, pr - 1) if pr else (blk - 1, 3)
                                g.wait("vec", ("p5", "pev", pblk, ppr))
                            for e2 in range(2):
                                g128 = blk * 8 + 2 * pr + e2
                                g.dma(lambda e, g128=g128, e2=e2: e.dma_start(
                                    out=xo[e2][:],
                                    in_=x_full[g128 * P:(g128 + 1) * P, :]))
                                g.inc("xo", 16,
                                      ("p5", "xo", blk, pr) if e2 else None)
                            g.wait("vec", ("p5", "pev", blk, pr))
                            for e2 in range(2):
                                g128 = blk * 8 + 2 * pr + e2
                                g.dma(lambda e, g128=g128, e2=e2: e.dma_start(
                                    out=part_d[g128], in_=part_sb[e2][:]))
                                g.inc("pw", 16,
                                      ("p5", "pw", blk, pr) if e2 else None)

                def p5_sync(g):
                    for blk in range(NBLK):
                        for k in range(2 * ITI):
                            it, isup = k // 2, k % 2
                            if k >= 4 or blk > 0:
                                pk = (blk, k - 4) if k >= 4 else (blk - 1, 2 * ITI + k - 4)
                                g.wait("pe", ("p5", "gu", pk[0], pk[1]))
                            src = ut_d if isup else gt_d
                            buf = (ut_t if isup else gt_t)[it % 2]
                            g.dma(lambda e, src=src, it=it, buf=buf: e.dma_start(
                                out=buf[:], in_=src[it]))
                            g.inc(f"{'su' if isup else 'sg'}{(it) % 2}", 16,
                                  ("p5", "gu_l", blk, k))
                        for pr in range(4):
                            for it in range(0, ITI, 2):
                                if it >= 4 or (blk, pr) != (0, 0):
                                    if it >= 4:
                                        pt = ("p5", "dn", blk, pr, it - 4)
                                    elif pr:
                                        pt = ("p5", "dn", blk, pr - 1, ITI - 4 + it)
                                    else:
                                        pt = ("p5", "dn", blk - 1, 3, ITI - 4 + it)
                                    g.wait("pe", pt)
                                g.dma(lambda e, it=it: e.dma_start(
                                    out=dt_t[(it // 2) % 2][:], in_=dt_d[it]))
                                g.inc(f"dts{(it // 2) % 2}", 16,
                                      ("p5", "dt", blk, pr, it))

                def p5_act(g):
                    for blk in range(NBLK):
                        for it in range(ITI):
                            g.wait("pe", ("p5", "gu", blk, 2 * it))
                            if it >= 2:
                                g.wait("vec", ("p5", "hdn", blk, it - 2))
                            g.do(lambda e, it=it: e.activation(
                                silu_g[it % 2][:], pgu[:, 0:1024], AF.Silu))
                            g.mark("act", ("p5", "silu", blk, it))
                        for pr in range(4):
                            for it in range(1, ITI, 2):
                                if it >= 5 or (blk, pr) != (0, 0):
                                    if it >= 5:
                                        pt = ("p5", "dn", blk, pr, it - 4)
                                    elif pr:
                                        pt = ("p5", "dn", blk, pr - 1, ITI - 4 + it)
                                    else:
                                        pt = ("p5", "dn", blk - 1, 3, ITI - 4 + it)
                                    g.wait("pe", pt)
                                g.dma(lambda e, it=it: e.dma_start(
                                    out=dt_t[2 + ((it - 1) // 2) % 2][:],
                                    in_=dt_d[it]))
                                g.inc(f"dta{((it - 1) // 2) % 2}", 16,
                                      ("p5", "dta", blk, pr, it))

                def p5_pe(g):
                    g.wait("vec", ("p4b", "ev", 4 * ITI - 1))
                    for blk in range(NBLK):
                        g.wait("xb", ("p5", "xb", blk))
                        for k in range(2 * ITI):
                            it, isup = k // 2, k % 2
                            g.wait(f"{'su' if isup else 'sg'}{it % 2}",
                                   ("p5", "gu_l", blk, k))
                            if isup == 0:
                                if it >= 1:
                                    g.wait("act", ("p5", "silu", blk, it - 1))
                                elif blk >= 1:
                                    g.wait("vec", ("p5", "pev", blk - 1, 3))
                            else:
                                if it >= 1:
                                    g.wait("vec", ("p5", "hdn", blk, it - 1))
                            buf = (ut_t if isup else gt_t)[it % 2]
                            dst = pgu[:, 1024:2048] if isup else pgu[:, 0:1024]
                            for ht in range(HT):
                                for tc in range(2):
                                    g.do(lambda e, dst=dst, buf=buf, ht=ht,
                                         tc=tc: e.matmul(
                                             dst[:, tc * 512:(tc + 1) * 512],
                                             buf[:, ht * P:(ht + 1) * P],
                                             xnT_b[:, tc * 8192 + ht * 512:
                                                   tc * 8192 + (ht + 1) * 512],
                                             start=(ht == 0), stop=(ht == HT - 1)))
                            g.inc("pe", 1, ("p5", "gu", blk, k))
                        for pr in range(4):
                            for it in range(ITI):
                                buf = dt_t[(it // 2) % 2] if it % 2 == 0 else \
                                    dt_t[2 + ((it - 1) // 2) % 2]
                                if it % 2 == 0:
                                    g.wait(f"dts{(it // 2) % 2}",
                                           ("p5", "dt", blk, pr, it))
                                else:
                                    g.wait(f"dta{((it - 1) // 2) % 2}",
                                           ("p5", "dta", blk, pr, it))
                                if it == 0:
                                    if (blk, pr) != (0, 0):
                                        pblk, ppr = (blk, pr - 1) if pr else (blk - 1, 3)
                                        g.wait("vec", ("p5", "pev", pblk, ppr))
                                    if pr == 0:
                                        g.wait("vec", ("p5", "hdn", blk, ITI - 1))
                                for e2 in range(2):
                                    tt8 = 2 * pr + e2
                                    dst = pd if e2 == 0 else pgu
                                    hsl = hdn[:, it * 1024 + tpos(tt8):
                                              it * 1024 + tpos(tt8) + P]
                                    for j in range(4):
                                        g.do(lambda e, dst=dst, hsl=hsl, j=j,
                                             buf=buf: e.matmul(
                                                 dst[:, j * 512:(j + 1) * 512],
                                                 hsl, buf[:, j * 512:(j + 1) * 512],
                                                 start=(it == 0),
                                                 stop=(it == ITI - 1)))
                                g.inc("pe", 1, ("p5", "dn", blk, pr, it))

                def p5_vec(g):
                    for blk in range(NBLK):
                        for it in range(ITI):
                            g.wait("act", ("p5", "silu", blk, it))
                            g.wait("pe", ("p5", "gu", blk, 2 * it + 1))
                            if it == 0 and blk >= 1:
                                g.wait("pe", ("p5", "dn", blk - 1, 3, ITI - 1))
                            g.do(lambda e, it=it: e.tensor_mul(
                                hdn[:, it * 1024:(it + 1) * 1024],
                                silu_g[it % 2][:], pgu[:, 1024:2048]))
                            g.mark("vec", ("p5", "hdn", blk, it))
                        for pr in range(4):
                            g.wait("pe", ("p5", "dn", blk, pr, ITI - 1))
                            g.wait("xo", ("p5", "xo", blk, pr))
                            if (blk, pr) != (0, 0):
                                pblk, ppr = (blk, pr - 1) if pr else (blk - 1, 3)
                                g.wait("pw", ("p5", "pw", pblk, ppr))
                            g.do(lambda e: e.tensor_add(
                                part_sb[0][:], pd[:], xo[0][:]))
                            g.do(lambda e: e.tensor_add(
                                part_sb[1][:], pgu[:], xo[1][:]))
                            g.mark("vec", ("p5", "pev", blk, pr))

                run_phase(block, ph, {"gpsimd": p5_gp, "sync": p5_sync,
                                      "scalar": p5_act, "tensor": p5_pe,
                                      "vector": p5_vec})

        with ExitStack() as C4:
            E4 = C4.enter_context
            QTb = E4(nc.sbuf_tensor("QTb", [P, HT * 512], BF16))
            Kc = [E4(nc.sbuf_tensor(f"Kc{i}", [P, HT * KCH], BF16))
                  for i in range(2)]
            Vc = [E4(nc.sbuf_tensor(f"Vc{i}", [P, 8 * D], BF16))
                  for i in range(2)]
            acc = E4(nc.sbuf_tensor("acc", [P, 4 * D], F32))
            P_sb = [E4(nc.sbuf_tensor(f"P_sb{i}", [P, CH], BF16)) for i in range(2)]
            PT_sb = [E4(nc.sbuf_tensor(f"PT_sb{i}", [P, CH], BF16)) for i in range(2)]
            part6 = E4(nc.sbuf_tensor("part6", [P, D], F32))
            out6 = E4(nc.sbuf_tensor("out6", [P, D], F32))
            cnm = [E4(nc.sbuf_tensor(f"cnm{i}", [P, 1], F32)) for i in range(2)]
            nm = [[E4(nc.sbuf_tensor(f"nm{q}_{i}", [P, 1], F32)) for i in range(2)]
                  for q in range(4)]
            delta = [E4(nc.sbuf_tensor(f"delta{i}", [P, 1], F32)) for i in range(2)]
            alpha = [E4(nc.sbuf_tensor(f"alpha{i}", [P, 1], F32)) for i in range(2)]
            csum = [E4(nc.sbuf_tensor(f"csum{i}", [P, 1], F32)) for i in range(2)]
            dsum = [E4(nc.sbuf_tensor(f"dsum{q}", [P, 1], F32)) for q in range(4)]
            rcp = E4(nc.sbuf_tensor("rcp", [P, 1], F32))

            NPAIR = NQB * NKC * 4
            NU6 = NPAIR * 2

            def pcoord(j):
                bcp, qt = j // 4, j % 4
                return bcp // NKC, bcp % NKC, qt

            def scoord(sq):
                j, e = sq // 2, sq % 2
                b, cp, qt = pcoord(j)
                return b, cp, qt, e

            with ExitStack() as CP:
                S_ps = [CP.enter_context(nc.psum_tensor(f"S_ps{i}", [P, CH], F32))
                        for i in range(2)]
                PT_ps = CP.enter_context(nc.psum_tensor("PT_ps", [P, CH], BF16))
                AV_ps = CP.enter_context(nc.psum_tensor("AV_ps", [P, D], F32))

                def p6_sync(g):
                    for b in range(NQB):
                        for cp in range(NKC):
                            gg = b * NKC + cp
                            if gg >= 2:
                                g.wait("pe", ("p6", "sc", ((gg - 1) * 4) * 2 - 1))
                                g.wait("pe", ("p6", "av", ((gg - 1) * 4) * 2 - 1))
                            g.dma(lambda e, gg=gg, cp=cp: e.dma_start(
                                out=Kc[gg % 2][:], in_=k_d[cp]))
                            g.inc(f"k6{gg % 2}", 16, ("p6", "k", gg))
                            g.dma(lambda e, gg=gg, cp=cp: e.dma_start(
                                out=Vc[gg % 2][:], in_=v_d[cp]))
                            g.inc(f"v6{gg % 2}", 16, ("p6", "v", gg))

                def p6_gp(g):
                    def fin_block(b):
                        for qt in range(4):
                            if (b, qt) != (0, 0):
                                pb, pq = (b, qt - 1) if qt else (b - 1, 3)
                                g.wait("vec", ("p6", "fin", pb, pq))
                            g.dma(lambda e, b=b, qt=qt: e.dma_start(
                                out=part6[:], in_=part_d[b * 4 + qt]))
                            g.inc("pt6", 16, ("p6", "part", b, qt))
                            g.wait("vec", ("p6", "fin", b, qt))
                            g.dma(lambda e, b=b, qt=qt: e.dma_start(
                                out=out_e[(b * 4 + qt) * P:(b * 4 + qt + 1) * P, :],
                                in_=out6[:]))
                            g.inc("ot6", 16, ("p6", "out", b, qt))

                    for b in range(NQB):
                        if b >= 1:
                            g.wait("pe", ("p6", "sc", b * NKC * 4 * 2 - 1))
                        g.dma(lambda e, b=b: e.dma_start(
                            out=QTb[:], in_=qt_d[b]))
                        g.inc(f"qt6{b % 2}", 16, ("p6", "qt", b))
                        if b >= 1:
                            fin_block(b - 1)
                    fin_block(NQB - 1)
                    if debug:
                        for nm_, (dap, sap) in dbg.items():
                            g.dma(lambda e, dap=dap, sap=sap: e.dma_start(
                                out=dap[:], in_=sap[:]))
                            g.inc("dgp_misc", 16, ("p6", "dbg", nm_))

                def p6_pe(g):
                    g.wait("gp", ("p0", "const"))
                    g.wait("vec", ("p5", "pev", NBLK - 1, 3))

                    def sc2(j):
                        b, cp, qt = pcoord(j)
                        gg = b * NKC + cp
                        g.wait(f"k6{gg % 2}", ("p6", "k", gg))
                        if qt == 0 and cp == 0:
                            g.wait(f"qt6{b % 2}", ("p6", "qt", b))
                        if j >= 1:
                            g.wait("act", ("p6", "exp", 2 * j - 1))
                        for ot in range(HT):
                            for e2 in range(2):
                                g.do(lambda e, gg=gg, qt=qt, ot=ot, e2=e2:
                                     e.matmul(
                                         S_ps[e2][:],
                                         QTb[:, ot * 512 + qt * P:
                                             ot * 512 + (qt + 1) * P],
                                         Kc[gg % 2][:, ot * KCH + e2 * 512:
                                                    ot * KCH + (e2 + 1) * 512],
                                         start=(ot == 0), stop=(ot == HT - 1)))
                        g.inc("pe", 1, ("p6", "sc", 2 * j))
                        g.inc("pe", 0) if False else None
                        g.mark("pe", ("p6", "sc", 2 * j + 1))

                    def trans(sq):
                        g.wait("act", ("p6", "exp", sq))
                        for j4 in range(4):
                            g.do(lambda e, sq=sq, j4=j4: e.transpose(
                                PT_ps[:, j4 * P:(j4 + 1) * P],
                                P_sb[sq % 2][:, j4 * P:(j4 + 1) * P],
                                ident[:]))
                        g.inc("pe", 1, ("p6", "tr", sq))

                    def av(sq):
                        b, cp, qt, e2 = scoord(sq)
                        gg = b * NKC + cp
                        g.wait("act", ("p6", "ptev", sq))
                        g.wait(f"v6{gg % 2}", ("p6", "v", gg))
                        if sq >= 1:
                            g.wait("vec", ("p6", "fl", sq - 1))
                        for kk in range(4):
                            kv = e2 * 4 + kk
                            for j4 in range(4):
                                g.do(lambda e, sq=sq, gg=gg, kv=kv, j4=j4:
                                     e.matmul(
                                         AV_ps[:, j4 * 512:(j4 + 1) * 512],
                                         PT_sb[sq % 2][:, kk * P:(kk + 1) * P],
                                         Vc[gg % 2][:, kv * D + j4 * 512:
                                                    kv * D + (j4 + 1) * 512],
                                         start=(kk == 0), stop=(kk == 3)))
                        g.inc("pe", 1, ("p6", "av", sq))

                    sc2(0)
                    for j in range(NPAIR):
                        if j >= 1:
                            av(2 * j - 1)
                        trans(2 * j)
                        av(2 * j)
                        trans(2 * j + 1)
                        if j + 1 < NPAIR:
                            sc2(j + 1)
                    av(2 * NPAIR - 1)

                def p6_vec(g):
                    def nm_ops(sq):
                        b, cp, qt, e2 = scoord(sq)
                        c = 2 * cp + e2
                        g.wait("pe", ("p6", "sc", sq))
                        g.do(lambda e, sq=sq, e2=e2: e.reduce_max(
                            cnm[sq % 2][:], S_ps[e2][:], axis=AX.X,
                            negate=True))
                        if c == 0:
                            g.do(lambda e, qt=qt, sq=sq: e.tensor_copy(
                                nm[qt][0][:], cnm[sq % 2][:]))
                        else:
                            g.do(lambda e, qt=qt, c=c, sq=sq: e.tensor_tensor(
                                nm[qt][c % 2][:], nm[qt][(c + 1) % 2][:],
                                cnm[sq % 2][:], ALU.min))
                            g.do(lambda e, qt=qt, c=c, sq=sq: e.tensor_sub(
                                delta[sq % 2][:], nm[qt][c % 2][:],
                                nm[qt][(c + 1) % 2][:]))
                        g.mark("vec", ("p6", "nm", sq))

                    def fl_ops(sq):
                        b, cp, qt, e2 = scoord(sq)
                        c = 2 * cp + e2
                        aq = acc[:, qt * D:(qt + 1) * D]
                        g.wait("pe", ("p6", "av", sq))
                        g.wait("act", ("p6", "exp", sq))
                        if c == 0:
                            g.do(lambda e, aq=aq: e.tensor_copy(aq, AV_ps[:]))
                            g.do(lambda e, qt=qt, sq=sq: e.tensor_copy(
                                dsum[qt][:], csum[sq % 2][:]))
                        else:
                            g.do(lambda e, aq=aq, sq=sq: e.scalar_tensor_tensor(
                                aq, aq, alpha[sq % 2][:, 0:1], AV_ps[:],
                                ALU.mult, ALU.add))
                            g.do(lambda e, qt=qt, sq=sq: e.scalar_tensor_tensor(
                                dsum[qt][:], dsum[qt][:], alpha[sq % 2][:, 0:1],
                                csum[sq % 2][:], ALU.mult, ALU.add))
                        g.mark("vec", ("p6", "fl", sq))
                        if c == NCH - 1 and qt == 3:
                            for q2 in range(4):
                                g.do(lambda e, q2=q2: e.reciprocal(
                                    rcp[:], dsum[q2][:]))
                                g.wait("pt6", ("p6", "part", b, q2))
                                if (b, q2) != (0, 0):
                                    pb, pq = (b, q2 - 1) if q2 else (b - 1, 3)
                                    g.wait("ot6", ("p6", "out", pb, pq))
                                g.do(lambda e, q2=q2: e.scalar_tensor_tensor(
                                    out6[:], acc[:, q2 * D:(q2 + 1) * D],
                                    rcp[:, 0:1], part6[:], ALU.mult, ALU.add))
                                g.mark("vec", ("p6", "fin", b, q2))

                    nm_ops(0)
                    for sq in range(NU6):
                        if sq + 1 < NU6:
                            nm_ops(sq + 1)
                        fl_ops(sq)

                def p6_act(g):
                    for sq in range(NU6):
                        b, cp, qt, e2 = scoord(sq)
                        c = 2 * cp + e2
                        g.wait("vec", ("p6", "nm", sq))
                        if c > 0:
                            g.do(lambda e, sq=sq: e.activation(
                                alpha[sq % 2][:], delta[sq % 2][:], AF.Exp))
                        g.do(lambda e, sq=sq, qt=qt, c=c, e2=e2: e.activation(
                            P_sb[sq % 2][:], S_ps[e2][:], AF.Exp,
                            bias=nm[qt][c % 2][:, 0:1],
                            accum_out=csum[sq % 2][:, 0:1]))
                        g.mark("act", ("p6", "exp", sq))
                        g.wait("pe", ("p6", "tr", sq))
                        g.do(lambda e, sq=sq: e.copy(PT_sb[sq % 2][:], PT_ps[:]))
                        g.mark("act", ("p6", "ptev", sq))

                run_phase(block, ph, {"sync": p6_sync, "gpsimd": p6_gp,
                                      "tensor": p6_pe, "vector": p6_vec,
                                      "scalar": p6_act})

    return nc


_N_CORES = 8


def kernel(L0_attn_input, L0_attn_Wq, L0_attn_bq, L0_attn_Wk, L0_attn_bk,
           L0_attn_Wv, L0_attn_bv, L0_ffn_gate_w, L0_ffn_up_w, L0_ffn_down_w,
           L0_ffn_gate_s, L0_ffn_up_s, L0_ffn_down_s):
    x = np.ascontiguousarray(np.asarray(L0_attn_input, dtype=np.float32))
    weights = {
        "wq": np.ascontiguousarray(np.asarray(L0_attn_Wq, np.float32)),
        "wk": np.ascontiguousarray(np.asarray(L0_attn_Wk, np.float32)),
        "wv": np.ascontiguousarray(np.asarray(L0_attn_Wv, np.float32)),
        "bq": np.ascontiguousarray(np.asarray(L0_attn_bq, np.float32)),
        "bk": np.ascontiguousarray(np.asarray(L0_attn_bk, np.float32)),
        "bv": np.ascontiguousarray(np.asarray(L0_attn_bv, np.float32)),
        "gw": np.ascontiguousarray(np.asarray(L0_ffn_gate_w, np.float32)),
        "uw": np.ascontiguousarray(np.asarray(L0_ffn_up_w, np.float32)),
        "dw": np.ascontiguousarray(np.asarray(L0_ffn_down_w, np.float32)),
        "gs": np.ascontiguousarray(np.asarray(L0_ffn_gate_s, np.float32)),
        "us": np.ascontiguousarray(np.asarray(L0_ffn_up_s, np.float32)),
        "ds": np.ascontiguousarray(np.asarray(L0_ffn_down_s, np.float32)),
    }
    in_maps = []
    for c in range(_N_CORES):
        b, s = c // 4, c % 4
        xb = x[b]
        xr = np.ascontiguousarray(
            np.concatenate([xb[s * TC:], xb[:s * TC]], axis=0))
        m = {"x_full": xr}
        m.update(weights)
        in_maps.append(m)

    nc = build_program()
    res = run_bass_kernel_spmd(nc, in_maps, core_ids=list(range(_N_CORES)))

    out = np.empty((2, SEQ, D), dtype=np.float32)
    for c in range(_N_CORES):
        b, s = c // 4, c % 4
        out[b, s * TC:(s + 1) * TC, :] = res.results[c]["out"]
    return out


# version 07
# speedup vs baseline: 1.0353x; 1.0353x over previous
import numpy as np
from contextlib import ExitStack

import concourse.bass as bass
import concourse.mybir as mybir
from concourse.bass_utils import run_bass_kernel_spmd

F32 = mybir.dt.float32
BF16 = mybir.dt.bfloat16
AF = mybir.ActivationFunctionType
ALU = mybir.AluOpType
AX = mybir.AxisListType

P = 128
D = 2048
HT = D // P
SEQ = 16384
TC = 4096
NT = SEQ // P
NB = SEQ // 512
I5 = 5632
ITI = I5 // P
CH = 512
NCH = SEQ // CH
KCH = 1024
NKC = SEQ // KCH
QB = 512
NQB = TC // QB
NU = NQB * NCH * 4

ENGINE_DMA_SEMS = {
    "gpsimd": ["dgp_misc", "x0", "x1", "x2", "x3", "xw0", "xw1", "xr0", "xr1", "xb",
               "xo", "pw", "qt60", "qt61", "pt6", "ot6"],
    "sync": ["wl0", "wl1", "dw0", "dw1", "dw2", "dw3", "sg0", "sg1",
             "su0", "su1", "dts0", "dts1", "k60", "k61", "v60", "v61"],
    "scalar": ["ks0", "ks1", "vs0", "vs1", "qs0", "qs1", "st0", "st1",
               "dta0", "dta1"],
}


class Phase:
    def __init__(self, sems):
        self.sems = sems
        self.marks = {}
        self.tot = {k: 0 for k in sems}
        self.bar_prev = 0


CHAIN_SEM = {"vector": "vec", "scalar": "act", "gpsimd": "gp"}


class Gen:
    __slots__ = ("ph", "eng", "cnt", "chain", "last", "last_waited")

    def __init__(self, ph, eng, cnt, engine_name):
        self.ph = ph
        self.eng = eng
        self.cnt = cnt
        self.chain = CHAIN_SEM.get(engine_name)
        self.last = None
        self.last_waited = cnt.get(self.chain, 0) if self.chain else 0

    def do(self, f, dma=False):
        c = self.chain
        if c is not None and self.cnt[c] > self.last_waited:
            if self.eng is not None:
                self.eng.wait_ge(self.ph.sems[c], self.cnt[c])
            self.last_waited = self.cnt[c]
        if self.eng is not None:
            self.last = f(self.eng)
        if c is not None and not dma:
            self.cnt[c] += 1
            if self.eng is not None:
                self.last.then_inc(self.ph.sems[c], 1)
        return self.last

    def dma(self, f):
        return self.do(f, dma=True)

    def inc(self, sem, n=1, tag=None):
        self.cnt[sem] += n
        if self.eng is None:
            if tag is not None:
                key = (sem,) + tag
                assert key not in self.ph.marks, key
                self.ph.marks[key] = self.cnt[sem]
        else:
            self.last.then_inc(self.ph.sems[sem], n)

    def mark(self, sem, tag):
        if self.eng is None:
            key = (sem,) + tag
            assert key not in self.ph.marks, key
            self.ph.marks[key] = self.cnt[sem]

    def wait(self, sem, tag):
        if self.eng is not None:
            self.eng.wait_ge(self.ph.sems[sem], self.ph.marks[(sem,) + tag])


def run_phase(block, ph, emitters):
    base = dict(ph.tot)
    for name in ("tensor", "vector", "scalar", "sync", "gpsimd"):
        fn = emitters.get(name)
        if fn:
            fn(Gen(ph, None, ph.tot, name))
    ph.tot["bar"] += 5
    bar_target = ph.tot["bar"]
    prev = ph.bar_prev

    def mk(name, fn):
        def body(eng):
            if prev > 0:
                eng.wait_ge(ph.sems["bar"], prev)
            if fn:
                g2 = Gen(ph, eng, dict(base), name)
                fn(g2)
            for s in ENGINE_DMA_SEMS.get(name, []):
                if ph.tot[s] > 0:
                    eng.wait_ge(ph.sems[s], ph.tot[s])
            cs = CHAIN_SEM.get(name)
            if cs is not None and ph.tot[cs] > 0:
                eng.wait_ge(ph.sems[cs], ph.tot[cs])
            eng.nop().then_inc(ph.sems["bar"], 1)
        return body

    block.tensor(mk("tensor", emitters.get("tensor")))
    block.vector(mk("vector", emitters.get("vector")))
    block.scalar(mk("scalar", emitters.get("scalar")))
    block.sync(mk("sync", emitters.get("sync")))
    block.gpsimd(mk("gpsimd", emitters.get("gpsimd")))
    ph.bar_prev = bar_target


def tpos(tt8):
    return (tt8 // 4) * 512 + (tt8 % 4) * 128


def build_program(debug=False):
    nc = bass.Bass("TRN2", target_bir_lowering=False)

    x_full = nc.dram_tensor("x_full", [SEQ, D], F32, kind="ExternalInput").ap()
    wq = nc.dram_tensor("wq", [D, D], F32, kind="ExternalInput").ap()
    wk = nc.dram_tensor("wk", [D, D], F32, kind="ExternalInput").ap()
    wv = nc.dram_tensor("wv", [D, D], F32, kind="ExternalInput").ap()
    bq = nc.dram_tensor("bq", [D], F32, kind="ExternalInput").ap()
    bk = nc.dram_tensor("bk", [D], F32, kind="ExternalInput").ap()
    bv = nc.dram_tensor("bv", [D], F32, kind="ExternalInput").ap()
    gw = nc.dram_tensor("gw", [I5, D], F32, kind="ExternalInput").ap()
    uw = nc.dram_tensor("uw", [I5, D], F32, kind="ExternalInput").ap()
    dw = nc.dram_tensor("dw", [D, I5], F32, kind="ExternalInput").ap()
    gs = nc.dram_tensor("gs", [I5], F32, kind="ExternalInput").ap()
    us = nc.dram_tensor("us", [I5], F32, kind="ExternalInput").ap()
    ds = nc.dram_tensor("ds", [D], F32, kind="ExternalInput").ap()
    out_e = nc.dram_tensor("out", [TC, D], F32, kind="ExternalOutput").ap()

    xnt_d = nc.dram_tensor("xnt_d", [NB, P, 16 * 512], BF16).ap()
    k_d = nc.dram_tensor("k_d", [NKC, P, HT * KCH], BF16).ap()
    v_d = nc.dram_tensor("v_d", [NKC, P, 8 * D], BF16).ap()
    qt_d = nc.dram_tensor("qt_d", [NQB, P, HT * 512], BF16).ap()
    gt_d = nc.dram_tensor("gt_d", [ITI, P, D], BF16).ap()
    ut_d = nc.dram_tensor("ut_d", [ITI, P, D], BF16).ap()
    dt_d = nc.dram_tensor("dt_d", [ITI, P, D], BF16).ap()
    part_d = nc.dram_tensor("part_d", [TC // P, P, D], F32).ap()

    dbg = {}
    if debug:
        for nm, src in (("dbg_xnt", xnt_d), ("dbg_k", k_d), ("dbg_v", v_d),
                        ("dbg_qt", qt_d), ("dbg_gt", gt_d), ("dbg_dt", dt_d),
                        ("dbg_part", part_d)):
            dbg[nm] = (nc.dram_tensor(nm, list(src.shape), src.dtype,
                                      kind="ExternalOutput").ap(), src)

    with ExitStack() as G:
        E = G.enter_context
        sems = {}
        sem_names = (["bar", "gp", "pe", "vec", "act", "fin", "dgp_misc",
                      "x0", "x1", "x2", "x3", "xw0", "xw1", "xr0", "xr1", "xb", "xo",
                      "pw", "qt60", "qt61", "pt6", "ot6",
                      "wl0", "wl1", "dw0", "dw1", "dw2", "dw3",
                      "sg0", "sg1", "su0", "su1", "dts0", "dts1",
                      "k60", "k61", "v60", "v61",
                      "ks0", "ks1", "vs0", "vs1", "qs0", "qs1",
                      "st0", "st1", "dta0", "dta1"])
        for s in sem_names:
            sems[s] = E(nc.semaphore(s))
        ph = Phase(sems)

        ident = E(nc.sbuf_tensor("ident", [P, P], BF16))
        identf = E(nc.sbuf_tensor("identf", [P, P], F32))
        ones = E(nc.sbuf_tensor("ones", [1, P], F32))
        bqc = E(nc.sbuf_tensor("bqc", [P, HT], F32))
        bkc = E(nc.sbuf_tensor("bkc", [P, HT], F32))
        gsc = E(nc.sbuf_tensor("gsc", [P, ITI], F32))
        usc = E(nc.sbuf_tensor("usc", [P, ITI], F32))
        dsc_sb = E(nc.sbuf_tensor("dsc_sb", [P, HT], F32))

        block = E(nc.Block())

        with ExitStack() as C:
            st44 = C.enter_context(nc.sbuf_tensor("st44", [44, P], F32))
            ps0 = C.enter_context(nc.psum_tensor("ps0", [P, 256], F32))

            loads = [
                (bq.rearrange("(a f) -> a f", a=16), 16, bqc),
                (bk.rearrange("(a f) -> a f", a=16), 16, bkc),
                (ds.rearrange("(a f) -> a f", a=16), 16, dsc_sb),
                (gs.rearrange("(a f) -> a f", a=44), 44, gsc),
                (us.rearrange("(a f) -> a f", a=44), 44, usc),
            ]

            def p0_gp(g):
                g.do(lambda e: e.memset(ident[:], 0.0))
                g.do(lambda e: e.memset(identf[:], 0.0))
                g.do(lambda e: e.memset(ones[:], 1.0))
                g.do(lambda e: e.affine_select(
                    out=ident[:], in_=ident[:], compare_op=ALU.not_equal,
                    fill=1.0, base=0, pattern=[[-1, P]], channel_multiplier=1))
                g.do(lambda e: e.affine_select(
                    out=identf[:], in_=identf[:], compare_op=ALU.not_equal,
                    fill=1.0, base=0, pattern=[[-1, P]], channel_multiplier=1))
                g.mark("gp", ("p0", "const"))
                for j, (src, rows, _) in enumerate(loads):
                    if j > 0:
                        g.wait("pe", ("p0", "tr", j - 1))
                    g.dma(lambda e, src=src, rows=rows:
                         e.dma_start(out=st44[0:rows, :], in_=src))
                    g.inc("dgp_misc", 16, ("p0", "ld", j))

            def p0_pe(g):
                g.wait("gp", ("p0", "const"))
                off = 0
                for j, (_, rows, _) in enumerate(loads):
                    g.wait("dgp_misc", ("p0", "ld", j))
                    if j > 0:
                        g.wait("vec", ("p0", "ev", j - 1))
                    g.do(lambda e, off=off, rows=rows: e.transpose(
                        ps0[:, off:off + rows], st44[0:rows, :],
                        identf[0:rows, 0:rows]))
                    g.inc("pe", 1, ("p0", "tr", j))
                    off += rows

            def p0_vec(g):
                off = 0
                for j, (_, rows, dest) in enumerate(loads):
                    g.wait("pe", ("p0", "tr", j))
                    g.do(lambda e, off=off, rows=rows, dest=dest:
                         e.tensor_copy(dest[:, 0:rows], ps0[:, off:off + rows]))
                    g.mark("vec", ("p0", "ev", j))
                    off += rows

            run_phase(block, ph, {"gpsimd": p0_gp, "tensor": p0_pe,
                                  "vector": p0_vec})

        with ExitStack() as C1:
            E1 = C1.enter_context
            WT = E1(nc.sbuf_tensor("WT", [P, HT * D], BF16))
            wrow = [E1(nc.sbuf_tensor(f"wrow{i}", [P, D], F32)) for i in range(2)]
            wbf = [E1(nc.sbuf_tensor(f"wbf{i}", [P, D], BF16)) for i in range(2)]
            xs = [E1(nc.sbuf_tensor(f"xs{i}", [P, D], F32)) for i in range(4)]
            sqs = E1(nc.sbuf_tensor("sqs", [P, D], F32))
            xnbf = [E1(nc.sbuf_tensor(f"xnbf{i}", [P, D], BF16)) for i in range(2)]
            stage = [E1(nc.sbuf_tensor(f"stage{i}", [P, 16 * 512], BF16)) for i in range(2)]
            ktbf = [E1(nc.sbuf_tensor(f"ktbf{i}", [P, 512], BF16)) for i in range(2)]
            vbf = [E1(nc.sbuf_tensor(f"vbf{i}", [P, D], BF16)) for i in range(2)]
            qtbf = [E1(nc.sbuf_tensor(f"qtbf{i}", [P, 512], BF16)) for i in range(2)]
            bvb = E1(nc.sbuf_tensor("bvb", [P, D], F32))
            bvrow = E1(nc.sbuf_tensor("bvrow", [1, D], F32))
            stats = {nm: [E1(nc.sbuf_tensor(f"{nm}{i}", [P, 1], F32))
                          for i in range(4)]
                     for nm in ("sumv", "ssq", "mb", "t2", "t3", "stdv", "rstd")}

            def wt_build(pn, w_ap, extra_pe=None, extra_vec=None, extra_gp=None):
                def f_sync(g):
                    for ot in range(HT):
                        if ot >= 2:
                            g.wait("vec", (pn, "cast", ot - 2))
                        g.dma(lambda e, ot=ot: e.dma_start(
                            out=wrow[ot % 2][:], in_=w_ap[ot * P:(ot + 1) * P, :]))
                        g.inc(f"wl{ot % 2}", 16, (pn, "wl", ot))

                def f_vec(g):
                    for ot in range(HT):
                        g.wait(f"wl{ot % 2}", (pn, "wl", ot))
                        if ot >= 2:
                            g.wait("pe", (pn, "tr", ot - 2))
                        g.do(lambda e, ot=ot: e.tensor_copy(
                            wbf[ot % 2][:], wrow[ot % 2][:]))
                        g.mark("vec", (pn, "cast", ot))
                        g.wait("pe", (pn, "tr", ot))
                        dst = WT.ap().rearrange("p (ht o) -> p ht o", ht=HT)
                        g.do(lambda e, ot=ot, dst=dst: e.tensor_copy(
                            dst[:, :, ot * P:(ot + 1) * P],
                            wt_ps.ap().rearrange("p (ht o) -> p ht o", ht=HT)))
                        g.mark("vec", (pn, "wtev", ot))
                    if extra_vec:
                        extra_vec(g)

                def f_pe(g):
                    g.wait("gp", ("p0", "const"))
                    g.wait("vec", ("p0", "ev", 4))
                    for ot in range(HT):
                        g.wait("vec", (pn, "cast", ot))
                        if ot >= 1:
                            g.wait("vec", (pn, "wtev", ot - 1))
                        for ht in range(HT):
                            g.do(lambda e, ot=ot, ht=ht: e.transpose(
                                wt_ps[:, ht * P:(ht + 1) * P],
                                wbf[ot % 2][:, ht * P:(ht + 1) * P], ident[:]))
                        g.inc("pe", 1, (pn, "tr", ot))
                    if extra_pe:
                        extra_pe(g)

                return {"sync": f_sync, "vector": f_vec, "tensor": f_pe,
                        "gpsimd": extra_gp}

            with ExitStack() as CP:
                wt_ps = CP.enter_context(nc.psum_tensor("wt_ps1", [P, D], BF16))
                run_phase(block, ph, wt_build("p1a", wk))

            with ExitStack() as CP:
                xnT_ps = CP.enter_context(nc.psum_tensor("xnT_ps", [P, D], BF16))
                kt_ps = [CP.enter_context(nc.psum_tensor(f"kt_ps{i}", [P, 512], F32))
                         for i in range(2)]

                def p1b_gp(g):
                    for tb in range(NB):
                        for tt in range(4):
                            i = 4 * tb + tt
                            if i >= 4:
                                g.wait("vec", ("p1b", "xn", i - 4))
                            g.dma(lambda e, i=i: e.dma_start(
                                out=xs[i % 4][:], in_=x_full[i * P:(i + 1) * P, :]))
                            g.inc(f"x{i % 4}", 16, ("p1b", "x", i))
                        g.wait("vec", ("p1b", "sev", 4 * tb + 3))
                        g.dma(lambda e, tb=tb: e.dma_start(
                            out=xnt_d[tb], in_=stage[tb % 2][:]))
                        g.inc(f"xw{tb % 2}", 16, ("p1b", "xw", tb))

                def p1b_vec(g):
                    for tb in range(NB):
                        for tt in range(4):
                            i = 4 * tb + tt
                            p = i % 4
                            g.wait(f"x{i % 4}", ("p1b", "x", i))
                            if i >= 4:
                                g.wait("act", ("p1b", "sqrt", i - 4))
                            g.do(lambda e, p=p: e.reduce_sum(
                                stats["sumv"][p][:], xs[p][:], axis=AX.X))
                            g.do(lambda e, p=p: e.scalar_tensor_tensor(
                                sqs[:], xs[p][:], 1.0, xs[p][:],
                                ALU.mult, ALU.mult,
                                accum_out=stats["ssq"][p][:]))
                            g.do(lambda e, p=p: e.tensor_scalar_mul(
                                stats["mb"][p][:], stats["sumv"][p][:], 1.0 / D))
                            g.do(lambda e, p=p: e.tensor_mul(
                                stats["t2"][p][:], stats["mb"][p][:],
                                stats["sumv"][p][:]))
                            g.do(lambda e, p=p: e.tensor_sub(
                                stats["t3"][p][:], stats["ssq"][p][:],
                                stats["t2"][p][:]))
                            g.do(lambda e, p=p: e.tensor_scalar(
                                stats["t3"][p][:], stats["t3"][p][:],
                                1.0 / D, 1e-5, ALU.mult, ALU.add))
                            g.mark("vec", ("p1b", "t3", i))
                        for tt in range(4):
                            i = 4 * tb + tt
                            p = i % 4
                            g.wait("act", ("p1b", "sqrt", i))
                            g.do(lambda e, p=p: e.reciprocal(
                                stats["rstd"][p][:], stats["stdv"][p][:]))
                            g.do(lambda e, p=p: e.tensor_scalar(
                                xnbf[i % 2][:], xs[p][:], stats["mb"][p][:, 0:1],
                                stats["rstd"][p][:, 0:1],
                                ALU.subtract, ALU.mult))
                            g.mark("vec", ("p1b", "xn", i))
                            g.wait("pe", ("p1b", "tr", i))
                            if tt == 0 and tb >= 2:
                                g.wait("pe", ("p1b", "k", (tb - 2) * HT + HT - 1))
                                g.wait(f"xw{tb % 2}", ("p1b", "xw", tb - 2))
                            dst = stage[tb % 2].ap().rearrange(
                                "p (ht t) -> p ht t", ht=HT)
                            g.do(lambda e, tt=tt, dst=dst: e.tensor_copy(
                                dst[:, :, tt * P:(tt + 1) * P],
                                xnT_ps.ap().rearrange("p (ht t) -> p ht t", ht=HT)))
                            g.mark("vec", ("p1b", "sev", i))

                def p1b_act(g):
                    g.wait("vec", ("p0", "ev", 1))
                    for tb in range(NB):
                        for tt in range(4):
                            i = 4 * tb + tt
                            p = i % 4
                            g.wait("vec", ("p1b", "t3", i))
                            if i >= 4:
                                g.wait("vec", ("p1b", "xn", i - 4))
                            g.do(lambda e, p=p: e.activation(
                                stats["stdv"][p][:], stats["t3"][p][:], AF.Sqrt))
                            g.mark("act", ("p1b", "sqrt", i))
                            if tb >= 1:
                                for e4 in range(4):
                                    ot = 4 * tt + e4
                                    kg = (tb - 1) * HT + ot
                                    g.wait("pe", ("p1b", "k", kg))
                                    if kg >= 2:
                                        g.wait(f"ks{kg % 2}", ("p1b", "ks", kg - 2))
                                    g.do(lambda e, kg=kg, ot=ot: e.activation(
                                        ktbf[kg % 2][:], kt_ps[ot % 2][:],
                                        AF.Identity, bias=bkc[:, ot:ot + 1]))
                                    g.mark("act", ("p1b", "ktev", kg))
                                    tb1 = tb - 1
                                    g.dma(lambda e, kg=kg, tb1=tb1, ot=ot:
                                         e.dma_start(
                                             out=k_d[tb1 // 2, :,
                                                     ot * KCH + (tb1 % 2) * 512:
                                                     ot * KCH + (tb1 % 2) * 512 + 512],
                                             in_=ktbf[kg % 2][:]))
                                    g.inc(f"ks{kg % 2}", 16, ("p1b", "ks", kg))
                    tb1 = NB - 1
                    for ot in range(HT):
                        kg = tb1 * HT + ot
                        g.wait("pe", ("p1b", "k", kg))
                        if kg >= 2:
                            g.wait(f"ks{kg % 2}", ("p1b", "ks", kg - 2))
                        g.do(lambda e, kg=kg, ot=ot: e.activation(
                            ktbf[kg % 2][:], kt_ps[ot % 2][:], AF.Identity,
                            bias=bkc[:, ot:ot + 1]))
                        g.mark("act", ("p1b", "ktev", kg))
                        g.dma(lambda e, kg=kg, tb1=tb1, ot=ot: e.dma_start(
                            out=k_d[tb1 // 2, :,
                                    ot * KCH + (tb1 % 2) * 512:
                                    ot * KCH + (tb1 % 2) * 512 + 512],
                            in_=ktbf[kg % 2][:]))
                        g.inc(f"ks{kg % 2}", 16, ("p1b", "ks", kg))

                def p1b_pe(g):
                    g.wait("vec", ("p1a", "wtev", HT - 1))
                    for tb in range(NB):
                        for tt in range(4):
                            i = 4 * tb + tt
                            g.wait("vec", ("p1b", "xn", i))
                            if i >= 1:
                                g.wait("vec", ("p1b", "sev", i - 1))
                            for ht in range(HT):
                                g.do(lambda e, i=i, ht=ht: e.transpose(
                                    xnT_ps[:, ht * P:(ht + 1) * P],
                                    xnbf[i % 2][:, ht * P:(ht + 1) * P],
                                    ident[:]))
                            g.inc("pe", 1, ("p1b", "tr", i))
                        g.wait("vec", ("p1b", "sev", 4 * tb + 3))
                        for ot in range(HT):
                            kg = tb * HT + ot
                            if kg >= 2:
                                g.wait("act", ("p1b", "ktev", kg - 2))
                            for ht in range(HT):
                                g.do(lambda e, ot=ot, ht=ht, tb=tb: e.matmul(
                                    kt_ps[ot % 2][:],
                                    WT[:, ht * D + ot * P: ht * D + (ot + 1) * P],
                                    stage[tb % 2][:, ht * 512:(ht + 1) * 512],
                                    start=(ht == 0), stop=(ht == HT - 1)))
                            g.inc("pe", 1, ("p1b", "k", kg))

                run_phase(block, ph, {"gpsimd": p1b_gp, "vector": p1b_vec,
                                      "scalar": p1b_act, "tensor": p1b_pe})

            with ExitStack() as CP:
                wt_ps = CP.enter_context(nc.psum_tensor("wt_ps2", [P, D], BF16))
                pbv = CP.enter_context(nc.psum_tensor("pbv", [P, D], F32))

                def p2a_gp(g):
                    g.dma(lambda e: e.dma_start(
                        out=bvrow[:], in_=bv.rearrange("(a f) -> a f", a=1)))
                    g.inc("dgp_misc", 16, ("p2a", "bv"))

                def p2a_pe_extra(g):
                    g.wait("dgp_misc", ("p2a", "bv"))
                    for k in range(4):
                        g.do(lambda e, k=k: e.matmul(
                            pbv[:, k * 512:(k + 1) * 512], ones[0:1, :],
                            bvrow[0:1, k * 512:(k + 1) * 512],
                            start=True, stop=True))
                    g.inc("pe", 1, ("p2a", "bvmm"))

                def p2a_vec_extra(g):
                    g.wait("pe", ("p2a", "bvmm"))
                    g.do(lambda e: e.tensor_copy(bvb[:], pbv[:]))
                    g.mark("vec", ("p2a", "bvev"))

                run_phase(block, ph, wt_build("p2a", wv, extra_pe=p2a_pe_extra,
                                              extra_vec=p2a_vec_extra,
                                              extra_gp=p2a_gp))

            with ExitStack() as CP:
                v_ps = [CP.enter_context(nc.psum_tensor(f"v_ps{i}", [P, D], F32))
                        for i in range(2)]

                def p2b_gp(g):
                    for tb in range(NB):
                        if tb >= 2:
                            g.wait("pe", ("p2b", "v", 4 * (tb - 2) + 3))
                        g.dma(lambda e, tb=tb: e.dma_start(
                            out=stage[tb % 2][:], in_=xnt_d[tb]))
                        g.inc(f"xr{tb % 2}", 16, ("p2b", "xr", tb))

                def p2b_pe(g):
                    g.wait("vec", ("p2a", "wtev", HT - 1))
                    g.wait("vec", ("p2a", "bvev"))
                    for tb in range(NB):
                        g.wait(f"xr{tb % 2}", ("p2b", "xr", tb))
                        for tt in range(4):
                            vg = 4 * tb + tt
                            if vg >= 2:
                                g.wait("vec", ("p2b", "vev", vg - 2))
                            for ht in range(HT):
                                for oc in range(4):
                                    g.do(lambda e, vg=vg, tb=tb, tt=tt, ht=ht,
                                         oc=oc: e.matmul(
                                             v_ps[vg % 2][:, oc * 512:(oc + 1) * 512],
                                             stage[tb % 2][:, ht * 512 + tt * P:
                                                           ht * 512 + (tt + 1) * P],
                                             WT[:, ht * D + oc * 512:
                                                ht * D + (oc + 1) * 512],
                                             start=(ht == 0), stop=(ht == HT - 1)))
                            g.inc("pe", 1, ("p2b", "v", vg))

                def p2b_vec(g):
                    for vg in range(NT):
                        g.wait("pe", ("p2b", "v", vg))
                        if vg >= 2:
                            g.wait(f"vs{vg % 2}", ("p2b", "vs", vg - 2))
                        g.do(lambda e, vg=vg: e.tensor_add(
                            vbf[vg % 2][:], v_ps[vg % 2][:], bvb[:]))
                        g.mark("vec", ("p2b", "vev", vg))

                def p2b_act(g):
                    for vg in range(NT):
                        g.wait("vec", ("p2b", "vev", vg))
                        g.dma(lambda e, vg=vg: e.dma_start(
                            out=v_d[vg // 8, :, (vg % 8) * D:(vg % 8 + 1) * D],
                            in_=vbf[vg % 2][:]))
                        g.inc(f"vs{vg % 2}", 16, ("p2b", "vs", vg))

                run_phase(block, ph, {"gpsimd": p2b_gp, "tensor": p2b_pe,
                                      "vector": p2b_vec, "scalar": p2b_act})

            with ExitStack() as CP:
                wt_ps = CP.enter_context(nc.psum_tensor("wt_ps3", [P, D], BF16))
                run_phase(block, ph, wt_build("p3a", wq))

            with ExitStack() as CP:
                qt_ps = [CP.enter_context(nc.psum_tensor(f"qt_ps{i}", [P, 512], F32))
                         for i in range(2)]

                def p3b_gp(g):
                    for qc in range(NQB):
                        if qc >= 2:
                            g.wait("pe", ("p3b", "q", (qc - 2) * HT + HT - 1))
                        g.dma(lambda e, qc=qc: e.dma_start(
                            out=stage[qc % 2][:], in_=xnt_d[qc]))
                        g.inc(f"xr{qc % 2}", 16, ("p3b", "xr", qc))

                def p3b_pe(g):
                    g.wait("vec", ("p3a", "wtev", HT - 1))
                    for qc in range(NQB):
                        g.wait(f"xr{qc % 2}", ("p3b", "xr", qc))
                        for ot in range(HT):
                            k = qc * HT + ot
                            if k >= 2:
                                g.wait("act", ("p3b", "qev", k - 2))
                            for ht in range(HT):
                                g.do(lambda e, k=k, qc=qc, ot=ot, ht=ht:
                                     e.matmul(
                                         qt_ps[k % 2][:],
                                         WT[:, ht * D + ot * P:
                                            ht * D + (ot + 1) * P],
                                         stage[qc % 2][:, ht * 512:(ht + 1) * 512],
                                         start=(ht == 0), stop=(ht == HT - 1)))
                            g.inc("pe", 1, ("p3b", "q", k))

                def p3b_act(g):
                    g.wait("vec", ("p0", "ev", 0))
                    for qc in range(NQB):
                        for ot in range(HT):
                            k = qc * HT + ot
                            g.wait("pe", ("p3b", "q", k))
                            if k >= 2:
                                g.wait(f"qs{k % 2}", ("p3b", "qs", k - 2))
                            g.do(lambda e, k=k, ot=ot: e.activation(
                                qtbf[k % 2][:], qt_ps[k % 2][:], AF.Identity,
                                bias=bqc[:, ot:ot + 1]))
                            g.mark("act", ("p3b", "qev", k))
                            g.dma(lambda e, k=k, qc=qc, ot=ot: e.dma_start(
                                out=qt_d[qc, :, ot * 512:(ot + 1) * 512],
                                in_=qtbf[k % 2][:]))
                            g.inc(f"qs{k % 2}", 16, ("p3b", "qs", k))

                run_phase(block, ph, {"gpsimd": p3b_gp, "tensor": p3b_pe,
                                      "scalar": p3b_act})

        with ExitStack() as C2:
            E2 = C2.enter_context
            gw_st = [E2(nc.sbuf_tensor(f"gw_st{i}", [P, D], F32)) for i in range(2)]
            gubf = [E2(nc.sbuf_tensor(f"gubf{i}", [P, D], BF16)) for i in range(2)]
            gu_stage = [E2(nc.sbuf_tensor(f"gu_stage{i}", [P, D], BF16))
                        for i in range(2)]
            with ExitStack() as CP:
                gt_ps = [CP.enter_context(nc.psum_tensor(f"gt_ps{i}", [P, D], BF16))
                         for i in range(2)]
                NK = 2 * ITI

                def p4a_sync(g):
                    for k in range(NK):
                        it, isup = k // 2, k % 2
                        if k >= 2:
                            g.wait("vec", ("p4a", "cast", k - 2))
                        src = uw if isup else gw
                        g.dma(lambda e, k=k, it=it, src=src: e.dma_start(
                            out=gw_st[k % 2][:], in_=src[it * P:(it + 1) * P, :]))
                        g.inc(f"wl{k % 2}", 16, ("p4a", "wl", k))

                def p4a_vec(g):
                    for k in range(NK):
                        it, isup = k // 2, k % 2
                        g.wait(f"wl{k % 2}", ("p4a", "wl", k))
                        if k >= 2:
                            g.wait("pe", ("p4a", "tr", k - 2))
                        sc = usc if isup else gsc
                        g.do(lambda e, k=k, it=it, sc=sc: e.tensor_scalar_mul(
                            gubf[k % 2][:], gw_st[k % 2][:], sc[:, it:it + 1]))
                        g.mark("vec", ("p4a", "cast", k))
                        g.wait("pe", ("p4a", "tr", k))
                        if k >= 2:
                            g.wait(f"st{k % 2}", ("p4a", "st", k - 2))
                        g.do(lambda e, k=k: e.tensor_copy(
                            gu_stage[k % 2][:], gt_ps[k % 2][:]))
                        g.mark("vec", ("p4a", "ev", k))

                def p4a_pe(g):
                    g.wait("gp", ("p0", "const"))
                    g.wait("act", ("p3b", "qev", NQB * HT - 1))
                    for k in range(NK):
                        g.wait("vec", ("p4a", "cast", k))
                        if k >= 2:
                            g.wait("vec", ("p4a", "ev", k - 2))
                        for ht in range(HT):
                            g.do(lambda e, k=k, ht=ht: e.transpose(
                                gt_ps[k % 2][:, ht * P:(ht + 1) * P],
                                gubf[k % 2][:, ht * P:(ht + 1) * P], ident[:]))
                        g.inc("pe", 1, ("p4a", "tr", k))

                def p4a_act(g):
                    for k in range(NK):
                        it, isup = k // 2, k % 2
                        g.wait("vec", ("p4a", "ev", k))
                        dst = ut_d if isup else gt_d
                        g.dma(lambda e, k=k, it=it, dst=dst: e.dma_start(
                            out=dst[it], in_=gu_stage[k % 2][:]))
                        g.inc(f"st{k % 2}", 16, ("p4a", "st", k))

                run_phase(block, ph, {"sync": p4a_sync, "vector": p4a_vec,
                                      "tensor": p4a_pe, "scalar": p4a_act})

        with ExitStack() as C2:
            E2 = C2.enter_context
            dw_st = [E2(nc.sbuf_tensor(f"dw_st{i}", [P, I5], F32)) for i in range(4)]
            dwbf = [E2(nc.sbuf_tensor(f"dwbf{i}", [P, I5], BF16)) for i in range(4)]
            dt_stage = [E2(nc.sbuf_tensor(f"dt_stage{i}", [P, 512], BF16))
                        for i in range(2)]
            with ExitStack() as CP:
                dt_ps = [CP.enter_context(nc.psum_tensor(f"dt_ps{i}", [P, 512], BF16))
                         for i in range(2)]

                def p4b_sync(g):
                    for h4 in range(4):
                        for kk in range(4):
                            if h4 >= 1:
                                g.wait("pe", ("p4b", "tr", h4 - 1, ITI - 1))
                            g.dma(lambda e, h4=h4, kk=kk: e.dma_start(
                                out=dw_st[kk][:],
                                in_=dw[(h4 * 4 + kk) * P:(h4 * 4 + kk + 1) * P, :]))
                            g.inc(f"dw{kk}", 16, ("p4b", "wl", h4, kk))

                def p4b_vec(g):
                    for h4 in range(4):
                        for kk in range(4):
                            g.wait(f"dw{kk}", ("p4b", "wl", h4, kk))
                            if h4 >= 1:
                                g.wait("pe", ("p4b", "tr", h4 - 1, ITI - 1))
                            g.do(lambda e, h4=h4, kk=kk: e.tensor_scalar_mul(
                                dwbf[kk][:], dw_st[kk][:],
                                dsc_sb[:, h4 * 4 + kk:h4 * 4 + kk + 1]))
                            g.mark("vec", ("p4b", "cast", h4, kk))
                        for it in range(ITI):
                            f = h4 * ITI + it
                            g.wait("pe", ("p4b", "tr", h4, it))
                            if f >= 2:
                                g.wait(f"st{f % 2}", ("p4b", "st", f - 2))
                            g.do(lambda e, it=it: e.tensor_copy(
                                dt_stage[it % 2][:], dt_ps[it % 2][:]))
                            g.mark("vec", ("p4b", "ev", f))

                def p4b_pe(g):
                    g.wait("gp", ("p0", "const"))
                    g.wait("vec", ("p4a", "ev", 2 * ITI - 1))
                    for h4 in range(4):
                        for kk in range(4):
                            g.wait("vec", ("p4b", "cast", h4, kk))
                        for it in range(ITI):
                            f = h4 * ITI + it
                            if f >= 2:
                                g.wait("vec", ("p4b", "ev", f - 2))
                            for kk in range(4):
                                g.do(lambda e, it=it, kk=kk: e.transpose(
                                    dt_ps[it % 2][:, kk * P:(kk + 1) * P],
                                    dwbf[kk][:, it * P:(it + 1) * P], ident[:]))
                            g.inc("pe", 1, ("p4b", "tr", h4, it))

                def p4b_act(g):
                    for h4 in range(4):
                        for it in range(ITI):
                            f = h4 * ITI + it
                            g.wait("vec", ("p4b", "ev", f))
                            g.dma(lambda e, h4=h4, it=it: e.dma_start(
                                out=dt_d[it, :, h4 * 512:(h4 + 1) * 512],
                                in_=dt_stage[it % 2][:]))
                            g.inc(f"st{f % 2}", 16, ("p4b", "st", f))

                run_phase(block, ph, {"sync": p4b_sync, "vector": p4b_vec,
                                      "tensor": p4b_pe, "scalar": p4b_act})

        with ExitStack() as C3:
            E3 = C3.enter_context
            xnT_b = E3(nc.sbuf_tensor("xnT_b", [P, 2 * 16 * 512], BF16))
            hdn = E3(nc.sbuf_tensor("hdn", [P, ITI * 1024], BF16))
            gt_t = [E3(nc.sbuf_tensor(f"gt_t{i}", [P, D], BF16)) for i in range(2)]
            ut_t = [E3(nc.sbuf_tensor(f"ut_t{i}", [P, D], BF16)) for i in range(2)]
            silu_g = [E3(nc.sbuf_tensor(f"silu_g{i}", [P, 1024], F32))
                      for i in range(2)]
            dt_t = [E3(nc.sbuf_tensor(f"dt_t{i}", [P, D], BF16)) for i in range(4)]
            xo = [E3(nc.sbuf_tensor(f"xo{i}", [P, D], F32)) for i in range(2)]
            part_sb = [E3(nc.sbuf_tensor(f"part_sb{i}", [P, D], F32))
                       for i in range(2)]
            with ExitStack() as CP:
                pgu = CP.enter_context(nc.psum_tensor("pgu", [P, D], F32))
                pd = CP.enter_context(nc.psum_tensor("pd", [P, D], F32))
                NBLK = TC // 1024

                def p5_gp(g):
                    for blk in range(NBLK):
                        if blk >= 1:
                            g.wait("pe", ("p5", "gu", blk - 1, 2 * ITI - 1))
                        g.dma(lambda e, blk=blk: e.dma_start(
                            out=xnT_b[:, 0:8192], in_=xnt_d[2 * blk]))
                        g.inc("xb", 16)
                        g.dma(lambda e, blk=blk: e.dma_start(
                            out=xnT_b[:, 8192:16384], in_=xnt_d[2 * blk + 1]))
                        g.inc("xb", 16, ("p5", "xb", blk))
                        for pr in range(4):
                            if (blk, pr) != (0, 0):
                                pblk, ppr = (blk, pr - 1) if pr else (blk - 1, 3)
                                g.wait("vec", ("p5", "pev", pblk, ppr))
                            for e2 in range(2):
                                g128 = blk * 8 + 2 * pr + e2
                                g.dma(lambda e, g128=g128, e2=e2: e.dma_start(
                                    out=xo[e2][:],
                                    in_=x_full[g128 * P:(g128 + 1) * P, :]))
                                g.inc("xo", 16,
                                      ("p5", "xo", blk, pr) if e2 else None)
                            g.wait("vec", ("p5", "pev", blk, pr))
                            for e2 in range(2):
                                g128 = blk * 8 + 2 * pr + e2
                                g.dma(lambda e, g128=g128, e2=e2: e.dma_start(
                                    out=part_d[g128], in_=part_sb[e2][:]))
                                g.inc("pw", 16,
                                      ("p5", "pw", blk, pr) if e2 else None)

                def p5_sync(g):
                    for blk in range(NBLK):
                        for k in range(2 * ITI):
                            it, isup = k // 2, k % 2
                            if k >= 4 or blk > 0:
                                pk = (blk, k - 4) if k >= 4 else (blk - 1, 2 * ITI + k - 4)
                                g.wait("pe", ("p5", "gu", pk[0], pk[1]))
                            src = ut_d if isup else gt_d
                            buf = (ut_t if isup else gt_t)[it % 2]
                            g.dma(lambda e, src=src, it=it, buf=buf: e.dma_start(
                                out=buf[:], in_=src[it]))
                            g.inc(f"{'su' if isup else 'sg'}{(it) % 2}", 16,
                                  ("p5", "gu_l", blk, k))
                        for pr in range(4):
                            for it in range(0, ITI, 2):
                                if it >= 4 or (blk, pr) != (0, 0):
                                    if it >= 4:
                                        pt = ("p5", "dn", blk, pr, it - 4)
                                    elif pr:
                                        pt = ("p5", "dn", blk, pr - 1, ITI - 4 + it)
                                    else:
                                        pt = ("p5", "dn", blk - 1, 3, ITI - 4 + it)
                                    g.wait("pe", pt)
                                g.dma(lambda e, it=it: e.dma_start(
                                    out=dt_t[(it // 2) % 2][:], in_=dt_d[it]))
                                g.inc(f"dts{(it // 2) % 2}", 16,
                                      ("p5", "dt", blk, pr, it))

                def p5_act(g):
                    for blk in range(NBLK):
                        for it in range(ITI):
                            g.wait("pe", ("p5", "gu", blk, 2 * it))
                            if it >= 2:
                                g.wait("vec", ("p5", "hdn", blk, it - 2))
                            g.do(lambda e, it=it: e.activation(
                                silu_g[it % 2][:], pgu[:, 0:1024], AF.Silu))
                            g.mark("act", ("p5", "silu", blk, it))
                        for pr in range(4):
                            for it in range(1, ITI, 2):
                                if it >= 5 or (blk, pr) != (0, 0):
                                    if it >= 5:
                                        pt = ("p5", "dn", blk, pr, it - 4)
                                    elif pr:
                                        pt = ("p5", "dn", blk, pr - 1, ITI - 4 + it)
                                    else:
                                        pt = ("p5", "dn", blk - 1, 3, ITI - 4 + it)
                                    g.wait("pe", pt)
                                g.dma(lambda e, it=it: e.dma_start(
                                    out=dt_t[2 + ((it - 1) // 2) % 2][:],
                                    in_=dt_d[it]))
                                g.inc(f"dta{((it - 1) // 2) % 2}", 16,
                                      ("p5", "dta", blk, pr, it))

                def p5_pe(g):
                    g.wait("vec", ("p4b", "ev", 4 * ITI - 1))
                    for blk in range(NBLK):
                        g.wait("xb", ("p5", "xb", blk))
                        for k in range(2 * ITI):
                            it, isup = k // 2, k % 2
                            g.wait(f"{'su' if isup else 'sg'}{it % 2}",
                                   ("p5", "gu_l", blk, k))
                            if isup == 0:
                                if it >= 1:
                                    g.wait("act", ("p5", "silu", blk, it - 1))
                                elif blk >= 1:
                                    g.wait("vec", ("p5", "pev", blk - 1, 3))
                            else:
                                if it >= 1:
                                    g.wait("vec", ("p5", "hdn", blk, it - 1))
                            buf = (ut_t if isup else gt_t)[it % 2]
                            dst = pgu[:, 1024:2048] if isup else pgu[:, 0:1024]
                            for ht in range(HT):
                                for tc in range(2):
                                    g.do(lambda e, dst=dst, buf=buf, ht=ht,
                                         tc=tc: e.matmul(
                                             dst[:, tc * 512:(tc + 1) * 512],
                                             buf[:, ht * P:(ht + 1) * P],
                                             xnT_b[:, tc * 8192 + ht * 512:
                                                   tc * 8192 + (ht + 1) * 512],
                                             start=(ht == 0), stop=(ht == HT - 1)))
                            g.inc("pe", 1, ("p5", "gu", blk, k))
                        for pr in range(4):
                            for it in range(ITI):
                                buf = dt_t[(it // 2) % 2] if it % 2 == 0 else \
                                    dt_t[2 + ((it - 1) // 2) % 2]
                                if it % 2 == 0:
                                    g.wait(f"dts{(it // 2) % 2}",
                                           ("p5", "dt", blk, pr, it))
                                else:
                                    g.wait(f"dta{((it - 1) // 2) % 2}",
                                           ("p5", "dta", blk, pr, it))
                                if it == 0:
                                    if (blk, pr) != (0, 0):
                                        pblk, ppr = (blk, pr - 1) if pr else (blk - 1, 3)
                                        g.wait("vec", ("p5", "pev", pblk, ppr))
                                    if pr == 0:
                                        g.wait("vec", ("p5", "hdn", blk, ITI - 1))
                                for e2 in range(2):
                                    tt8 = 2 * pr + e2
                                    dst = pd if e2 == 0 else pgu
                                    hsl = hdn[:, it * 1024 + tpos(tt8):
                                              it * 1024 + tpos(tt8) + P]
                                    for j in range(4):
                                        g.do(lambda e, dst=dst, hsl=hsl, j=j,
                                             buf=buf: e.matmul(
                                                 dst[:, j * 512:(j + 1) * 512],
                                                 hsl, buf[:, j * 512:(j + 1) * 512],
                                                 start=(it == 0),
                                                 stop=(it == ITI - 1)))
                                g.inc("pe", 1, ("p5", "dn", blk, pr, it))

                def p5_vec(g):
                    for blk in range(NBLK):
                        for it in range(ITI):
                            g.wait("act", ("p5", "silu", blk, it))
                            g.wait("pe", ("p5", "gu", blk, 2 * it + 1))
                            if it == 0 and blk >= 1:
                                g.wait("pe", ("p5", "dn", blk - 1, 3, ITI - 1))
                            g.do(lambda e, it=it: e.tensor_mul(
                                hdn[:, it * 1024:(it + 1) * 1024],
                                silu_g[it % 2][:], pgu[:, 1024:2048]))
                            g.mark("vec", ("p5", "hdn", blk, it))
                        for pr in range(4):
                            g.wait("pe", ("p5", "dn", blk, pr, ITI - 1))
                            g.wait("xo", ("p5", "xo", blk, pr))
                            if (blk, pr) != (0, 0):
                                pblk, ppr = (blk, pr - 1) if pr else (blk - 1, 3)
                                g.wait("pw", ("p5", "pw", pblk, ppr))
                            g.do(lambda e: e.tensor_add(
                                part_sb[0][:], pd[:], xo[0][:]))
                            g.do(lambda e: e.tensor_add(
                                part_sb[1][:], pgu[:], xo[1][:]))
                            g.mark("vec", ("p5", "pev", blk, pr))

                run_phase(block, ph, {"gpsimd": p5_gp, "sync": p5_sync,
                                      "scalar": p5_act, "tensor": p5_pe,
                                      "vector": p5_vec})

        with ExitStack() as C4:
            E4 = C4.enter_context
            QTb = E4(nc.sbuf_tensor("QTb", [P, HT * 512], BF16))
            Kc = [E4(nc.sbuf_tensor(f"Kc{i}", [P, HT * KCH], BF16))
                  for i in range(2)]
            Vc = [E4(nc.sbuf_tensor(f"Vc{i}", [P, 8 * D], BF16))
                  for i in range(2)]
            acc = E4(nc.sbuf_tensor("acc", [P, 4 * D], F32))
            P_sb = [E4(nc.sbuf_tensor(f"P_sb{i}", [P, CH], BF16)) for i in range(2)]
            PT_sb = [E4(nc.sbuf_tensor(f"PT_sb{i}", [P, CH], BF16)) for i in range(2)]
            part6 = E4(nc.sbuf_tensor("part6", [P, D], F32))
            out6 = E4(nc.sbuf_tensor("out6", [P, D], F32))
            cnm = [E4(nc.sbuf_tensor(f"cnm{i}", [P, 1], F32)) for i in range(2)]
            nm = [[E4(nc.sbuf_tensor(f"nm{q}_{i}", [P, 1], F32)) for i in range(2)]
                  for q in range(4)]
            delta = [E4(nc.sbuf_tensor(f"delta{i}", [P, 1], F32)) for i in range(2)]
            alpha = [E4(nc.sbuf_tensor(f"alpha{i}", [P, 1], F32)) for i in range(2)]
            csum_a = [E4(nc.sbuf_tensor(f"csum_a{i}", [P, 1], F32)) for i in range(2)]
            csum_b = [E4(nc.sbuf_tensor(f"csum_b{i}", [P, 1], F32)) for i in range(2)]
            dsum = [E4(nc.sbuf_tensor(f"dsum{q}", [P, 1], F32)) for q in range(4)]
            rcp = E4(nc.sbuf_tensor("rcp", [P, 1], F32))

            NPAIR = NQB * NKC * 4

            def pcoord(j):
                bcp, qt = j // 4, j % 4
                return bcp // NKC, bcp % NKC, qt

            with ExitStack() as CP:
                S_ps = [CP.enter_context(nc.psum_tensor(f"S_ps{i}", [P, CH], F32))
                        for i in range(2)]
                PT_ps = CP.enter_context(nc.psum_tensor("PT_ps", [P, CH], BF16))
                AV_ps = CP.enter_context(nc.psum_tensor("AV_ps", [P, D], F32))

                def p6_sync(g):
                    for gg in range(NQB * NKC):
                        cp = gg % NKC
                        if gg >= 2:
                            g.wait("pe", ("p6", "av", gg * 4 - 5))
                        g.dma(lambda e, gg=gg, cp=cp: e.dma_start(
                            out=Kc[gg % 2][:], in_=k_d[cp]))
                        g.inc(f"k6{gg % 2}", 16, ("p6", "k", gg))
                        g.dma(lambda e, gg=gg, cp=cp: e.dma_start(
                            out=Vc[gg % 2][:], in_=v_d[cp]))
                        g.inc(f"v6{gg % 2}", 16, ("p6", "v", gg))

                def p6_gp(g):
                    def fin_block(b):
                        for qt in range(4):
                            if (b, qt) != (0, 0):
                                pb, pq = (b, qt - 1) if qt else (b - 1, 3)
                                g.wait("vec", ("p6", "fin", pb, pq))
                            g.dma(lambda e, b=b, qt=qt: e.dma_start(
                                out=part6[:], in_=part_d[b * 4 + qt]))
                            g.inc("pt6", 16, ("p6", "part", b, qt))
                            g.wait("vec", ("p6", "fin", b, qt))
                            g.dma(lambda e, b=b, qt=qt: e.dma_start(
                                out=out_e[(b * 4 + qt) * P:(b * 4 + qt + 1) * P, :],
                                in_=out6[:]))
                            g.inc("ot6", 16, ("p6", "out", b, qt))

                    for b in range(NQB):
                        if b >= 1:
                            g.wait("pe", ("p6", "sc", b * NKC * 4 - 1))
                        g.dma(lambda e, b=b: e.dma_start(
                            out=QTb[:], in_=qt_d[b]))
                        g.inc(f"qt6{b % 2}", 16, ("p6", "qt", b))
                        if b >= 1:
                            fin_block(b - 1)
                    fin_block(NQB - 1)
                    if debug:
                        for nm_, (dap, sap) in dbg.items():
                            g.dma(lambda e, dap=dap, sap=sap: e.dma_start(
                                out=dap[:], in_=sap[:]))
                            g.inc("dgp_misc", 16, ("p6", "dbg", nm_))

                def p6_pe(g):
                    g.wait("gp", ("p0", "const"))
                    g.wait("vec", ("p5", "pev", NBLK - 1, 3))

                    def sc2(j):
                        b, cp, qt = pcoord(j)
                        gg = b * NKC + cp
                        g.wait(f"k6{gg % 2}", ("p6", "k", gg))
                        if qt == 0 and cp == 0:
                            g.wait(f"qt6{b % 2}", ("p6", "qt", b))
                        if j >= 1:
                            g.wait("act", ("p6", "expb", j - 1))
                        for ot in range(HT):
                            for e2 in range(2):
                                g.do(lambda e, gg=gg, qt=qt, ot=ot, e2=e2:
                                     e.matmul(
                                         S_ps[e2][:],
                                         QTb[:, ot * 512 + qt * P:
                                             ot * 512 + (qt + 1) * P],
                                         Kc[gg % 2][:, ot * KCH + e2 * 512:
                                                    ot * KCH + (e2 + 1) * 512],
                                         start=(ot == 0), stop=(ot == HT - 1)))
                        g.inc("pe", 1, ("p6", "sc", j))

                    def trans(j, e2):
                        g.wait("act", ("p6", "expa" if e2 == 0 else "expb", j))
                        if e2 == 1:
                            g.wait("act", ("p6", "ptva", j))
                        for j4 in range(4):
                            g.do(lambda e, e2=e2, j=j, j4=j4: e.transpose(
                                PT_ps[:, j4 * P:(j4 + 1) * P],
                                P_sb[e2][:, j4 * P:(j4 + 1) * P],
                                ident[:]))
                        g.inc("pe", 1, ("p6", "tra" if e2 == 0 else "trb", j))

                    def av(j, e2):
                        b, cp, qt = pcoord(j)
                        gg = b * NKC + cp
                        g.wait("act", ("p6", "ptva" if e2 == 0 else "ptvb", j))
                        if e2 == 0:
                            g.wait(f"v6{gg % 2}", ("p6", "v", gg))
                            if j >= 1:
                                g.wait("vec", ("p6", "fl", j - 1))
                        for kk in range(4):
                            kv = e2 * 4 + kk
                            for j4 in range(4):
                                g.do(lambda e, e2=e2, gg=gg, kv=kv, kk=kk, j4=j4:
                                     e.matmul(
                                         AV_ps[:, j4 * 512:(j4 + 1) * 512],
                                         PT_sb[e2][:, kk * P:(kk + 1) * P],
                                         Vc[gg % 2][:, kv * D + j4 * 512:
                                                    kv * D + (j4 + 1) * 512],
                                         start=(e2 == 0 and kk == 0),
                                         stop=(e2 == 1 and kk == 3)))
                        if e2 == 1:
                            g.inc("pe", 1, ("p6", "av", j))
                        else:
                            g.inc("pe", 1, ("p6", "ava", j))

                    sc2(0)
                    for j in range(NPAIR):
                        if j + 1 < NPAIR:
                            sc2(j + 1)
                        trans(j, 0)
                        av(j, 0)
                        trans(j, 1)
                        av(j, 1)

                def p6_vec(g):
                    def nm_ops(j):
                        b, cp, qt = pcoord(j)
                        g.wait("pe", ("p6", "sc", j))
                        g.do(lambda e, j=j: e.reduce_max(
                            cnm[0][:], S_ps[0][:], axis=AX.X, negate=True))
                        g.do(lambda e, j=j: e.reduce_max(
                            cnm[1][:], S_ps[1][:], axis=AX.X, negate=True))
                        g.do(lambda e: e.tensor_tensor(
                            cnm[0][:], cnm[0][:], cnm[1][:], ALU.min))
                        if cp == 0:
                            g.do(lambda e, qt=qt: e.tensor_copy(
                                nm[qt][0][:], cnm[0][:]))
                        else:
                            g.do(lambda e, qt=qt, cp=cp: e.tensor_tensor(
                                nm[qt][cp % 2][:], nm[qt][(cp + 1) % 2][:],
                                cnm[0][:], ALU.min))
                            g.do(lambda e, qt=qt, cp=cp, j=j: e.tensor_sub(
                                delta[j % 2][:], nm[qt][cp % 2][:],
                                nm[qt][(cp + 1) % 2][:]))
                        g.mark("vec", ("p6", "nm", j))

                    def fl_ops(j):
                        b, cp, qt = pcoord(j)
                        aq = acc[:, qt * D:(qt + 1) * D]
                        g.wait("pe", ("p6", "av", j))
                        g.wait("act", ("p6", "expb", j))
                        if cp == 0:
                            g.do(lambda e, aq=aq: e.tensor_copy(aq, AV_ps[:]))
                            g.do(lambda e, qt=qt, j=j: e.tensor_add(
                                dsum[qt][:], csum_a[j % 2][:], csum_b[j % 2][:]))
                        else:
                            g.do(lambda e, aq=aq, j=j: e.scalar_tensor_tensor(
                                aq, aq, alpha[j % 2][:, 0:1], AV_ps[:],
                                ALU.mult, ALU.add))
                            g.do(lambda e, qt=qt, j=j: e.scalar_tensor_tensor(
                                dsum[qt][:], dsum[qt][:], alpha[j % 2][:, 0:1],
                                csum_a[j % 2][:], ALU.mult, ALU.add))
                            g.do(lambda e, qt=qt, j=j: e.tensor_add(
                                dsum[qt][:], dsum[qt][:], csum_b[j % 2][:]))
                        g.mark("vec", ("p6", "fl", j))
                        if cp == NKC - 1 and qt == 3:
                            for q2 in range(4):
                                g.do(lambda e, q2=q2: e.reciprocal(
                                    rcp[:], dsum[q2][:]))
                                g.wait("pt6", ("p6", "part", b, q2))
                                if (b, q2) != (0, 0):
                                    pb, pq = (b, q2 - 1) if q2 else (b - 1, 3)
                                    g.wait("ot6", ("p6", "out", pb, pq))
                                g.do(lambda e, q2=q2: e.scalar_tensor_tensor(
                                    out6[:], acc[:, q2 * D:(q2 + 1) * D],
                                    rcp[:, 0:1], part6[:], ALU.mult, ALU.add))
                                g.mark("vec", ("p6", "fin", b, q2))

                    nm_ops(0)
                    for j in range(NPAIR):
                        if j + 1 < NPAIR:
                            nm_ops(j + 1)
                        fl_ops(j)

                def p6_act(g):
                    for j in range(NPAIR):
                        b, cp, qt = pcoord(j)
                        g.wait("vec", ("p6", "nm", j))
                        if cp > 0:
                            g.do(lambda e, j=j: e.activation(
                                alpha[j % 2][:], delta[j % 2][:], AF.Exp))
                        g.do(lambda e, j=j, qt=qt, cp=cp: e.activation(
                            P_sb[0][:], S_ps[0][:], AF.Exp,
                            bias=nm[qt][cp % 2][:, 0:1],
                            accum_out=csum_a[j % 2][:, 0:1]))
                        g.mark("act", ("p6", "expa", j))
                        g.do(lambda e, j=j, qt=qt, cp=cp: e.activation(
                            P_sb[1][:], S_ps[1][:], AF.Exp,
                            bias=nm[qt][cp % 2][:, 0:1],
                            accum_out=csum_b[j % 2][:, 0:1]))
                        g.mark("act", ("p6", "expb", j))
                        g.wait("pe", ("p6", "tra", j))
                        g.do(lambda e, j=j: e.copy(PT_sb[0][:], PT_ps[:]))
                        g.mark("act", ("p6", "ptva", j))
                        g.wait("pe", ("p6", "trb", j))
                        g.do(lambda e, j=j: e.copy(PT_sb[1][:], PT_ps[:]))
                        g.mark("act", ("p6", "ptvb", j))

                run_phase(block, ph, {"sync": p6_sync, "gpsimd": p6_gp,
                                      "tensor": p6_pe, "vector": p6_vec,
                                      "scalar": p6_act})

    return nc


_N_CORES = 8


def kernel(L0_attn_input, L0_attn_Wq, L0_attn_bq, L0_attn_Wk, L0_attn_bk,
           L0_attn_Wv, L0_attn_bv, L0_ffn_gate_w, L0_ffn_up_w, L0_ffn_down_w,
           L0_ffn_gate_s, L0_ffn_up_s, L0_ffn_down_s):
    x = np.ascontiguousarray(np.asarray(L0_attn_input, dtype=np.float32))
    weights = {
        "wq": np.ascontiguousarray(np.asarray(L0_attn_Wq, np.float32)),
        "wk": np.ascontiguousarray(np.asarray(L0_attn_Wk, np.float32)),
        "wv": np.ascontiguousarray(np.asarray(L0_attn_Wv, np.float32)),
        "bq": np.ascontiguousarray(np.asarray(L0_attn_bq, np.float32)),
        "bk": np.ascontiguousarray(np.asarray(L0_attn_bk, np.float32)),
        "bv": np.ascontiguousarray(np.asarray(L0_attn_bv, np.float32)),
        "gw": np.ascontiguousarray(np.asarray(L0_ffn_gate_w, np.float32)),
        "uw": np.ascontiguousarray(np.asarray(L0_ffn_up_w, np.float32)),
        "dw": np.ascontiguousarray(np.asarray(L0_ffn_down_w, np.float32)),
        "gs": np.ascontiguousarray(np.asarray(L0_ffn_gate_s, np.float32)),
        "us": np.ascontiguousarray(np.asarray(L0_ffn_up_s, np.float32)),
        "ds": np.ascontiguousarray(np.asarray(L0_ffn_down_s, np.float32)),
    }
    in_maps = []
    for c in range(_N_CORES):
        b, s = c // 4, c % 4
        xb = x[b]
        xr = np.ascontiguousarray(
            np.concatenate([xb[s * TC:], xb[:s * TC]], axis=0))
        m = {"x_full": xr}
        m.update(weights)
        in_maps.append(m)

    nc = build_program()
    res = run_bass_kernel_spmd(nc, in_maps, core_ids=list(range(_N_CORES)))

    out = np.empty((2, SEQ, D), dtype=np.float32)
    for c in range(_N_CORES):
        b, s = c // 4, c % 4
        out[b, s * TC:(s + 1) * TC, :] = res.results[c]["out"]
    return out


# version 08
# speedup vs baseline: 1.0728x; 1.0363x over previous
import numpy as np
from contextlib import ExitStack

import concourse.bass as bass
import concourse.mybir as mybir
from concourse.bass_utils import run_bass_kernel_spmd

F32 = mybir.dt.float32
BF16 = mybir.dt.bfloat16
AF = mybir.ActivationFunctionType
ALU = mybir.AluOpType
AX = mybir.AxisListType

P = 128
D = 2048
HT = D // P
SEQ = 16384
TC = 4096
NT = SEQ // P
NB = SEQ // 512
I5 = 5632
ITI = I5 // P
CH = 512
NCH = SEQ // CH
KCH = 1024
NKC = SEQ // KCH
QB = 512
NQB = TC // QB
NU = NQB * NCH * 4

ENGINE_DMA_SEMS = {
    "gpsimd": ["dgp_misc", "x0", "x1", "x2", "x3", "xw0", "xw1", "xr0", "xr1", "xb",
               "xo", "pw", "qt60", "qt61", "pt60", "pt61", "ot60", "ot61"],
    "sync": ["wl0", "wl1", "dw0", "dw1", "dw2", "dw3", "sg0", "sg1",
             "su0", "su1", "dts0", "dts1", "k60", "k61", "v60", "v61"],
    "scalar": ["ks0", "ks1", "vs0", "vs1", "qs0", "qs1", "st0", "st1",
               "dta0", "dta1"],
}


class Phase:
    def __init__(self, sems):
        self.sems = sems
        self.marks = {}
        self.tot = {k: 0 for k in sems}
        self.bar_prev = 0


CHAIN_SEM = {"vector": "vec", "scalar": "act", "gpsimd": "gp"}


class Gen:
    __slots__ = ("ph", "eng", "cnt", "chain", "last", "last_waited")

    def __init__(self, ph, eng, cnt, engine_name):
        self.ph = ph
        self.eng = eng
        self.cnt = cnt
        self.chain = CHAIN_SEM.get(engine_name)
        self.last = None
        self.last_waited = cnt.get(self.chain, 0) if self.chain else 0

    def do(self, f, dma=False):
        c = self.chain
        if c is not None and self.cnt[c] > self.last_waited:
            if self.eng is not None:
                self.eng.wait_ge(self.ph.sems[c], self.cnt[c])
            self.last_waited = self.cnt[c]
        if self.eng is not None:
            self.last = f(self.eng)
        if c is not None and not dma:
            self.cnt[c] += 1
            if self.eng is not None:
                self.last.then_inc(self.ph.sems[c], 1)
        return self.last

    def dma(self, f):
        return self.do(f, dma=True)

    def inc(self, sem, n=1, tag=None):
        self.cnt[sem] += n
        if self.eng is None:
            if tag is not None:
                key = (sem,) + tag
                assert key not in self.ph.marks, key
                self.ph.marks[key] = self.cnt[sem]
        else:
            self.last.then_inc(self.ph.sems[sem], n)

    def mark(self, sem, tag):
        if self.eng is None:
            key = (sem,) + tag
            assert key not in self.ph.marks, key
            self.ph.marks[key] = self.cnt[sem]

    def wait(self, sem, tag):
        if self.eng is not None:
            self.eng.wait_ge(self.ph.sems[sem], self.ph.marks[(sem,) + tag])


def run_phase(block, ph, emitters):
    base = dict(ph.tot)
    for name in ("tensor", "vector", "scalar", "sync", "gpsimd"):
        fn = emitters.get(name)
        if fn:
            fn(Gen(ph, None, ph.tot, name))
    ph.tot["bar"] += 5
    bar_target = ph.tot["bar"]
    prev = ph.bar_prev

    def mk(name, fn):
        def body(eng):
            if prev > 0:
                eng.wait_ge(ph.sems["bar"], prev)
            if fn:
                g2 = Gen(ph, eng, dict(base), name)
                fn(g2)
            for s in ENGINE_DMA_SEMS.get(name, []):
                if ph.tot[s] > 0:
                    eng.wait_ge(ph.sems[s], ph.tot[s])
            cs = CHAIN_SEM.get(name)
            if cs is not None and ph.tot[cs] > 0:
                eng.wait_ge(ph.sems[cs], ph.tot[cs])
            eng.nop().then_inc(ph.sems["bar"], 1)
        return body

    block.tensor(mk("tensor", emitters.get("tensor")))
    block.vector(mk("vector", emitters.get("vector")))
    block.scalar(mk("scalar", emitters.get("scalar")))
    block.sync(mk("sync", emitters.get("sync")))
    block.gpsimd(mk("gpsimd", emitters.get("gpsimd")))
    ph.bar_prev = bar_target


def tpos(tt8):
    return (tt8 // 4) * 512 + (tt8 % 4) * 128


def build_program(debug=False):
    nc = bass.Bass("TRN2", target_bir_lowering=False)

    x_full = nc.dram_tensor("x_full", [SEQ, D], F32, kind="ExternalInput").ap()
    wq = nc.dram_tensor("wq", [D, D], F32, kind="ExternalInput").ap()
    wk = nc.dram_tensor("wk", [D, D], F32, kind="ExternalInput").ap()
    wv = nc.dram_tensor("wv", [D, D], F32, kind="ExternalInput").ap()
    bq = nc.dram_tensor("bq", [D], F32, kind="ExternalInput").ap()
    bk = nc.dram_tensor("bk", [D], F32, kind="ExternalInput").ap()
    bv = nc.dram_tensor("bv", [D], F32, kind="ExternalInput").ap()
    gw = nc.dram_tensor("gw", [I5, D], F32, kind="ExternalInput").ap()
    uw = nc.dram_tensor("uw", [I5, D], F32, kind="ExternalInput").ap()
    dw = nc.dram_tensor("dw", [D, I5], F32, kind="ExternalInput").ap()
    gs = nc.dram_tensor("gs", [I5], F32, kind="ExternalInput").ap()
    us = nc.dram_tensor("us", [I5], F32, kind="ExternalInput").ap()
    ds = nc.dram_tensor("ds", [D], F32, kind="ExternalInput").ap()
    out_e = nc.dram_tensor("out", [TC, D], F32, kind="ExternalOutput").ap()

    xnt_d = nc.dram_tensor("xnt_d", [NB, P, 16 * 512], BF16).ap()
    k_d = nc.dram_tensor("k_d", [NKC, P, HT * KCH], BF16).ap()
    v_d = nc.dram_tensor("v_d", [NKC, P, 8 * D], BF16).ap()
    qt_d = nc.dram_tensor("qt_d", [NQB, P, HT * 512], BF16).ap()
    gt_d = nc.dram_tensor("gt_d", [ITI, P, D], BF16).ap()
    ut_d = nc.dram_tensor("ut_d", [ITI, P, D], BF16).ap()
    dt_d = nc.dram_tensor("dt_d", [ITI, P, D], BF16).ap()
    part_d = nc.dram_tensor("part_d", [TC // P, P, D], F32).ap()

    dbg = {}
    if debug:
        for nm, src in (("dbg_xnt", xnt_d), ("dbg_k", k_d), ("dbg_v", v_d),
                        ("dbg_qt", qt_d), ("dbg_gt", gt_d), ("dbg_dt", dt_d),
                        ("dbg_part", part_d)):
            dbg[nm] = (nc.dram_tensor(nm, list(src.shape), src.dtype,
                                      kind="ExternalOutput").ap(), src)

    with ExitStack() as G:
        E = G.enter_context
        sems = {}
        sem_names = (["bar", "gp", "pe", "vec", "act", "fin", "dgp_misc",
                      "x0", "x1", "x2", "x3", "xw0", "xw1", "xr0", "xr1", "xb", "xo",
                      "pw", "qt60", "qt61", "pt60", "pt61", "ot60", "ot61",
                      "wl0", "wl1", "dw0", "dw1", "dw2", "dw3",
                      "sg0", "sg1", "su0", "su1", "dts0", "dts1",
                      "k60", "k61", "v60", "v61",
                      "ks0", "ks1", "vs0", "vs1", "qs0", "qs1",
                      "st0", "st1", "dta0", "dta1"])
        for s in sem_names:
            sems[s] = E(nc.semaphore(s))
        ph = Phase(sems)

        ident = E(nc.sbuf_tensor("ident", [P, P], BF16))
        identf = E(nc.sbuf_tensor("identf", [P, P], F32))
        ones = E(nc.sbuf_tensor("ones", [1, P], F32))
        bqc = E(nc.sbuf_tensor("bqc", [P, HT], F32))
        bkc = E(nc.sbuf_tensor("bkc", [P, HT], F32))
        gsc = E(nc.sbuf_tensor("gsc", [P, ITI], F32))
        usc = E(nc.sbuf_tensor("usc", [P, ITI], F32))
        dsc_sb = E(nc.sbuf_tensor("dsc_sb", [P, HT], F32))

        block = E(nc.Block())

        with ExitStack() as C:
            st44 = C.enter_context(nc.sbuf_tensor("st44", [44, P], F32))
            ps0 = C.enter_context(nc.psum_tensor("ps0", [P, 256], F32))

            loads = [
                (bq.rearrange("(a f) -> a f", a=16), 16, bqc),
                (bk.rearrange("(a f) -> a f", a=16), 16, bkc),
                (ds.rearrange("(a f) -> a f", a=16), 16, dsc_sb),
                (gs.rearrange("(a f) -> a f", a=44), 44, gsc),
                (us.rearrange("(a f) -> a f", a=44), 44, usc),
            ]

            def p0_gp(g):
                g.do(lambda e: e.memset(ident[:], 0.0))
                g.do(lambda e: e.memset(identf[:], 0.0))
                g.do(lambda e: e.memset(ones[:], 1.0))
                g.do(lambda e: e.affine_select(
                    out=ident[:], in_=ident[:], compare_op=ALU.not_equal,
                    fill=1.0, base=0, pattern=[[-1, P]], channel_multiplier=1))
                g.do(lambda e: e.affine_select(
                    out=identf[:], in_=identf[:], compare_op=ALU.not_equal,
                    fill=1.0, base=0, pattern=[[-1, P]], channel_multiplier=1))
                g.mark("gp", ("p0", "const"))
                for j, (src, rows, _) in enumerate(loads):
                    if j > 0:
                        g.wait("pe", ("p0", "tr", j - 1))
                    g.dma(lambda e, src=src, rows=rows:
                         e.dma_start(out=st44[0:rows, :], in_=src))
                    g.inc("dgp_misc", 16, ("p0", "ld", j))

            def p0_pe(g):
                g.wait("gp", ("p0", "const"))
                off = 0
                for j, (_, rows, _) in enumerate(loads):
                    g.wait("dgp_misc", ("p0", "ld", j))
                    if j > 0:
                        g.wait("vec", ("p0", "ev", j - 1))
                    g.do(lambda e, off=off, rows=rows: e.transpose(
                        ps0[:, off:off + rows], st44[0:rows, :],
                        identf[0:rows, 0:rows]))
                    g.inc("pe", 1, ("p0", "tr", j))
                    off += rows

            def p0_vec(g):
                off = 0
                for j, (_, rows, dest) in enumerate(loads):
                    g.wait("pe", ("p0", "tr", j))
                    g.do(lambda e, off=off, rows=rows, dest=dest:
                         e.tensor_copy(dest[:, 0:rows], ps0[:, off:off + rows]))
                    g.mark("vec", ("p0", "ev", j))
                    off += rows

            run_phase(block, ph, {"gpsimd": p0_gp, "tensor": p0_pe,
                                  "vector": p0_vec})

        with ExitStack() as C1:
            E1 = C1.enter_context
            WT = E1(nc.sbuf_tensor("WT", [P, HT * D], BF16))
            wrow = [E1(nc.sbuf_tensor(f"wrow{i}", [P, D], F32)) for i in range(2)]
            wbf = [E1(nc.sbuf_tensor(f"wbf{i}", [P, D], BF16)) for i in range(2)]
            xs = [E1(nc.sbuf_tensor(f"xs{i}", [P, D], F32)) for i in range(4)]
            sqs = E1(nc.sbuf_tensor("sqs", [P, D], F32))
            xnbf = [E1(nc.sbuf_tensor(f"xnbf{i}", [P, D], BF16)) for i in range(2)]
            stage = [E1(nc.sbuf_tensor(f"stage{i}", [P, 16 * 512], BF16)) for i in range(2)]
            ktbf = [E1(nc.sbuf_tensor(f"ktbf{i}", [P, 512], BF16)) for i in range(2)]
            vbf = [E1(nc.sbuf_tensor(f"vbf{i}", [P, D], BF16)) for i in range(2)]
            qtbf = [E1(nc.sbuf_tensor(f"qtbf{i}", [P, 512], BF16)) for i in range(2)]
            bvb = E1(nc.sbuf_tensor("bvb", [P, D], F32))
            bvrow = E1(nc.sbuf_tensor("bvrow", [1, D], F32))
            stats = {nm: [E1(nc.sbuf_tensor(f"{nm}{i}", [P, 1], F32))
                          for i in range(4)]
                     for nm in ("sumv", "ssq", "mb", "t2", "t3", "stdv", "rstd")}

            def wt_build(pn, w_ap, extra_pe=None, extra_vec=None, extra_gp=None):
                def f_sync(g):
                    for ot in range(HT):
                        if ot >= 2:
                            g.wait("vec", (pn, "cast", ot - 2))
                        g.dma(lambda e, ot=ot: e.dma_start(
                            out=wrow[ot % 2][:], in_=w_ap[ot * P:(ot + 1) * P, :]))
                        g.inc(f"wl{ot % 2}", 16, (pn, "wl", ot))

                def f_vec(g):
                    for ot in range(HT):
                        g.wait(f"wl{ot % 2}", (pn, "wl", ot))
                        if ot >= 2:
                            g.wait("pe", (pn, "tr", ot - 2))
                        g.do(lambda e, ot=ot: e.tensor_copy(
                            wbf[ot % 2][:], wrow[ot % 2][:]))
                        g.mark("vec", (pn, "cast", ot))
                        g.wait("pe", (pn, "tr", ot))
                        dst = WT.ap().rearrange("p (ht o) -> p ht o", ht=HT)
                        g.do(lambda e, ot=ot, dst=dst: e.tensor_copy(
                            dst[:, :, ot * P:(ot + 1) * P],
                            wt_ps.ap().rearrange("p (ht o) -> p ht o", ht=HT)))
                        g.mark("vec", (pn, "wtev", ot))
                    if extra_vec:
                        extra_vec(g)

                def f_pe(g):
                    g.wait("gp", ("p0", "const"))
                    g.wait("vec", ("p0", "ev", 4))
                    for ot in range(HT):
                        g.wait("vec", (pn, "cast", ot))
                        if ot >= 1:
                            g.wait("vec", (pn, "wtev", ot - 1))
                        for ht in range(HT):
                            g.do(lambda e, ot=ot, ht=ht: e.transpose(
                                wt_ps[:, ht * P:(ht + 1) * P],
                                wbf[ot % 2][:, ht * P:(ht + 1) * P], ident[:]))
                        g.inc("pe", 1, (pn, "tr", ot))
                    if extra_pe:
                        extra_pe(g)

                return {"sync": f_sync, "vector": f_vec, "tensor": f_pe,
                        "gpsimd": extra_gp}

            with ExitStack() as CP:
                wt_ps = CP.enter_context(nc.psum_tensor("wt_ps1", [P, D], BF16))
                run_phase(block, ph, wt_build("p1a", wk))

            with ExitStack() as CP:
                xnT_ps = [CP.enter_context(nc.psum_tensor(f"xnT_ps{i}", [P, D], BF16))
                          for i in range(2)]
                kt_ps = [CP.enter_context(nc.psum_tensor(f"kt_ps{i}", [P, 512], F32))
                         for i in range(2)]

                def p1b_gp(g):
                    for tb in range(NB):
                        for tt in range(4):
                            i = 4 * tb + tt
                            if i >= 4:
                                g.wait("vec", ("p1b", "xn", i - 4))
                            g.dma(lambda e, i=i: e.dma_start(
                                out=xs[i % 4][:], in_=x_full[i * P:(i + 1) * P, :]))
                            g.inc(f"x{i % 4}", 16, ("p1b", "x", i))
                        g.wait("vec", ("p1b", "sev", 4 * tb + 3))
                        g.dma(lambda e, tb=tb: e.dma_start(
                            out=xnt_d[tb], in_=stage[tb % 2][:]))
                        g.inc(f"xw{tb % 2}", 16, ("p1b", "xw", tb))

                def p1b_vec(g):
                    for tb in range(NB):
                        for tt in range(4):
                            i = 4 * tb + tt
                            p = i % 4
                            g.wait(f"x{i % 4}", ("p1b", "x", i))
                            if i >= 4:
                                g.wait("act", ("p1b", "sqrt", i - 4))
                            g.do(lambda e, p=p: e.reduce_sum(
                                stats["sumv"][p][:], xs[p][:], axis=AX.X))
                            g.do(lambda e, p=p: e.scalar_tensor_tensor(
                                sqs[:], xs[p][:], 1.0, xs[p][:],
                                ALU.mult, ALU.mult,
                                accum_out=stats["ssq"][p][:]))
                            g.do(lambda e, p=p: e.tensor_scalar_mul(
                                stats["mb"][p][:], stats["sumv"][p][:], 1.0 / D))
                            g.do(lambda e, p=p: e.tensor_mul(
                                stats["t2"][p][:], stats["mb"][p][:],
                                stats["sumv"][p][:]))
                            g.do(lambda e, p=p: e.tensor_sub(
                                stats["t3"][p][:], stats["ssq"][p][:],
                                stats["t2"][p][:]))
                            g.do(lambda e, p=p: e.tensor_scalar(
                                stats["t3"][p][:], stats["t3"][p][:],
                                1.0 / D, 1e-5, ALU.mult, ALU.add))
                            g.mark("vec", ("p1b", "t3", i))
                        for tt in range(4):
                            i = 4 * tb + tt
                            p = i % 4
                            g.wait("act", ("p1b", "sqrt", i))
                            g.do(lambda e, p=p: e.reciprocal(
                                stats["rstd"][p][:], stats["stdv"][p][:]))
                            g.do(lambda e, p=p: e.tensor_scalar(
                                xnbf[i % 2][:], xs[p][:], stats["mb"][p][:, 0:1],
                                stats["rstd"][p][:, 0:1],
                                ALU.subtract, ALU.mult))
                            g.mark("vec", ("p1b", "xn", i))
                            g.wait("pe", ("p1b", "tr", i))
                            if tt == 0 and tb >= 2:
                                g.wait("pe", ("p1b", "k", (tb - 2) * HT + HT - 1))
                                g.wait(f"xw{tb % 2}", ("p1b", "xw", tb - 2))
                            dst = stage[tb % 2].ap().rearrange(
                                "p (ht t) -> p ht t", ht=HT)
                            g.do(lambda e, tt=tt, dst=dst, i=i: e.tensor_copy(
                                dst[:, :, tt * P:(tt + 1) * P],
                                xnT_ps[i % 2].ap().rearrange(
                                    "p (ht t) -> p ht t", ht=HT)))
                            g.mark("vec", ("p1b", "sev", i))

                def p1b_act(g):
                    g.wait("vec", ("p0", "ev", 1))
                    for tb in range(NB):
                        for tt in range(4):
                            i = 4 * tb + tt
                            p = i % 4
                            g.wait("vec", ("p1b", "t3", i))
                            if i >= 4:
                                g.wait("vec", ("p1b", "xn", i - 4))
                            g.do(lambda e, p=p: e.activation(
                                stats["stdv"][p][:], stats["t3"][p][:], AF.Sqrt))
                            g.mark("act", ("p1b", "sqrt", i))
                            if tb >= 1:
                                for e4 in range(4):
                                    ot = 4 * tt + e4
                                    kg = (tb - 1) * HT + ot
                                    g.wait("pe", ("p1b", "k", kg))
                                    if kg >= 2:
                                        g.wait(f"ks{kg % 2}", ("p1b", "ks", kg - 2))
                                    g.do(lambda e, kg=kg, ot=ot: e.activation(
                                        ktbf[kg % 2][:], kt_ps[ot % 2][:],
                                        AF.Identity, bias=bkc[:, ot:ot + 1]))
                                    g.mark("act", ("p1b", "ktev", kg))
                                    tb1 = tb - 1
                                    g.dma(lambda e, kg=kg, tb1=tb1, ot=ot:
                                         e.dma_start(
                                             out=k_d[tb1 // 2, :,
                                                     ot * KCH + (tb1 % 2) * 512:
                                                     ot * KCH + (tb1 % 2) * 512 + 512],
                                             in_=ktbf[kg % 2][:]))
                                    g.inc(f"ks{kg % 2}", 16, ("p1b", "ks", kg))
                    tb1 = NB - 1
                    for ot in range(HT):
                        kg = tb1 * HT + ot
                        g.wait("pe", ("p1b", "k", kg))
                        if kg >= 2:
                            g.wait(f"ks{kg % 2}", ("p1b", "ks", kg - 2))
                        g.do(lambda e, kg=kg, ot=ot: e.activation(
                            ktbf[kg % 2][:], kt_ps[ot % 2][:], AF.Identity,
                            bias=bkc[:, ot:ot + 1]))
                        g.mark("act", ("p1b", "ktev", kg))
                        g.dma(lambda e, kg=kg, tb1=tb1, ot=ot: e.dma_start(
                            out=k_d[tb1 // 2, :,
                                    ot * KCH + (tb1 % 2) * 512:
                                    ot * KCH + (tb1 % 2) * 512 + 512],
                            in_=ktbf[kg % 2][:]))
                        g.inc(f"ks{kg % 2}", 16, ("p1b", "ks", kg))

                def p1b_pe(g):
                    g.wait("vec", ("p1a", "wtev", HT - 1))
                    for tb in range(NB):
                        for tt in range(4):
                            i = 4 * tb + tt
                            g.wait("vec", ("p1b", "xn", i))
                            if i >= 2:
                                g.wait("vec", ("p1b", "sev", i - 2))
                            for ht in range(HT):
                                g.do(lambda e, i=i, ht=ht: e.transpose(
                                    xnT_ps[i % 2][:, ht * P:(ht + 1) * P],
                                    xnbf[i % 2][:, ht * P:(ht + 1) * P],
                                    ident[:]))
                            g.inc("pe", 1, ("p1b", "tr", i))
                        g.wait("vec", ("p1b", "sev", 4 * tb + 3))
                        for ot in range(HT):
                            kg = tb * HT + ot
                            if kg >= 2:
                                g.wait("act", ("p1b", "ktev", kg - 2))
                            for ht in range(HT):
                                g.do(lambda e, ot=ot, ht=ht, tb=tb: e.matmul(
                                    kt_ps[ot % 2][:],
                                    WT[:, ht * D + ot * P: ht * D + (ot + 1) * P],
                                    stage[tb % 2][:, ht * 512:(ht + 1) * 512],
                                    start=(ht == 0), stop=(ht == HT - 1)))
                            g.inc("pe", 1, ("p1b", "k", kg))

                run_phase(block, ph, {"gpsimd": p1b_gp, "vector": p1b_vec,
                                      "scalar": p1b_act, "tensor": p1b_pe})

            with ExitStack() as CP:
                wt_ps = CP.enter_context(nc.psum_tensor("wt_ps2", [P, D], BF16))
                pbv = CP.enter_context(nc.psum_tensor("pbv", [P, D], F32))

                def p2a_gp(g):
                    g.dma(lambda e: e.dma_start(
                        out=bvrow[:], in_=bv.rearrange("(a f) -> a f", a=1)))
                    g.inc("dgp_misc", 16, ("p2a", "bv"))

                def p2a_pe_extra(g):
                    g.wait("dgp_misc", ("p2a", "bv"))
                    for k in range(4):
                        g.do(lambda e, k=k: e.matmul(
                            pbv[:, k * 512:(k + 1) * 512], ones[0:1, :],
                            bvrow[0:1, k * 512:(k + 1) * 512],
                            start=True, stop=True))
                    g.inc("pe", 1, ("p2a", "bvmm"))

                def p2a_vec_extra(g):
                    g.wait("pe", ("p2a", "bvmm"))
                    g.do(lambda e: e.tensor_copy(bvb[:], pbv[:]))
                    g.mark("vec", ("p2a", "bvev"))

                run_phase(block, ph, wt_build("p2a", wv, extra_pe=p2a_pe_extra,
                                              extra_vec=p2a_vec_extra,
                                              extra_gp=p2a_gp))

            with ExitStack() as CP:
                v_ps = [CP.enter_context(nc.psum_tensor(f"v_ps{i}", [P, D], F32))
                        for i in range(2)]

                def p2b_gp(g):
                    for tb in range(NB):
                        if tb >= 2:
                            g.wait("pe", ("p2b", "v", 4 * (tb - 2) + 3))
                        g.dma(lambda e, tb=tb: e.dma_start(
                            out=stage[tb % 2][:], in_=xnt_d[tb]))
                        g.inc(f"xr{tb % 2}", 16, ("p2b", "xr", tb))

                def p2b_pe(g):
                    g.wait("vec", ("p2a", "wtev", HT - 1))
                    g.wait("vec", ("p2a", "bvev"))
                    for tb in range(NB):
                        g.wait(f"xr{tb % 2}", ("p2b", "xr", tb))
                        for tt in range(4):
                            vg = 4 * tb + tt
                            if vg >= 2:
                                g.wait("vec", ("p2b", "vev", vg - 2))
                            for ht in range(HT):
                                for oc in range(4):
                                    g.do(lambda e, vg=vg, tb=tb, tt=tt, ht=ht,
                                         oc=oc: e.matmul(
                                             v_ps[vg % 2][:, oc * 512:(oc + 1) * 512],
                                             stage[tb % 2][:, ht * 512 + tt * P:
                                                           ht * 512 + (tt + 1) * P],
                                             WT[:, ht * D + oc * 512:
                                                ht * D + (oc + 1) * 512],
                                             start=(ht == 0), stop=(ht == HT - 1)))
                            g.inc("pe", 1, ("p2b", "v", vg))

                def p2b_vec(g):
                    for vg in range(NT):
                        g.wait("pe", ("p2b", "v", vg))
                        if vg >= 2:
                            g.wait(f"vs{vg % 2}", ("p2b", "vs", vg - 2))
                        g.do(lambda e, vg=vg: e.tensor_add(
                            vbf[vg % 2][:], v_ps[vg % 2][:], bvb[:]))
                        g.mark("vec", ("p2b", "vev", vg))

                def p2b_act(g):
                    for vg in range(NT):
                        g.wait("vec", ("p2b", "vev", vg))
                        g.dma(lambda e, vg=vg: e.dma_start(
                            out=v_d[vg // 8, :, (vg % 8) * D:(vg % 8 + 1) * D],
                            in_=vbf[vg % 2][:]))
                        g.inc(f"vs{vg % 2}", 16, ("p2b", "vs", vg))

                run_phase(block, ph, {"gpsimd": p2b_gp, "tensor": p2b_pe,
                                      "vector": p2b_vec, "scalar": p2b_act})

            with ExitStack() as CP:
                wt_ps = CP.enter_context(nc.psum_tensor("wt_ps3", [P, D], BF16))
                run_phase(block, ph, wt_build("p3a", wq))

            with ExitStack() as CP:
                qt_ps = [CP.enter_context(nc.psum_tensor(f"qt_ps{i}", [P, 512], F32))
                         for i in range(2)]

                def p3b_gp(g):
                    for qc in range(NQB):
                        if qc >= 2:
                            g.wait("pe", ("p3b", "q", (qc - 2) * HT + HT - 1))
                        g.dma(lambda e, qc=qc: e.dma_start(
                            out=stage[qc % 2][:], in_=xnt_d[qc]))
                        g.inc(f"xr{qc % 2}", 16, ("p3b", "xr", qc))

                def p3b_pe(g):
                    g.wait("vec", ("p3a", "wtev", HT - 1))
                    for qc in range(NQB):
                        g.wait(f"xr{qc % 2}", ("p3b", "xr", qc))
                        for ot in range(HT):
                            k = qc * HT + ot
                            if k >= 2:
                                g.wait("act", ("p3b", "qev", k - 2))
                            for ht in range(HT):
                                g.do(lambda e, k=k, qc=qc, ot=ot, ht=ht:
                                     e.matmul(
                                         qt_ps[k % 2][:],
                                         WT[:, ht * D + ot * P:
                                            ht * D + (ot + 1) * P],
                                         stage[qc % 2][:, ht * 512:(ht + 1) * 512],
                                         start=(ht == 0), stop=(ht == HT - 1)))
                            g.inc("pe", 1, ("p3b", "q", k))

                def p3b_act(g):
                    g.wait("vec", ("p0", "ev", 0))
                    for qc in range(NQB):
                        for ot in range(HT):
                            k = qc * HT + ot
                            g.wait("pe", ("p3b", "q", k))
                            if k >= 2:
                                g.wait(f"qs{k % 2}", ("p3b", "qs", k - 2))
                            g.do(lambda e, k=k, ot=ot: e.activation(
                                qtbf[k % 2][:], qt_ps[k % 2][:], AF.Identity,
                                bias=bqc[:, ot:ot + 1]))
                            g.mark("act", ("p3b", "qev", k))
                            g.dma(lambda e, k=k, qc=qc, ot=ot: e.dma_start(
                                out=qt_d[qc, :, ot * 512:(ot + 1) * 512],
                                in_=qtbf[k % 2][:]))
                            g.inc(f"qs{k % 2}", 16, ("p3b", "qs", k))

                run_phase(block, ph, {"gpsimd": p3b_gp, "tensor": p3b_pe,
                                      "scalar": p3b_act})

        with ExitStack() as C2:
            E2 = C2.enter_context
            gw_st = [E2(nc.sbuf_tensor(f"gw_st{i}", [P, D], F32)) for i in range(2)]
            gubf = [E2(nc.sbuf_tensor(f"gubf{i}", [P, D], BF16)) for i in range(2)]
            gu_stage = [E2(nc.sbuf_tensor(f"gu_stage{i}", [P, D], BF16))
                        for i in range(2)]
            with ExitStack() as CP:
                gt_ps = [CP.enter_context(nc.psum_tensor(f"gt_ps{i}", [P, D], BF16))
                         for i in range(2)]
                NK = 2 * ITI

                def p4a_sync(g):
                    for k in range(NK):
                        it, isup = k // 2, k % 2
                        if k >= 2:
                            g.wait("vec", ("p4a", "cast", k - 2))
                        src = uw if isup else gw
                        g.dma(lambda e, k=k, it=it, src=src: e.dma_start(
                            out=gw_st[k % 2][:], in_=src[it * P:(it + 1) * P, :]))
                        g.inc(f"wl{k % 2}", 16, ("p4a", "wl", k))

                def p4a_vec(g):
                    for k in range(NK):
                        it, isup = k // 2, k % 2
                        g.wait(f"wl{k % 2}", ("p4a", "wl", k))
                        if k >= 2:
                            g.wait("pe", ("p4a", "tr", k - 2))
                        sc = usc if isup else gsc
                        g.do(lambda e, k=k, it=it, sc=sc: e.tensor_scalar_mul(
                            gubf[k % 2][:], gw_st[k % 2][:], sc[:, it:it + 1]))
                        g.mark("vec", ("p4a", "cast", k))
                        g.wait("pe", ("p4a", "tr", k))
                        if k >= 2:
                            g.wait(f"st{k % 2}", ("p4a", "st", k - 2))
                        g.do(lambda e, k=k: e.tensor_copy(
                            gu_stage[k % 2][:], gt_ps[k % 2][:]))
                        g.mark("vec", ("p4a", "ev", k))

                def p4a_pe(g):
                    g.wait("gp", ("p0", "const"))
                    g.wait("act", ("p3b", "qev", NQB * HT - 1))
                    for k in range(NK):
                        g.wait("vec", ("p4a", "cast", k))
                        if k >= 2:
                            g.wait("vec", ("p4a", "ev", k - 2))
                        for ht in range(HT):
                            g.do(lambda e, k=k, ht=ht: e.transpose(
                                gt_ps[k % 2][:, ht * P:(ht + 1) * P],
                                gubf[k % 2][:, ht * P:(ht + 1) * P], ident[:]))
                        g.inc("pe", 1, ("p4a", "tr", k))

                def p4a_act(g):
                    for k in range(NK):
                        it, isup = k // 2, k % 2
                        g.wait("vec", ("p4a", "ev", k))
                        dst = ut_d if isup else gt_d
                        g.dma(lambda e, k=k, it=it, dst=dst: e.dma_start(
                            out=dst[it], in_=gu_stage[k % 2][:]))
                        g.inc(f"st{k % 2}", 16, ("p4a", "st", k))

                run_phase(block, ph, {"sync": p4a_sync, "vector": p4a_vec,
                                      "tensor": p4a_pe, "scalar": p4a_act})

        with ExitStack() as C2:
            E2 = C2.enter_context
            dw_st = [E2(nc.sbuf_tensor(f"dw_st{i}", [P, I5], F32)) for i in range(4)]
            dwbf = [E2(nc.sbuf_tensor(f"dwbf{i}", [P, I5], BF16)) for i in range(4)]
            dt_stage = [E2(nc.sbuf_tensor(f"dt_stage{i}", [P, 512], BF16))
                        for i in range(2)]
            with ExitStack() as CP:
                dt_ps = [CP.enter_context(nc.psum_tensor(f"dt_ps{i}", [P, 512], BF16))
                         for i in range(2)]

                def p4b_sync(g):
                    for h4 in range(4):
                        for kk in range(4):
                            if h4 >= 1:
                                g.wait("pe", ("p4b", "tr", h4 - 1, ITI - 1))
                            g.dma(lambda e, h4=h4, kk=kk: e.dma_start(
                                out=dw_st[kk][:],
                                in_=dw[(h4 * 4 + kk) * P:(h4 * 4 + kk + 1) * P, :]))
                            g.inc(f"dw{kk}", 16, ("p4b", "wl", h4, kk))

                def p4b_vec(g):
                    for h4 in range(4):
                        for kk in range(4):
                            g.wait(f"dw{kk}", ("p4b", "wl", h4, kk))
                            if h4 >= 1:
                                g.wait("pe", ("p4b", "tr", h4 - 1, ITI - 1))
                            g.do(lambda e, h4=h4, kk=kk: e.tensor_scalar_mul(
                                dwbf[kk][:], dw_st[kk][:],
                                dsc_sb[:, h4 * 4 + kk:h4 * 4 + kk + 1]))
                            g.mark("vec", ("p4b", "cast", h4, kk))
                        for it in range(ITI):
                            f = h4 * ITI + it
                            g.wait("pe", ("p4b", "tr", h4, it))
                            if f >= 2:
                                g.wait(f"st{f % 2}", ("p4b", "st", f - 2))
                            g.do(lambda e, it=it: e.tensor_copy(
                                dt_stage[it % 2][:], dt_ps[it % 2][:]))
                            g.mark("vec", ("p4b", "ev", f))

                def p4b_pe(g):
                    g.wait("gp", ("p0", "const"))
                    g.wait("vec", ("p4a", "ev", 2 * ITI - 1))
                    for h4 in range(4):
                        for kk in range(4):
                            g.wait("vec", ("p4b", "cast", h4, kk))
                        for it in range(ITI):
                            f = h4 * ITI + it
                            if f >= 2:
                                g.wait("vec", ("p4b", "ev", f - 2))
                            for kk in range(4):
                                g.do(lambda e, it=it, kk=kk: e.transpose(
                                    dt_ps[it % 2][:, kk * P:(kk + 1) * P],
                                    dwbf[kk][:, it * P:(it + 1) * P], ident[:]))
                            g.inc("pe", 1, ("p4b", "tr", h4, it))

                def p4b_act(g):
                    for h4 in range(4):
                        for it in range(ITI):
                            f = h4 * ITI + it
                            g.wait("vec", ("p4b", "ev", f))
                            g.dma(lambda e, h4=h4, it=it: e.dma_start(
                                out=dt_d[it, :, h4 * 512:(h4 + 1) * 512],
                                in_=dt_stage[it % 2][:]))
                            g.inc(f"st{f % 2}", 16, ("p4b", "st", f))

                run_phase(block, ph, {"sync": p4b_sync, "vector": p4b_vec,
                                      "tensor": p4b_pe, "scalar": p4b_act})

        with ExitStack() as C3:
            E3 = C3.enter_context
            xnT_b = E3(nc.sbuf_tensor("xnT_b", [P, 2 * 16 * 512], BF16))
            hdn = E3(nc.sbuf_tensor("hdn", [P, ITI * 1024], BF16))
            gt_t = [E3(nc.sbuf_tensor(f"gt_t{i}", [P, D], BF16)) for i in range(2)]
            ut_t = [E3(nc.sbuf_tensor(f"ut_t{i}", [P, D], BF16)) for i in range(2)]
            silu_g = [E3(nc.sbuf_tensor(f"silu_g{i}", [P, 1024], F32))
                      for i in range(2)]
            dt_t = [E3(nc.sbuf_tensor(f"dt_t{i}", [P, D], BF16)) for i in range(4)]
            xo = [E3(nc.sbuf_tensor(f"xo{i}", [P, D], F32)) for i in range(2)]
            part_sb = [E3(nc.sbuf_tensor(f"part_sb{i}", [P, D], F32))
                       for i in range(2)]
            with ExitStack() as CP:
                pgu = CP.enter_context(nc.psum_tensor("pgu", [P, D], F32))
                pd = CP.enter_context(nc.psum_tensor("pd", [P, D], F32))
                NBLK = TC // 1024

                def p5_gp(g):
                    for blk in range(NBLK):
                        if blk >= 1:
                            g.wait("pe", ("p5", "gu", blk - 1, 2 * ITI - 1))
                        g.dma(lambda e, blk=blk: e.dma_start(
                            out=xnT_b[:, 0:8192], in_=xnt_d[2 * blk]))
                        g.inc("xb", 16)
                        g.dma(lambda e, blk=blk: e.dma_start(
                            out=xnT_b[:, 8192:16384], in_=xnt_d[2 * blk + 1]))
                        g.inc("xb", 16, ("p5", "xb", blk))
                        for pr in range(4):
                            if (blk, pr) != (0, 0):
                                pblk, ppr = (blk, pr - 1) if pr else (blk - 1, 3)
                                g.wait("vec", ("p5", "pev", pblk, ppr))
                            for e2 in range(2):
                                g128 = blk * 8 + 2 * pr + e2
                                g.dma(lambda e, g128=g128, e2=e2: e.dma_start(
                                    out=xo[e2][:],
                                    in_=x_full[g128 * P:(g128 + 1) * P, :]))
                                g.inc("xo", 16,
                                      ("p5", "xo", blk, pr) if e2 else None)
                            g.wait("vec", ("p5", "pev", blk, pr))
                            for e2 in range(2):
                                g128 = blk * 8 + 2 * pr + e2
                                g.dma(lambda e, g128=g128, e2=e2: e.dma_start(
                                    out=part_d[g128], in_=part_sb[e2][:]))
                                g.inc("pw", 16,
                                      ("p5", "pw", blk, pr) if e2 else None)

                def p5_sync(g):
                    for blk in range(NBLK):
                        for k in range(2 * ITI):
                            it, isup = k // 2, k % 2
                            if k >= 4 or blk > 0:
                                pk = (blk, k - 4) if k >= 4 else (blk - 1, 2 * ITI + k - 4)
                                g.wait("pe", ("p5", "gu", pk[0], pk[1]))
                            src = ut_d if isup else gt_d
                            buf = (ut_t if isup else gt_t)[it % 2]
                            g.dma(lambda e, src=src, it=it, buf=buf: e.dma_start(
                                out=buf[:], in_=src[it]))
                            g.inc(f"{'su' if isup else 'sg'}{(it) % 2}", 16,
                                  ("p5", "gu_l", blk, k))
                        for pr in range(4):
                            for it in range(0, ITI, 2):
                                if it >= 4 or (blk, pr) != (0, 0):
                                    if it >= 4:
                                        pt = ("p5", "dn", blk, pr, it - 4)
                                    elif pr:
                                        pt = ("p5", "dn", blk, pr - 1, ITI - 4 + it)
                                    else:
                                        pt = ("p5", "dn", blk - 1, 3, ITI - 4 + it)
                                    g.wait("pe", pt)
                                g.dma(lambda e, it=it: e.dma_start(
                                    out=dt_t[(it // 2) % 2][:], in_=dt_d[it]))
                                g.inc(f"dts{(it // 2) % 2}", 16,
                                      ("p5", "dt", blk, pr, it))

                def p5_act(g):
                    for blk in range(NBLK):
                        for it in range(ITI):
                            g.wait("pe", ("p5", "gu", blk, 2 * it))
                            if it >= 2:
                                g.wait("vec", ("p5", "hdn", blk, it - 2))
                            g.do(lambda e, it=it: e.activation(
                                silu_g[it % 2][:], pgu[:, 0:1024], AF.Silu))
                            g.mark("act", ("p5", "silu", blk, it))
                        for pr in range(4):
                            for it in range(1, ITI, 2):
                                if it >= 5 or (blk, pr) != (0, 0):
                                    if it >= 5:
                                        pt = ("p5", "dn", blk, pr, it - 4)
                                    elif pr:
                                        pt = ("p5", "dn", blk, pr - 1, ITI - 4 + it)
                                    else:
                                        pt = ("p5", "dn", blk - 1, 3, ITI - 4 + it)
                                    g.wait("pe", pt)
                                g.dma(lambda e, it=it: e.dma_start(
                                    out=dt_t[2 + ((it - 1) // 2) % 2][:],
                                    in_=dt_d[it]))
                                g.inc(f"dta{((it - 1) // 2) % 2}", 16,
                                      ("p5", "dta", blk, pr, it))

                def p5_pe(g):
                    g.wait("vec", ("p4b", "ev", 4 * ITI - 1))
                    for blk in range(NBLK):
                        g.wait("xb", ("p5", "xb", blk))
                        for k in range(2 * ITI):
                            it, isup = k // 2, k % 2
                            g.wait(f"{'su' if isup else 'sg'}{it % 2}",
                                   ("p5", "gu_l", blk, k))
                            if isup == 0:
                                if it >= 1:
                                    g.wait("act", ("p5", "silu", blk, it - 1))
                                elif blk >= 1:
                                    g.wait("vec", ("p5", "pev", blk - 1, 3))
                            else:
                                if it >= 1:
                                    g.wait("vec", ("p5", "hdn", blk, it - 1))
                            buf = (ut_t if isup else gt_t)[it % 2]
                            dst = pgu[:, 1024:2048] if isup else pgu[:, 0:1024]
                            for ht in range(HT):
                                for tc in range(2):
                                    g.do(lambda e, dst=dst, buf=buf, ht=ht,
                                         tc=tc: e.matmul(
                                             dst[:, tc * 512:(tc + 1) * 512],
                                             buf[:, ht * P:(ht + 1) * P],
                                             xnT_b[:, tc * 8192 + ht * 512:
                                                   tc * 8192 + (ht + 1) * 512],
                                             start=(ht == 0), stop=(ht == HT - 1)))
                            g.inc("pe", 1, ("p5", "gu", blk, k))
                        for pr in range(4):
                            for it in range(ITI):
                                buf = dt_t[(it // 2) % 2] if it % 2 == 0 else \
                                    dt_t[2 + ((it - 1) // 2) % 2]
                                if it % 2 == 0:
                                    g.wait(f"dts{(it // 2) % 2}",
                                           ("p5", "dt", blk, pr, it))
                                else:
                                    g.wait(f"dta{((it - 1) // 2) % 2}",
                                           ("p5", "dta", blk, pr, it))
                                if it == 0:
                                    if (blk, pr) != (0, 0):
                                        pblk, ppr = (blk, pr - 1) if pr else (blk - 1, 3)
                                        g.wait("vec", ("p5", "pev", pblk, ppr))
                                    if pr == 0:
                                        g.wait("vec", ("p5", "hdn", blk, ITI - 1))
                                for e2 in range(2):
                                    tt8 = 2 * pr + e2
                                    dst = pd if e2 == 0 else pgu
                                    hsl = hdn[:, it * 1024 + tpos(tt8):
                                              it * 1024 + tpos(tt8) + P]
                                    for j in range(4):
                                        g.do(lambda e, dst=dst, hsl=hsl, j=j,
                                             buf=buf: e.matmul(
                                                 dst[:, j * 512:(j + 1) * 512],
                                                 hsl, buf[:, j * 512:(j + 1) * 512],
                                                 start=(it == 0),
                                                 stop=(it == ITI - 1)))
                                g.inc("pe", 1, ("p5", "dn", blk, pr, it))

                def p5_vec(g):
                    for blk in range(NBLK):
                        for it in range(ITI):
                            g.wait("act", ("p5", "silu", blk, it))
                            g.wait("pe", ("p5", "gu", blk, 2 * it + 1))
                            if it == 0 and blk >= 1:
                                g.wait("pe", ("p5", "dn", blk - 1, 3, ITI - 1))
                            g.do(lambda e, it=it: e.tensor_mul(
                                hdn[:, it * 1024:(it + 1) * 1024],
                                silu_g[it % 2][:], pgu[:, 1024:2048]))
                            g.mark("vec", ("p5", "hdn", blk, it))
                        for pr in range(4):
                            g.wait("pe", ("p5", "dn", blk, pr, ITI - 1))
                            g.wait("xo", ("p5", "xo", blk, pr))
                            if (blk, pr) != (0, 0):
                                pblk, ppr = (blk, pr - 1) if pr else (blk - 1, 3)
                                g.wait("pw", ("p5", "pw", pblk, ppr))
                            g.do(lambda e: e.tensor_add(
                                part_sb[0][:], pd[:], xo[0][:]))
                            g.do(lambda e: e.tensor_add(
                                part_sb[1][:], pgu[:], xo[1][:]))
                            g.mark("vec", ("p5", "pev", blk, pr))

                run_phase(block, ph, {"gpsimd": p5_gp, "sync": p5_sync,
                                      "scalar": p5_act, "tensor": p5_pe,
                                      "vector": p5_vec})

        with ExitStack() as C4:
            E4 = C4.enter_context
            QTb = E4(nc.sbuf_tensor("QTb", [P, HT * 512], BF16))
            Kc = [E4(nc.sbuf_tensor(f"Kc{i}", [P, HT * KCH], BF16))
                  for i in range(2)]
            Vc = [E4(nc.sbuf_tensor(f"Vc{i}", [P, 8 * D], BF16))
                  for i in range(2)]
            acc = E4(nc.sbuf_tensor("acc", [P, 4 * D], F32))
            P_sb = [E4(nc.sbuf_tensor(f"P_sb{i}", [P, CH], BF16)) for i in range(2)]
            PT_sb = [E4(nc.sbuf_tensor(f"PT_sb{i}", [P, CH], BF16)) for i in range(2)]
            part6 = [E4(nc.sbuf_tensor(f"part6{i}", [P, D], F32))
                     for i in range(2)]
            cnm = [E4(nc.sbuf_tensor(f"cnm{i}", [P, 1], F32)) for i in range(2)]
            nm = [[E4(nc.sbuf_tensor(f"nm{q}_{i}", [P, 1], F32)) for i in range(2)]
                  for q in range(4)]
            delta = [E4(nc.sbuf_tensor(f"delta{i}", [P, 1], F32)) for i in range(2)]
            alpha = [E4(nc.sbuf_tensor(f"alpha{i}", [P, 1], F32)) for i in range(2)]
            csum_a = [E4(nc.sbuf_tensor(f"csum_a{i}", [P, 1], F32)) for i in range(2)]
            csum_b = [E4(nc.sbuf_tensor(f"csum_b{i}", [P, 1], F32)) for i in range(2)]
            dsum = [E4(nc.sbuf_tensor(f"dsum{q}", [P, 1], F32)) for q in range(4)]
            rcp = E4(nc.sbuf_tensor("rcp", [P, 1], F32))

            NPAIR = NQB * NKC * 4

            def pcoord(j):
                bcp, qt = j // 4, j % 4
                return bcp // NKC, bcp % NKC, qt

            with ExitStack() as CP:
                S_ps = [CP.enter_context(nc.psum_tensor(f"S_ps{i}", [P, CH], F32))
                        for i in range(2)]
                PT_ps = [CP.enter_context(nc.psum_tensor(f"PT_ps{i}", [P, CH], BF16))
                         for i in range(2)]
                AV_ps = CP.enter_context(nc.psum_tensor("AV_ps", [P, D], F32))

                def p6_sync(g):
                    for gg in range(NQB * NKC):
                        cp = gg % NKC
                        if gg >= 2:
                            g.wait("pe", ("p6", "av", gg * 4 - 5))
                        g.dma(lambda e, gg=gg, cp=cp: e.dma_start(
                            out=Kc[gg % 2][:], in_=k_d[cp]))
                        g.inc(f"k6{gg % 2}", 16, ("p6", "k", gg))
                        g.dma(lambda e, gg=gg, cp=cp: e.dma_start(
                            out=Vc[gg % 2][:], in_=v_d[cp]))
                        g.inc(f"v6{gg % 2}", 16, ("p6", "v", gg))

                def p6_gp(g):
                    def part_load(b, qt):
                        if qt >= 2:
                            g.wait(f"ot6{qt % 2}", ("p6", "out", b, qt - 2))
                        elif b >= 1:
                            g.wait(f"ot6{qt % 2}", ("p6", "out", b - 1, qt + 2))
                        g.dma(lambda e, b=b, qt=qt: e.dma_start(
                            out=part6[qt % 2][:], in_=part_d[b * 4 + qt]))
                        g.inc(f"pt6{qt % 2}", 16, ("p6", "part", b, qt))

                    def out_store(b, qt):
                        g.wait("vec", ("p6", "fin", b, qt))
                        g.dma(lambda e, b=b, qt=qt: e.dma_start(
                            out=out_e[(b * 4 + qt) * P:(b * 4 + qt + 1) * P, :],
                            in_=part6[qt % 2][:]))
                        g.inc(f"ot6{qt % 2}", 16, ("p6", "out", b, qt))

                    def fin_block(b):
                        out_store(b, 0)
                        part_load(b, 2)
                        out_store(b, 1)
                        part_load(b, 3)
                        out_store(b, 2)
                        out_store(b, 3)

                    for b in range(NQB):
                        if b >= 1:
                            g.wait("pe", ("p6", "sc", b * NKC * 4 - 1))
                        g.dma(lambda e, b=b: e.dma_start(
                            out=QTb[:], in_=qt_d[b]))
                        g.inc(f"qt6{b % 2}", 16, ("p6", "qt", b))
                        if b >= 1:
                            fin_block(b - 1)
                        part_load(b, 0)
                        part_load(b, 1)
                    fin_block(NQB - 1)
                    if debug:
                        for nm_, (dap, sap) in dbg.items():
                            g.dma(lambda e, dap=dap, sap=sap: e.dma_start(
                                out=dap[:], in_=sap[:]))
                            g.inc("dgp_misc", 16, ("p6", "dbg", nm_))

                def p6_pe(g):
                    g.wait("gp", ("p0", "const"))
                    g.wait("vec", ("p5", "pev", NBLK - 1, 3))

                    def sc2(j):
                        b, cp, qt = pcoord(j)
                        gg = b * NKC + cp
                        g.wait(f"k6{gg % 2}", ("p6", "k", gg))
                        if qt == 0 and cp == 0:
                            g.wait(f"qt6{b % 2}", ("p6", "qt", b))
                        if j >= 1:
                            g.wait("act", ("p6", "expb", j - 1))
                        for ot in range(HT):
                            for e2 in range(2):
                                g.do(lambda e, gg=gg, qt=qt, ot=ot, e2=e2:
                                     e.matmul(
                                         S_ps[e2][:],
                                         QTb[:, ot * 512 + qt * P:
                                             ot * 512 + (qt + 1) * P],
                                         Kc[gg % 2][:, ot * KCH + e2 * 512:
                                                    ot * KCH + (e2 + 1) * 512],
                                         start=(ot == 0), stop=(ot == HT - 1)))
                        g.inc("pe", 1, ("p6", "sc", j))

                    def trans(j, e2):
                        g.wait("act", ("p6", "expa" if e2 == 0 else "expb", j))
                        if j >= 1:
                            g.wait("act", ("p6", "ptva" if e2 == 0 else "ptvb",
                                           j - 1))
                        for j4 in range(4):
                            g.do(lambda e, e2=e2, j=j, j4=j4: e.transpose(
                                PT_ps[e2][:, j4 * P:(j4 + 1) * P],
                                P_sb[e2][:, j4 * P:(j4 + 1) * P],
                                ident[:]))
                        g.inc("pe", 1, ("p6", "tra" if e2 == 0 else "trb", j))

                    def av(j, e2):
                        b, cp, qt = pcoord(j)
                        gg = b * NKC + cp
                        g.wait("act", ("p6", "ptva" if e2 == 0 else "ptvb", j))
                        if e2 == 0:
                            g.wait(f"v6{gg % 2}", ("p6", "v", gg))
                            if j >= 1:
                                g.wait("vec", ("p6", "fl", j - 1))
                        for kk in range(4):
                            kv = e2 * 4 + kk
                            for j4 in range(4):
                                g.do(lambda e, e2=e2, gg=gg, kv=kv, kk=kk, j4=j4:
                                     e.matmul(
                                         AV_ps[:, j4 * 512:(j4 + 1) * 512],
                                         PT_sb[e2][:, kk * P:(kk + 1) * P],
                                         Vc[gg % 2][:, kv * D + j4 * 512:
                                                    kv * D + (j4 + 1) * 512],
                                         start=(e2 == 0 and kk == 0),
                                         stop=(e2 == 1 and kk == 3)))
                        if e2 == 1:
                            g.inc("pe", 1, ("p6", "av", j))
                        else:
                            g.inc("pe", 1, ("p6", "ava", j))

                    sc2(0)
                    for j in range(NPAIR):
                        if j + 1 < NPAIR:
                            sc2(j + 1)
                        trans(j, 0)
                        trans(j, 1)
                        av(j, 0)
                        av(j, 1)

                def p6_vec(g):
                    def nm_ops(j):
                        b, cp, qt = pcoord(j)
                        g.wait("pe", ("p6", "sc", j))
                        g.do(lambda e, j=j: e.reduce_max(
                            cnm[0][:], S_ps[0][:], axis=AX.X, negate=True))
                        g.do(lambda e, j=j: e.reduce_max(
                            cnm[1][:], S_ps[1][:], axis=AX.X, negate=True))
                        g.do(lambda e: e.tensor_tensor(
                            cnm[0][:], cnm[0][:], cnm[1][:], ALU.min))
                        if cp == 0:
                            g.do(lambda e, qt=qt: e.tensor_copy(
                                nm[qt][0][:], cnm[0][:]))
                        else:
                            g.do(lambda e, qt=qt, cp=cp: e.tensor_tensor(
                                nm[qt][cp % 2][:], nm[qt][(cp + 1) % 2][:],
                                cnm[0][:], ALU.min))
                            g.do(lambda e, qt=qt, cp=cp, j=j: e.tensor_sub(
                                delta[j % 2][:], nm[qt][cp % 2][:],
                                nm[qt][(cp + 1) % 2][:]))
                        g.mark("vec", ("p6", "nm", j))

                    def fl_ops(j):
                        b, cp, qt = pcoord(j)
                        aq = acc[:, qt * D:(qt + 1) * D]
                        g.wait("pe", ("p6", "av", j))
                        g.wait("act", ("p6", "expb", j))
                        if cp == 0:
                            g.do(lambda e, aq=aq: e.tensor_copy(aq, AV_ps[:]))
                            g.do(lambda e, qt=qt, j=j: e.tensor_add(
                                dsum[qt][:], csum_a[j % 2][:], csum_b[j % 2][:]))
                        else:
                            g.do(lambda e, aq=aq, j=j: e.scalar_tensor_tensor(
                                aq, aq, alpha[j % 2][:, 0:1], AV_ps[:],
                                ALU.mult, ALU.add))
                            g.do(lambda e, qt=qt, j=j: e.scalar_tensor_tensor(
                                dsum[qt][:], dsum[qt][:], alpha[j % 2][:, 0:1],
                                csum_a[j % 2][:], ALU.mult, ALU.add))
                            g.do(lambda e, qt=qt, j=j: e.tensor_add(
                                dsum[qt][:], dsum[qt][:], csum_b[j % 2][:]))
                        g.mark("vec", ("p6", "fl", j))
                        if cp == NKC - 1 and qt == 3:
                            for q2 in range(4):
                                g.do(lambda e, q2=q2: e.reciprocal(
                                    rcp[:], dsum[q2][:]))
                                g.wait(f"pt6{q2 % 2}", ("p6", "part", b, q2))
                                g.do(lambda e, q2=q2: e.scalar_tensor_tensor(
                                    part6[q2 % 2][:],
                                    acc[:, q2 * D:(q2 + 1) * D],
                                    rcp[:, 0:1], part6[q2 % 2][:],
                                    ALU.mult, ALU.add))
                                g.mark("vec", ("p6", "fin", b, q2))

                    nm_ops(0)
                    for j in range(NPAIR):
                        if j + 1 < NPAIR:
                            nm_ops(j + 1)
                        fl_ops(j)

                def p6_act(g):
                    for j in range(NPAIR):
                        b, cp, qt = pcoord(j)
                        g.wait("vec", ("p6", "nm", j))
                        if cp > 0:
                            g.do(lambda e, j=j: e.activation(
                                alpha[j % 2][:], delta[j % 2][:], AF.Exp))
                        g.do(lambda e, j=j, qt=qt, cp=cp: e.activation(
                            P_sb[0][:], S_ps[0][:], AF.Exp,
                            bias=nm[qt][cp % 2][:, 0:1],
                            accum_out=csum_a[j % 2][:, 0:1]))
                        g.mark("act", ("p6", "expa", j))
                        g.do(lambda e, j=j, qt=qt, cp=cp: e.activation(
                            P_sb[1][:], S_ps[1][:], AF.Exp,
                            bias=nm[qt][cp % 2][:, 0:1],
                            accum_out=csum_b[j % 2][:, 0:1]))
                        g.mark("act", ("p6", "expb", j))
                        g.wait("pe", ("p6", "tra", j))
                        g.do(lambda e, j=j: e.copy(PT_sb[0][:], PT_ps[0][:]))
                        g.mark("act", ("p6", "ptva", j))
                        g.wait("pe", ("p6", "trb", j))
                        g.do(lambda e, j=j: e.copy(PT_sb[1][:], PT_ps[1][:]))
                        g.mark("act", ("p6", "ptvb", j))

                run_phase(block, ph, {"sync": p6_sync, "gpsimd": p6_gp,
                                      "tensor": p6_pe, "vector": p6_vec,
                                      "scalar": p6_act})

    return nc


_N_CORES = 8


def kernel(L0_attn_input, L0_attn_Wq, L0_attn_bq, L0_attn_Wk, L0_attn_bk,
           L0_attn_Wv, L0_attn_bv, L0_ffn_gate_w, L0_ffn_up_w, L0_ffn_down_w,
           L0_ffn_gate_s, L0_ffn_up_s, L0_ffn_down_s):
    x = np.ascontiguousarray(np.asarray(L0_attn_input, dtype=np.float32))
    weights = {
        "wq": np.ascontiguousarray(np.asarray(L0_attn_Wq, np.float32)),
        "wk": np.ascontiguousarray(np.asarray(L0_attn_Wk, np.float32)),
        "wv": np.ascontiguousarray(np.asarray(L0_attn_Wv, np.float32)),
        "bq": np.ascontiguousarray(np.asarray(L0_attn_bq, np.float32)),
        "bk": np.ascontiguousarray(np.asarray(L0_attn_bk, np.float32)),
        "bv": np.ascontiguousarray(np.asarray(L0_attn_bv, np.float32)),
        "gw": np.ascontiguousarray(np.asarray(L0_ffn_gate_w, np.float32)),
        "uw": np.ascontiguousarray(np.asarray(L0_ffn_up_w, np.float32)),
        "dw": np.ascontiguousarray(np.asarray(L0_ffn_down_w, np.float32)),
        "gs": np.ascontiguousarray(np.asarray(L0_ffn_gate_s, np.float32)),
        "us": np.ascontiguousarray(np.asarray(L0_ffn_up_s, np.float32)),
        "ds": np.ascontiguousarray(np.asarray(L0_ffn_down_s, np.float32)),
    }
    in_maps = []
    for c in range(_N_CORES):
        b, s = c // 4, c % 4
        xb = x[b]
        xr = np.ascontiguousarray(
            np.concatenate([xb[s * TC:], xb[:s * TC]], axis=0))
        m = {"x_full": xr}
        m.update(weights)
        in_maps.append(m)

    nc = build_program()
    res = run_bass_kernel_spmd(nc, in_maps, core_ids=list(range(_N_CORES)))

    out = np.empty((2, SEQ, D), dtype=np.float32)
    for c in range(_N_CORES):
        b, s = c // 4, c % 4
        out[b, s * TC:(s + 1) * TC, :] = res.results[c]["out"]
    return out


# version 09
# speedup vs baseline: 1.1995x; 1.1181x over previous
import numpy as np
from contextlib import ExitStack

import concourse.bass as bass
import concourse.mybir as mybir
from concourse.bass_utils import run_bass_kernel_spmd

F32 = mybir.dt.float32
BF16 = mybir.dt.bfloat16
AF = mybir.ActivationFunctionType
ALU = mybir.AluOpType
AX = mybir.AxisListType

P = 128
D = 2048
HT = D // P
SEQ = 16384
TC = 4096
NT = SEQ // P
NB = SEQ // 512
I5 = 5632
ITI = I5 // P
CH = 512
NCH = SEQ // CH
KCH = 1024
NKC = SEQ // KCH
QB = 512
NQB = TC // QB
NU = NQB * NCH * 4

ENGINE_DMA_SEMS = {
    "gpsimd": ["dgp_misc", "x0", "x1", "x2", "x3", "xw0", "xw1", "xr0", "xr1", "xb",
               "xo", "pw", "qt60", "qt61", "pt60", "pt61", "ot60", "ot61"],
    "sync": ["wl0", "wl1", "dw0", "dw1", "dw2", "dw3", "sg0", "sg1",
             "su0", "su1", "dts0", "dts1", "k60", "k61", "v60", "v61"],
    "scalar": ["ks0", "ks1", "vs0", "vs1", "qs0", "qs1", "st0", "st1",
               "dta0", "dta1"],
}


class Phase:
    def __init__(self, sems):
        self.sems = sems
        self.marks = {}
        self.tot = {k: 0 for k in sems}
        self.bar_prev = 0


CHAIN_SEM = {"vector": "vec", "scalar": "act", "gpsimd": "gp"}


class Gen:
    __slots__ = ("ph", "eng", "cnt", "chain", "last", "last_waited")

    def __init__(self, ph, eng, cnt, engine_name):
        self.ph = ph
        self.eng = eng
        self.cnt = cnt
        self.chain = CHAIN_SEM.get(engine_name)
        self.last = None
        self.last_waited = cnt.get(self.chain, 0) if self.chain else 0

    def do(self, f, dma=False):
        c = self.chain
        if c is not None and self.cnt[c] > self.last_waited:
            if self.eng is not None:
                self.eng.wait_ge(self.ph.sems[c], self.cnt[c])
            self.last_waited = self.cnt[c]
        if self.eng is not None:
            self.last = f(self.eng)
        if c is not None and not dma:
            self.cnt[c] += 1
            if self.eng is not None:
                self.last.then_inc(self.ph.sems[c], 1)
        return self.last

    def dma(self, f):
        return self.do(f, dma=True)

    def inc(self, sem, n=1, tag=None):
        self.cnt[sem] += n
        if self.eng is None:
            if tag is not None:
                key = (sem,) + tag
                assert key not in self.ph.marks, key
                self.ph.marks[key] = self.cnt[sem]
        else:
            self.last.then_inc(self.ph.sems[sem], n)

    def mark(self, sem, tag):
        if self.eng is None:
            key = (sem,) + tag
            assert key not in self.ph.marks, key
            self.ph.marks[key] = self.cnt[sem]

    def wait(self, sem, tag):
        if self.eng is not None:
            self.eng.wait_ge(self.ph.sems[sem], self.ph.marks[(sem,) + tag])


def run_phase(block, ph, emitters):
    base = dict(ph.tot)
    for name in ("tensor", "vector", "scalar", "sync", "gpsimd"):
        fn = emitters.get(name)
        if fn:
            fn(Gen(ph, None, ph.tot, name))
    ph.tot["bar"] += 5
    bar_target = ph.tot["bar"]
    prev = ph.bar_prev

    def mk(name, fn):
        def body(eng):
            if prev > 0:
                eng.wait_ge(ph.sems["bar"], prev)
            if fn:
                g2 = Gen(ph, eng, dict(base), name)
                fn(g2)
            for s in ENGINE_DMA_SEMS.get(name, []):
                if ph.tot[s] > 0:
                    eng.wait_ge(ph.sems[s], ph.tot[s])
            cs = CHAIN_SEM.get(name)
            if cs is not None and ph.tot[cs] > 0:
                eng.wait_ge(ph.sems[cs], ph.tot[cs])
            eng.nop().then_inc(ph.sems["bar"], 1)
        return body

    block.tensor(mk("tensor", emitters.get("tensor")))
    block.vector(mk("vector", emitters.get("vector")))
    block.scalar(mk("scalar", emitters.get("scalar")))
    block.sync(mk("sync", emitters.get("sync")))
    block.gpsimd(mk("gpsimd", emitters.get("gpsimd")))
    ph.bar_prev = bar_target


def tpos(tt8):
    return (tt8 // 4) * 512 + (tt8 % 4) * 128


def build_program(debug=False):
    nc = bass.Bass("TRN2", target_bir_lowering=False)

    x_full = nc.dram_tensor("x_full", [SEQ, D], F32, kind="ExternalInput").ap()
    wq = nc.dram_tensor("wq", [D, D], F32, kind="ExternalInput").ap()
    wk = nc.dram_tensor("wk", [D, D], F32, kind="ExternalInput").ap()
    wv = nc.dram_tensor("wv", [D, D], F32, kind="ExternalInput").ap()
    bq = nc.dram_tensor("bq", [D], F32, kind="ExternalInput").ap()
    bk = nc.dram_tensor("bk", [D], F32, kind="ExternalInput").ap()
    bv = nc.dram_tensor("bv", [D], F32, kind="ExternalInput").ap()
    gw = nc.dram_tensor("gw", [I5, D], F32, kind="ExternalInput").ap()
    uw = nc.dram_tensor("uw", [I5, D], F32, kind="ExternalInput").ap()
    dw = nc.dram_tensor("dw", [D, I5], F32, kind="ExternalInput").ap()
    gs = nc.dram_tensor("gs", [I5], F32, kind="ExternalInput").ap()
    us = nc.dram_tensor("us", [I5], F32, kind="ExternalInput").ap()
    ds = nc.dram_tensor("ds", [D], F32, kind="ExternalInput").ap()
    out_e = nc.dram_tensor("out", [TC, D], F32, kind="ExternalOutput").ap()

    xnt_d = nc.dram_tensor("xnt_d", [NB, P, 16 * 512], BF16).ap()
    k_d = nc.dram_tensor("k_d", [NKC, P, HT * KCH], BF16).ap()
    v_d = nc.dram_tensor("v_d", [NKC, P, 8 * D], BF16).ap()
    qt_d = nc.dram_tensor("qt_d", [NQB, P, HT * 512], BF16).ap()
    gt_d = nc.dram_tensor("gt_d", [ITI, P, D], BF16).ap()
    ut_d = nc.dram_tensor("ut_d", [ITI, P, D], BF16).ap()
    dt_d = nc.dram_tensor("dt_d", [ITI, P, D], BF16).ap()
    part_d = nc.dram_tensor("part_d", [TC // P, P, D], F32).ap()

    dbg = {}
    if debug:
        for nm, src in (("dbg_xnt", xnt_d), ("dbg_k", k_d), ("dbg_v", v_d),
                        ("dbg_qt", qt_d), ("dbg_gt", gt_d), ("dbg_dt", dt_d),
                        ("dbg_part", part_d)):
            dbg[nm] = (nc.dram_tensor(nm, list(src.shape), src.dtype,
                                      kind="ExternalOutput").ap(), src)

    with ExitStack() as G:
        E = G.enter_context
        sems = {}
        sem_names = (["bar", "gp", "pe", "vec", "act", "fin", "dgp_misc",
                      "x0", "x1", "x2", "x3", "xw0", "xw1", "xr0", "xr1", "xb", "xo",
                      "pw", "qt60", "qt61", "pt60", "pt61", "ot60", "ot61",
                      "wl0", "wl1", "dw0", "dw1", "dw2", "dw3",
                      "sg0", "sg1", "su0", "su1", "dts0", "dts1",
                      "k60", "k61", "v60", "v61",
                      "ks0", "ks1", "vs0", "vs1", "qs0", "qs1",
                      "st0", "st1", "dta0", "dta1"])
        for s in sem_names:
            sems[s] = E(nc.semaphore(s))
        ph = Phase(sems)

        ident = E(nc.sbuf_tensor("ident", [P, P], BF16))
        identf = E(nc.sbuf_tensor("identf", [P, P], F32))
        ones = E(nc.sbuf_tensor("ones", [1, P], F32))
        bqc = E(nc.sbuf_tensor("bqc", [P, HT], F32))
        bkc = E(nc.sbuf_tensor("bkc", [P, HT], F32))
        gsc = E(nc.sbuf_tensor("gsc", [P, ITI], F32))
        usc = E(nc.sbuf_tensor("usc", [P, ITI], F32))
        dsc_sb = E(nc.sbuf_tensor("dsc_sb", [P, HT], F32))

        block = E(nc.Block())

        with ExitStack() as C:
            st44 = C.enter_context(nc.sbuf_tensor("st44", [44, P], F32))
            ps0 = C.enter_context(nc.psum_tensor("ps0", [P, 256], F32))

            loads = [
                (bq.rearrange("(a f) -> a f", a=16), 16, bqc),
                (bk.rearrange("(a f) -> a f", a=16), 16, bkc),
                (ds.rearrange("(a f) -> a f", a=16), 16, dsc_sb),
                (gs.rearrange("(a f) -> a f", a=44), 44, gsc),
                (us.rearrange("(a f) -> a f", a=44), 44, usc),
            ]

            def p0_gp(g):
                g.do(lambda e: e.memset(ident[:], 0.0))
                g.do(lambda e: e.memset(identf[:], 0.0))
                g.do(lambda e: e.memset(ones[:], 1.0))
                g.do(lambda e: e.affine_select(
                    out=ident[:], in_=ident[:], compare_op=ALU.not_equal,
                    fill=1.0, base=0, pattern=[[-1, P]], channel_multiplier=1))
                g.do(lambda e: e.affine_select(
                    out=identf[:], in_=identf[:], compare_op=ALU.not_equal,
                    fill=1.0, base=0, pattern=[[-1, P]], channel_multiplier=1))
                g.mark("gp", ("p0", "const"))
                for j, (src, rows, _) in enumerate(loads):
                    if j > 0:
                        g.wait("pe", ("p0", "tr", j - 1))
                    g.dma(lambda e, src=src, rows=rows:
                         e.dma_start(out=st44[0:rows, :], in_=src))
                    g.inc("dgp_misc", 16, ("p0", "ld", j))

            def p0_pe(g):
                g.wait("gp", ("p0", "const"))
                off = 0
                for j, (_, rows, _) in enumerate(loads):
                    g.wait("dgp_misc", ("p0", "ld", j))
                    if j > 0:
                        g.wait("vec", ("p0", "ev", j - 1))
                    g.do(lambda e, off=off, rows=rows: e.transpose(
                        ps0[:, off:off + rows], st44[0:rows, :],
                        identf[0:rows, 0:rows]))
                    g.inc("pe", 1, ("p0", "tr", j))
                    off += rows

            def p0_vec(g):
                off = 0
                for j, (_, rows, dest) in enumerate(loads):
                    g.wait("pe", ("p0", "tr", j))
                    g.do(lambda e, off=off, rows=rows, dest=dest:
                         e.tensor_copy(dest[:, 0:rows], ps0[:, off:off + rows]))
                    g.mark("vec", ("p0", "ev", j))
                    off += rows

            run_phase(block, ph, {"gpsimd": p0_gp, "tensor": p0_pe,
                                  "vector": p0_vec})

        with ExitStack() as C1:
            E1 = C1.enter_context
            WT = E1(nc.sbuf_tensor("WT", [P, HT * D], BF16))
            wrow = [E1(nc.sbuf_tensor(f"wrow{i}", [P, D], F32)) for i in range(2)]
            wbf = [E1(nc.sbuf_tensor(f"wbf{i}", [P, D], BF16)) for i in range(2)]
            xs = [E1(nc.sbuf_tensor(f"xs{i}", [P, D], F32)) for i in range(4)]
            sqs = E1(nc.sbuf_tensor("sqs", [P, D], F32))
            xnbf = [E1(nc.sbuf_tensor(f"xnbf{i}", [P, D], BF16)) for i in range(4)]
            stage = [E1(nc.sbuf_tensor(f"stage{i}", [P, 16 * 512], BF16)) for i in range(2)]
            ktbf = [E1(nc.sbuf_tensor(f"ktbf{i}", [P, 512], BF16)) for i in range(2)]
            vbf = [E1(nc.sbuf_tensor(f"vbf{i}", [P, D], BF16)) for i in range(2)]
            qtbf = [E1(nc.sbuf_tensor(f"qtbf{i}", [P, 512], BF16)) for i in range(2)]
            bvb = E1(nc.sbuf_tensor("bvb", [P, D], F32))
            bvrow = E1(nc.sbuf_tensor("bvrow", [1, D], F32))
            stats = {nm: [E1(nc.sbuf_tensor(f"{nm}{i}", [P, 1], F32))
                          for i in range(4)]
                     for nm in ("sumv", "ssq", "mb", "t2", "t3", "stdv", "rstd")}

            def wt_build(pn, w_ap, extra_pe=None, extra_vec=None, extra_gp=None):
                def f_sync(g):
                    for ot in range(HT):
                        if ot >= 2:
                            g.wait("vec", (pn, "cast", ot - 2))
                        g.dma(lambda e, ot=ot: e.dma_start(
                            out=wrow[ot % 2][:], in_=w_ap[ot * P:(ot + 1) * P, :]))
                        g.inc(f"wl{ot % 2}", 16, (pn, "wl", ot))

                def f_vec(g):
                    for ot in range(HT):
                        g.wait(f"wl{ot % 2}", (pn, "wl", ot))
                        if ot >= 2:
                            g.wait("pe", (pn, "tr", ot - 2))
                        g.do(lambda e, ot=ot: e.tensor_copy(
                            wbf[ot % 2][:], wrow[ot % 2][:]))
                        g.mark("vec", (pn, "cast", ot))
                        g.wait("pe", (pn, "tr", ot))
                        dst = WT.ap().rearrange("p (ht o) -> p ht o", ht=HT)
                        g.do(lambda e, ot=ot, dst=dst: e.tensor_copy(
                            dst[:, :, ot * P:(ot + 1) * P],
                            wt_ps.ap().rearrange("p (ht o) -> p ht o", ht=HT)))
                        g.mark("vec", (pn, "wtev", ot))
                    if extra_vec:
                        extra_vec(g)

                def f_pe(g):
                    g.wait("gp", ("p0", "const"))
                    g.wait("vec", ("p0", "ev", 4))
                    for ot in range(HT):
                        g.wait("vec", (pn, "cast", ot))
                        if ot >= 1:
                            g.wait("vec", (pn, "wtev", ot - 1))
                        for ht in range(HT):
                            g.do(lambda e, ot=ot, ht=ht: e.transpose(
                                wt_ps[:, ht * P:(ht + 1) * P],
                                wbf[ot % 2][:, ht * P:(ht + 1) * P], ident[:]))
                        g.inc("pe", 1, (pn, "tr", ot))
                    if extra_pe:
                        extra_pe(g)

                return {"sync": f_sync, "vector": f_vec, "tensor": f_pe,
                        "gpsimd": extra_gp}

            with ExitStack() as CP:
                wt_ps = CP.enter_context(nc.psum_tensor("wt_ps1", [P, D], BF16))
                run_phase(block, ph, wt_build("p1a", wk))

            with ExitStack() as CP:
                xnT_ps = [CP.enter_context(nc.psum_tensor(f"xnT_ps{i}", [P, D], BF16))
                          for i in range(2)]
                kt_ps = [CP.enter_context(nc.psum_tensor(f"kt_ps{i}", [P, 512], F32))
                         for i in range(2)]

                def p1b_gp(g):
                    for tb in range(NB):
                        for tt in range(4):
                            i = 4 * tb + tt
                            if i >= 4:
                                g.wait("vec", ("p1b", "xn", i - 4))
                            g.dma(lambda e, i=i: e.dma_start(
                                out=xs[i % 4][:], in_=x_full[i * P:(i + 1) * P, :]))
                            g.inc(f"x{i % 4}", 16, ("p1b", "x", i))
                        g.wait("act", ("p1b", "sev", 4 * tb + 3))
                        g.dma(lambda e, tb=tb: e.dma_start(
                            out=xnt_d[tb], in_=stage[tb % 2][:]))
                        g.inc(f"xw{tb % 2}", 16, ("p1b", "xw", tb))

                def p1b_vec(g):
                    for tb in range(NB):
                        for tt in range(4):
                            i = 4 * tb + tt
                            p = i % 4
                            g.wait(f"x{i % 4}", ("p1b", "x", i))
                            if i >= 4:
                                g.wait("act", ("p1b", "sqrt", i - 4))
                            g.do(lambda e, p=p: e.reduce_sum(
                                stats["sumv"][p][:], xs[p][:], axis=AX.X))
                            g.do(lambda e, p=p: e.scalar_tensor_tensor(
                                sqs[:], xs[p][:], 1.0, xs[p][:],
                                ALU.mult, ALU.mult,
                                accum_out=stats["ssq"][p][:]))
                            g.do(lambda e, p=p: e.tensor_scalar_mul(
                                stats["mb"][p][:], stats["sumv"][p][:], 1.0 / D))
                            g.do(lambda e, p=p: e.tensor_mul(
                                stats["t2"][p][:], stats["mb"][p][:],
                                stats["sumv"][p][:]))
                            g.do(lambda e, p=p: e.tensor_sub(
                                stats["t3"][p][:], stats["ssq"][p][:],
                                stats["t2"][p][:]))
                            g.do(lambda e, p=p: e.tensor_scalar(
                                stats["t3"][p][:], stats["t3"][p][:],
                                1.0 / D, 1e-5, ALU.mult, ALU.add))
                            g.mark("vec", ("p1b", "t3", i))
                        for tt in range(4):
                            i = 4 * tb + tt
                            p = i % 4
                            g.wait("act", ("p1b", "sqrt", i))
                            g.do(lambda e, p=p: e.reciprocal(
                                stats["rstd"][p][:], stats["stdv"][p][:]))
                            if i >= 4:
                                g.wait("pe", ("p1b", "tr", i - 4))
                            g.do(lambda e, p=p, i=i: e.tensor_scalar(
                                xnbf[i % 4][:], xs[p][:], stats["mb"][p][:, 0:1],
                                stats["rstd"][p][:, 0:1],
                                ALU.subtract, ALU.mult))
                            g.mark("vec", ("p1b", "xn", i))

                def p1b_act(g):
                    g.wait("vec", ("p0", "ev", 1))
                    for tb in range(NB):
                        for tt in range(4):
                            i = 4 * tb + tt
                            p = i % 4
                            g.wait("vec", ("p1b", "t3", i))
                            if i >= 4:
                                g.wait("vec", ("p1b", "xn", i - 4))
                            g.do(lambda e, p=p: e.activation(
                                stats["stdv"][p][:], stats["t3"][p][:], AF.Sqrt))
                            g.mark("act", ("p1b", "sqrt", i))
                            if tb >= 1:
                                for e4 in range(4):
                                    ot = 4 * tt + e4
                                    kg = (tb - 1) * HT + ot
                                    g.wait("pe", ("p1b", "k", kg))
                                    if kg >= 2:
                                        g.wait(f"ks{kg % 2}", ("p1b", "ks", kg - 2))
                                    g.do(lambda e, kg=kg, ot=ot: e.activation(
                                        ktbf[kg % 2][:], kt_ps[ot % 2][:],
                                        AF.Identity, bias=bkc[:, ot:ot + 1]))
                                    g.mark("act", ("p1b", "ktev", kg))
                                    tb1 = tb - 1
                                    g.dma(lambda e, kg=kg, tb1=tb1, ot=ot:
                                         e.dma_start(
                                             out=k_d[tb1 // 2, :,
                                                     ot * KCH + (tb1 % 2) * 512:
                                                     ot * KCH + (tb1 % 2) * 512 + 512],
                                             in_=ktbf[kg % 2][:]))
                                    g.inc(f"ks{kg % 2}", 16, ("p1b", "ks", kg))
                        for tt in range(4):
                            i = 4 * tb + tt
                            g.wait("pe", ("p1b", "tr", i))
                            if tt == 0 and tb >= 2:
                                g.wait("pe", ("p1b", "k", (tb - 2) * HT + HT - 1))
                                g.wait(f"xw{tb % 2}", ("p1b", "xw", tb - 2))
                            dst = stage[tb % 2].ap().rearrange(
                                "p (ht t) -> p ht t", ht=HT)
                            g.do(lambda e, tt=tt, dst=dst, i=i: e.copy(
                                dst[:, :, tt * P:(tt + 1) * P],
                                xnT_ps[i % 2].ap().rearrange(
                                    "p (ht t) -> p ht t", ht=HT)))
                            g.mark("act", ("p1b", "sev", i))
                    tb1 = NB - 1
                    for ot in range(HT):
                        kg = tb1 * HT + ot
                        g.wait("pe", ("p1b", "k", kg))
                        if kg >= 2:
                            g.wait(f"ks{kg % 2}", ("p1b", "ks", kg - 2))
                        g.do(lambda e, kg=kg, ot=ot: e.activation(
                            ktbf[kg % 2][:], kt_ps[ot % 2][:], AF.Identity,
                            bias=bkc[:, ot:ot + 1]))
                        g.mark("act", ("p1b", "ktev", kg))
                        g.dma(lambda e, kg=kg, tb1=tb1, ot=ot: e.dma_start(
                            out=k_d[tb1 // 2, :,
                                    ot * KCH + (tb1 % 2) * 512:
                                    ot * KCH + (tb1 % 2) * 512 + 512],
                            in_=ktbf[kg % 2][:]))
                        g.inc(f"ks{kg % 2}", 16, ("p1b", "ks", kg))

                def p1b_pe(g):
                    g.wait("vec", ("p1a", "wtev", HT - 1))
                    for tb in range(NB):
                        for tt in range(4):
                            i = 4 * tb + tt
                            g.wait("vec", ("p1b", "xn", i))
                            if i >= 2:
                                g.wait("act", ("p1b", "sev", i - 2))
                            for ht in range(HT):
                                g.do(lambda e, i=i, ht=ht: e.transpose(
                                    xnT_ps[i % 2][:, ht * P:(ht + 1) * P],
                                    xnbf[i % 4][:, ht * P:(ht + 1) * P],
                                    ident[:]))
                            g.inc("pe", 1, ("p1b", "tr", i))
                        g.wait("act", ("p1b", "sev", 4 * tb + 3))
                        for ot in range(HT):
                            kg = tb * HT + ot
                            if kg >= 2:
                                g.wait("act", ("p1b", "ktev", kg - 2))
                            for ht in range(HT):
                                g.do(lambda e, ot=ot, ht=ht, tb=tb: e.matmul(
                                    kt_ps[ot % 2][:],
                                    WT[:, ht * D + ot * P: ht * D + (ot + 1) * P],
                                    stage[tb % 2][:, ht * 512:(ht + 1) * 512],
                                    start=(ht == 0), stop=(ht == HT - 1)))
                            g.inc("pe", 1, ("p1b", "k", kg))

                run_phase(block, ph, {"gpsimd": p1b_gp, "vector": p1b_vec,
                                      "scalar": p1b_act, "tensor": p1b_pe})

            with ExitStack() as CP:
                wt_ps = CP.enter_context(nc.psum_tensor("wt_ps2", [P, D], BF16))
                pbv = CP.enter_context(nc.psum_tensor("pbv", [P, D], F32))

                def p2a_gp(g):
                    g.dma(lambda e: e.dma_start(
                        out=bvrow[:], in_=bv.rearrange("(a f) -> a f", a=1)))
                    g.inc("dgp_misc", 16, ("p2a", "bv"))

                def p2a_pe_extra(g):
                    g.wait("dgp_misc", ("p2a", "bv"))
                    for k in range(4):
                        g.do(lambda e, k=k: e.matmul(
                            pbv[:, k * 512:(k + 1) * 512], ones[0:1, :],
                            bvrow[0:1, k * 512:(k + 1) * 512],
                            start=True, stop=True))
                    g.inc("pe", 1, ("p2a", "bvmm"))

                def p2a_vec_extra(g):
                    g.wait("pe", ("p2a", "bvmm"))
                    g.do(lambda e: e.tensor_copy(bvb[:], pbv[:]))
                    g.mark("vec", ("p2a", "bvev"))

                run_phase(block, ph, wt_build("p2a", wv, extra_pe=p2a_pe_extra,
                                              extra_vec=p2a_vec_extra,
                                              extra_gp=p2a_gp))

            with ExitStack() as CP:
                v_ps = [CP.enter_context(nc.psum_tensor(f"v_ps{i}", [P, D], F32))
                        for i in range(2)]

                def p2b_gp(g):
                    for tb in range(NB):
                        if tb >= 2:
                            g.wait("pe", ("p2b", "v", 4 * (tb - 2) + 3))
                        g.dma(lambda e, tb=tb: e.dma_start(
                            out=stage[tb % 2][:], in_=xnt_d[tb]))
                        g.inc(f"xr{tb % 2}", 16, ("p2b", "xr", tb))

                def p2b_pe(g):
                    g.wait("vec", ("p2a", "wtev", HT - 1))
                    g.wait("vec", ("p2a", "bvev"))
                    for tb in range(NB):
                        g.wait(f"xr{tb % 2}", ("p2b", "xr", tb))
                        for tt in range(4):
                            vg = 4 * tb + tt
                            if vg >= 2:
                                g.wait("vec", ("p2b", "vev", vg - 2))
                            for ht in range(HT):
                                for oc in range(4):
                                    g.do(lambda e, vg=vg, tb=tb, tt=tt, ht=ht,
                                         oc=oc: e.matmul(
                                             v_ps[vg % 2][:, oc * 512:(oc + 1) * 512],
                                             stage[tb % 2][:, ht * 512 + tt * P:
                                                           ht * 512 + (tt + 1) * P],
                                             WT[:, ht * D + oc * 512:
                                                ht * D + (oc + 1) * 512],
                                             start=(ht == 0), stop=(ht == HT - 1)))
                            g.inc("pe", 1, ("p2b", "v", vg))

                def p2b_vec(g):
                    for vg in range(NT):
                        g.wait("pe", ("p2b", "v", vg))
                        if vg >= 2:
                            g.wait(f"vs{vg % 2}", ("p2b", "vs", vg - 2))
                        g.do(lambda e, vg=vg: e.tensor_add(
                            vbf[vg % 2][:], v_ps[vg % 2][:], bvb[:]))
                        g.mark("vec", ("p2b", "vev", vg))

                def p2b_act(g):
                    for vg in range(NT):
                        g.wait("vec", ("p2b", "vev", vg))
                        g.dma(lambda e, vg=vg: e.dma_start(
                            out=v_d[vg // 8, :, (vg % 8) * D:(vg % 8 + 1) * D],
                            in_=vbf[vg % 2][:]))
                        g.inc(f"vs{vg % 2}", 16, ("p2b", "vs", vg))

                run_phase(block, ph, {"gpsimd": p2b_gp, "tensor": p2b_pe,
                                      "vector": p2b_vec, "scalar": p2b_act})

            with ExitStack() as CP:
                wt_ps = CP.enter_context(nc.psum_tensor("wt_ps3", [P, D], BF16))
                run_phase(block, ph, wt_build("p3a", wq))

            with ExitStack() as CP:
                qt_ps = [CP.enter_context(nc.psum_tensor(f"qt_ps{i}", [P, 512], F32))
                         for i in range(2)]

                def p3b_gp(g):
                    for qc in range(NQB):
                        if qc >= 2:
                            g.wait("pe", ("p3b", "q", (qc - 2) * HT + HT - 1))
                        g.dma(lambda e, qc=qc: e.dma_start(
                            out=stage[qc % 2][:], in_=xnt_d[qc]))
                        g.inc(f"xr{qc % 2}", 16, ("p3b", "xr", qc))

                def p3b_pe(g):
                    g.wait("vec", ("p3a", "wtev", HT - 1))
                    for qc in range(NQB):
                        g.wait(f"xr{qc % 2}", ("p3b", "xr", qc))
                        for ot in range(HT):
                            k = qc * HT + ot
                            if k >= 2:
                                g.wait("act", ("p3b", "qev", k - 2))
                            for ht in range(HT):
                                g.do(lambda e, k=k, qc=qc, ot=ot, ht=ht:
                                     e.matmul(
                                         qt_ps[k % 2][:],
                                         WT[:, ht * D + ot * P:
                                            ht * D + (ot + 1) * P],
                                         stage[qc % 2][:, ht * 512:(ht + 1) * 512],
                                         start=(ht == 0), stop=(ht == HT - 1)))
                            g.inc("pe", 1, ("p3b", "q", k))

                def p3b_act(g):
                    g.wait("vec", ("p0", "ev", 0))
                    for qc in range(NQB):
                        for ot in range(HT):
                            k = qc * HT + ot
                            g.wait("pe", ("p3b", "q", k))
                            if k >= 2:
                                g.wait(f"qs{k % 2}", ("p3b", "qs", k - 2))
                            g.do(lambda e, k=k, ot=ot: e.activation(
                                qtbf[k % 2][:], qt_ps[k % 2][:], AF.Identity,
                                bias=bqc[:, ot:ot + 1]))
                            g.mark("act", ("p3b", "qev", k))
                            g.dma(lambda e, k=k, qc=qc, ot=ot: e.dma_start(
                                out=qt_d[qc, :, ot * 512:(ot + 1) * 512],
                                in_=qtbf[k % 2][:]))
                            g.inc(f"qs{k % 2}", 16, ("p3b", "qs", k))

                run_phase(block, ph, {"gpsimd": p3b_gp, "tensor": p3b_pe,
                                      "scalar": p3b_act})

        with ExitStack() as C2:
            E2 = C2.enter_context
            gw_st = [E2(nc.sbuf_tensor(f"gw_st{i}", [P, D], F32)) for i in range(2)]
            gubf = [E2(nc.sbuf_tensor(f"gubf{i}", [P, D], BF16)) for i in range(2)]
            gu_stage = [E2(nc.sbuf_tensor(f"gu_stage{i}", [P, D], BF16))
                        for i in range(2)]
            with ExitStack() as CP:
                gt_ps = [CP.enter_context(nc.psum_tensor(f"gt_ps{i}", [P, D], BF16))
                         for i in range(2)]
                NK = 2 * ITI

                def p4a_sync(g):
                    for k in range(NK):
                        it, isup = k // 2, k % 2
                        if k >= 2:
                            g.wait("vec", ("p4a", "cast", k - 2))
                        src = uw if isup else gw
                        g.dma(lambda e, k=k, it=it, src=src: e.dma_start(
                            out=gw_st[k % 2][:], in_=src[it * P:(it + 1) * P, :]))
                        g.inc(f"wl{k % 2}", 16, ("p4a", "wl", k))

                def p4a_vec(g):
                    for k in range(NK):
                        it, isup = k // 2, k % 2
                        g.wait(f"wl{k % 2}", ("p4a", "wl", k))
                        if k >= 2:
                            g.wait("pe", ("p4a", "tr", k - 2))
                        sc = usc if isup else gsc
                        g.do(lambda e, k=k, it=it, sc=sc: e.tensor_scalar_mul(
                            gubf[k % 2][:], gw_st[k % 2][:], sc[:, it:it + 1]))
                        g.mark("vec", ("p4a", "cast", k))
                        g.wait("pe", ("p4a", "tr", k))
                        if k >= 2:
                            g.wait(f"st{k % 2}", ("p4a", "st", k - 2))
                        g.do(lambda e, k=k: e.tensor_copy(
                            gu_stage[k % 2][:], gt_ps[k % 2][:]))
                        g.mark("vec", ("p4a", "ev", k))

                def p4a_pe(g):
                    g.wait("gp", ("p0", "const"))
                    g.wait("act", ("p3b", "qev", NQB * HT - 1))
                    for k in range(NK):
                        g.wait("vec", ("p4a", "cast", k))
                        if k >= 2:
                            g.wait("vec", ("p4a", "ev", k - 2))
                        for ht in range(HT):
                            g.do(lambda e, k=k, ht=ht: e.transpose(
                                gt_ps[k % 2][:, ht * P:(ht + 1) * P],
                                gubf[k % 2][:, ht * P:(ht + 1) * P], ident[:]))
                        g.inc("pe", 1, ("p4a", "tr", k))

                def p4a_act(g):
                    for k in range(NK):
                        it, isup = k // 2, k % 2
                        g.wait("vec", ("p4a", "ev", k))
                        dst = ut_d if isup else gt_d
                        g.dma(lambda e, k=k, it=it, dst=dst: e.dma_start(
                            out=dst[it], in_=gu_stage[k % 2][:]))
                        g.inc(f"st{k % 2}", 16, ("p4a", "st", k))

                run_phase(block, ph, {"sync": p4a_sync, "vector": p4a_vec,
                                      "tensor": p4a_pe, "scalar": p4a_act})

        with ExitStack() as C2:
            E2 = C2.enter_context
            dw_st = [E2(nc.sbuf_tensor(f"dw_st{i}", [P, I5], F32)) for i in range(4)]
            dwbf = [E2(nc.sbuf_tensor(f"dwbf{i}", [P, I5], BF16)) for i in range(4)]
            dt_stage = [E2(nc.sbuf_tensor(f"dt_stage{i}", [P, 512], BF16))
                        for i in range(2)]
            with ExitStack() as CP:
                dt_ps = [CP.enter_context(nc.psum_tensor(f"dt_ps{i}", [P, 512], BF16))
                         for i in range(2)]

                def p4b_sync(g):
                    for h4 in range(4):
                        for kk in range(4):
                            if h4 >= 1:
                                g.wait("pe", ("p4b", "tr", h4 - 1, ITI - 1))
                            g.dma(lambda e, h4=h4, kk=kk: e.dma_start(
                                out=dw_st[kk][:],
                                in_=dw[(h4 * 4 + kk) * P:(h4 * 4 + kk + 1) * P, :]))
                            g.inc(f"dw{kk}", 16, ("p4b", "wl", h4, kk))

                def p4b_vec(g):
                    for h4 in range(4):
                        for kk in range(4):
                            g.wait(f"dw{kk}", ("p4b", "wl", h4, kk))
                            if h4 >= 1:
                                g.wait("pe", ("p4b", "tr", h4 - 1, ITI - 1))
                            g.do(lambda e, h4=h4, kk=kk: e.tensor_scalar_mul(
                                dwbf[kk][:], dw_st[kk][:],
                                dsc_sb[:, h4 * 4 + kk:h4 * 4 + kk + 1]))
                            g.mark("vec", ("p4b", "cast", h4, kk))
                        for it in range(ITI):
                            f = h4 * ITI + it
                            g.wait("pe", ("p4b", "tr", h4, it))
                            if f >= 2:
                                g.wait(f"st{f % 2}", ("p4b", "st", f - 2))
                            g.do(lambda e, it=it: e.tensor_copy(
                                dt_stage[it % 2][:], dt_ps[it % 2][:]))
                            g.mark("vec", ("p4b", "ev", f))

                def p4b_pe(g):
                    g.wait("gp", ("p0", "const"))
                    g.wait("vec", ("p4a", "ev", 2 * ITI - 1))
                    for h4 in range(4):
                        for kk in range(4):
                            g.wait("vec", ("p4b", "cast", h4, kk))
                        for it in range(ITI):
                            f = h4 * ITI + it
                            if f >= 2:
                                g.wait("vec", ("p4b", "ev", f - 2))
                            for kk in range(4):
                                g.do(lambda e, it=it, kk=kk: e.transpose(
                                    dt_ps[it % 2][:, kk * P:(kk + 1) * P],
                                    dwbf[kk][:, it * P:(it + 1) * P], ident[:]))
                            g.inc("pe", 1, ("p4b", "tr", h4, it))

                def p4b_act(g):
                    for h4 in range(4):
                        for it in range(ITI):
                            f = h4 * ITI + it
                            g.wait("vec", ("p4b", "ev", f))
                            g.dma(lambda e, h4=h4, it=it: e.dma_start(
                                out=dt_d[it, :, h4 * 512:(h4 + 1) * 512],
                                in_=dt_stage[it % 2][:]))
                            g.inc(f"st{f % 2}", 16, ("p4b", "st", f))

                run_phase(block, ph, {"sync": p4b_sync, "vector": p4b_vec,
                                      "tensor": p4b_pe, "scalar": p4b_act})

        with ExitStack() as C3:
            E3 = C3.enter_context
            xnT_b = E3(nc.sbuf_tensor("xnT_b", [P, 2 * 16 * 512], BF16))
            hdn = E3(nc.sbuf_tensor("hdn", [P, ITI * 1024], BF16))
            gt_t = [E3(nc.sbuf_tensor(f"gt_t{i}", [P, D], BF16)) for i in range(2)]
            ut_t = [E3(nc.sbuf_tensor(f"ut_t{i}", [P, D], BF16)) for i in range(2)]
            silu_g = [E3(nc.sbuf_tensor(f"silu_g{i}", [P, 1024], F32))
                      for i in range(2)]
            dt_t = [E3(nc.sbuf_tensor(f"dt_t{i}", [P, D], BF16)) for i in range(4)]
            xo = [E3(nc.sbuf_tensor(f"xo{i}", [P, D], F32)) for i in range(2)]
            part_sb = [E3(nc.sbuf_tensor(f"part_sb{i}", [P, D], F32))
                       for i in range(2)]
            with ExitStack() as CP:
                pgu = CP.enter_context(nc.psum_tensor("pgu", [P, D], F32))
                pd = CP.enter_context(nc.psum_tensor("pd", [P, D], F32))
                NBLK = TC // 1024

                def p5_gp(g):
                    for blk in range(NBLK):
                        if blk >= 1:
                            g.wait("pe", ("p5", "gu", blk - 1, 2 * ITI - 1))
                        g.dma(lambda e, blk=blk: e.dma_start(
                            out=xnT_b[:, 0:8192], in_=xnt_d[2 * blk]))
                        g.inc("xb", 16)
                        g.dma(lambda e, blk=blk: e.dma_start(
                            out=xnT_b[:, 8192:16384], in_=xnt_d[2 * blk + 1]))
                        g.inc("xb", 16, ("p5", "xb", blk))
                        for pr in range(4):
                            if (blk, pr) != (0, 0):
                                pblk, ppr = (blk, pr - 1) if pr else (blk - 1, 3)
                                g.wait("vec", ("p5", "pev", pblk, ppr))
                            for e2 in range(2):
                                g128 = blk * 8 + 2 * pr + e2
                                g.dma(lambda e, g128=g128, e2=e2: e.dma_start(
                                    out=xo[e2][:],
                                    in_=x_full[g128 * P:(g128 + 1) * P, :]))
                                g.inc("xo", 16,
                                      ("p5", "xo", blk, pr) if e2 else None)
                            g.wait("vec", ("p5", "pev", blk, pr))
                            for e2 in range(2):
                                g128 = blk * 8 + 2 * pr + e2
                                g.dma(lambda e, g128=g128, e2=e2: e.dma_start(
                                    out=part_d[g128], in_=part_sb[e2][:]))
                                g.inc("pw", 16,
                                      ("p5", "pw", blk, pr) if e2 else None)

                def p5_sync(g):
                    for blk in range(NBLK):
                        for k in range(2 * ITI):
                            it, isup = k // 2, k % 2
                            if k >= 4 or blk > 0:
                                pk = (blk, k - 4) if k >= 4 else (blk - 1, 2 * ITI + k - 4)
                                g.wait("pe", ("p5", "gu", pk[0], pk[1]))
                            src = ut_d if isup else gt_d
                            buf = (ut_t if isup else gt_t)[it % 2]
                            g.dma(lambda e, src=src, it=it, buf=buf: e.dma_start(
                                out=buf[:], in_=src[it]))
                            g.inc(f"{'su' if isup else 'sg'}{(it) % 2}", 16,
                                  ("p5", "gu_l", blk, k))
                        for pr in range(4):
                            for it in range(0, ITI, 2):
                                if it >= 4 or (blk, pr) != (0, 0):
                                    if it >= 4:
                                        pt = ("p5", "dn", blk, pr, it - 4)
                                    elif pr:
                                        pt = ("p5", "dn", blk, pr - 1, ITI - 4 + it)
                                    else:
                                        pt = ("p5", "dn", blk - 1, 3, ITI - 4 + it)
                                    g.wait("pe", pt)
                                g.dma(lambda e, it=it: e.dma_start(
                                    out=dt_t[(it // 2) % 2][:], in_=dt_d[it]))
                                g.inc(f"dts{(it // 2) % 2}", 16,
                                      ("p5", "dt", blk, pr, it))

                def p5_act(g):
                    for blk in range(NBLK):
                        for it in range(ITI):
                            g.wait("pe", ("p5", "gu", blk, 2 * it))
                            if it >= 2:
                                g.wait("vec", ("p5", "hdn", blk, it - 2))
                            g.do(lambda e, it=it: e.activation(
                                silu_g[it % 2][:], pgu[:, 0:1024], AF.Silu))
                            g.mark("act", ("p5", "silu", blk, it))
                        for pr in range(4):
                            for it in range(1, ITI, 2):
                                if it >= 5 or (blk, pr) != (0, 0):
                                    if it >= 5:
                                        pt = ("p5", "dn", blk, pr, it - 4)
                                    elif pr:
                                        pt = ("p5", "dn", blk, pr - 1, ITI - 4 + it)
                                    else:
                                        pt = ("p5", "dn", blk - 1, 3, ITI - 4 + it)
                                    g.wait("pe", pt)
                                g.dma(lambda e, it=it: e.dma_start(
                                    out=dt_t[2 + ((it - 1) // 2) % 2][:],
                                    in_=dt_d[it]))
                                g.inc(f"dta{((it - 1) // 2) % 2}", 16,
                                      ("p5", "dta", blk, pr, it))

                def p5_pe(g):
                    g.wait("vec", ("p4b", "ev", 4 * ITI - 1))
                    for blk in range(NBLK):
                        g.wait("xb", ("p5", "xb", blk))
                        for k in range(2 * ITI):
                            it, isup = k // 2, k % 2
                            g.wait(f"{'su' if isup else 'sg'}{it % 2}",
                                   ("p5", "gu_l", blk, k))
                            if isup == 0:
                                if it >= 1:
                                    g.wait("act", ("p5", "silu", blk, it - 1))
                                elif blk >= 1:
                                    g.wait("vec", ("p5", "pev", blk - 1, 3))
                            else:
                                if it >= 1:
                                    g.wait("vec", ("p5", "hdn", blk, it - 1))
                            buf = (ut_t if isup else gt_t)[it % 2]
                            dst = pgu[:, 1024:2048] if isup else pgu[:, 0:1024]
                            for ht in range(HT):
                                for tc in range(2):
                                    g.do(lambda e, dst=dst, buf=buf, ht=ht,
                                         tc=tc: e.matmul(
                                             dst[:, tc * 512:(tc + 1) * 512],
                                             buf[:, ht * P:(ht + 1) * P],
                                             xnT_b[:, tc * 8192 + ht * 512:
                                                   tc * 8192 + (ht + 1) * 512],
                                             start=(ht == 0), stop=(ht == HT - 1)))
                            g.inc("pe", 1, ("p5", "gu", blk, k))
                        for pr in range(4):
                            for it in range(ITI):
                                buf = dt_t[(it // 2) % 2] if it % 2 == 0 else \
                                    dt_t[2 + ((it - 1) // 2) % 2]
                                if it % 2 == 0:
                                    g.wait(f"dts{(it // 2) % 2}",
                                           ("p5", "dt", blk, pr, it))
                                else:
                                    g.wait(f"dta{((it - 1) // 2) % 2}",
                                           ("p5", "dta", blk, pr, it))
                                if it == 0:
                                    if (blk, pr) != (0, 0):
                                        pblk, ppr = (blk, pr - 1) if pr else (blk - 1, 3)
                                        g.wait("vec", ("p5", "pev", pblk, ppr))
                                    if pr == 0:
                                        g.wait("vec", ("p5", "hdn", blk, ITI - 1))
                                for e2 in range(2):
                                    tt8 = 2 * pr + e2
                                    dst = pd if e2 == 0 else pgu
                                    hsl = hdn[:, it * 1024 + tpos(tt8):
                                              it * 1024 + tpos(tt8) + P]
                                    for j in range(4):
                                        g.do(lambda e, dst=dst, hsl=hsl, j=j,
                                             buf=buf: e.matmul(
                                                 dst[:, j * 512:(j + 1) * 512],
                                                 hsl, buf[:, j * 512:(j + 1) * 512],
                                                 start=(it == 0),
                                                 stop=(it == ITI - 1)))
                                g.inc("pe", 1, ("p5", "dn", blk, pr, it))

                def p5_vec(g):
                    for blk in range(NBLK):
                        for it in range(ITI):
                            g.wait("act", ("p5", "silu", blk, it))
                            g.wait("pe", ("p5", "gu", blk, 2 * it + 1))
                            if it == 0 and blk >= 1:
                                g.wait("pe", ("p5", "dn", blk - 1, 3, ITI - 1))
                            g.do(lambda e, it=it: e.tensor_mul(
                                hdn[:, it * 1024:(it + 1) * 1024],
                                silu_g[it % 2][:], pgu[:, 1024:2048]))
                            g.mark("vec", ("p5", "hdn", blk, it))
                        for pr in range(4):
                            g.wait("pe", ("p5", "dn", blk, pr, ITI - 1))
                            g.wait("xo", ("p5", "xo", blk, pr))
                            if (blk, pr) != (0, 0):
                                pblk, ppr = (blk, pr - 1) if pr else (blk - 1, 3)
                                g.wait("pw", ("p5", "pw", pblk, ppr))
                            g.do(lambda e: e.tensor_add(
                                part_sb[0][:], pd[:], xo[0][:]))
                            g.do(lambda e: e.tensor_add(
                                part_sb[1][:], pgu[:], xo[1][:]))
                            g.mark("vec", ("p5", "pev", blk, pr))

                run_phase(block, ph, {"gpsimd": p5_gp, "sync": p5_sync,
                                      "scalar": p5_act, "tensor": p5_pe,
                                      "vector": p5_vec})

        with ExitStack() as C4:
            E4 = C4.enter_context
            QTb = E4(nc.sbuf_tensor("QTb", [P, HT * 512], BF16))
            Kc = [E4(nc.sbuf_tensor(f"Kc{i}", [P, HT * KCH], BF16))
                  for i in range(2)]
            Vc = [E4(nc.sbuf_tensor(f"Vc{i}", [P, 8 * D], BF16))
                  for i in range(2)]
            acc = E4(nc.sbuf_tensor("acc", [P, 4 * D], F32))
            P_sb = [E4(nc.sbuf_tensor(f"P_sb{i}", [P, CH], BF16)) for i in range(2)]
            PT_sb = [E4(nc.sbuf_tensor(f"PT_sb{i}", [P, CH], BF16)) for i in range(2)]
            part6 = [E4(nc.sbuf_tensor(f"part6{i}", [P, D], F32))
                     for i in range(2)]
            cnm = [E4(nc.sbuf_tensor(f"cnm{i}", [P, 1], F32)) for i in range(2)]
            nm = [[E4(nc.sbuf_tensor(f"nm{q}_{i}", [P, 1], F32)) for i in range(2)]
                  for q in range(4)]
            delta = [E4(nc.sbuf_tensor(f"delta{i}", [P, 1], F32)) for i in range(2)]
            alpha = [E4(nc.sbuf_tensor(f"alpha{i}", [P, 1], F32)) for i in range(2)]
            csum_a = [E4(nc.sbuf_tensor(f"csum_a{i}", [P, 1], F32)) for i in range(2)]
            csum_b = [E4(nc.sbuf_tensor(f"csum_b{i}", [P, 1], F32)) for i in range(2)]
            dsum = [E4(nc.sbuf_tensor(f"dsum{q}", [P, 1], F32)) for q in range(4)]
            rcp = E4(nc.sbuf_tensor("rcp", [P, 1], F32))

            NPAIR = NQB * NKC * 4

            def pcoord(j):
                bcp, qt = j // 4, j % 4
                return bcp // NKC, bcp % NKC, qt

            with ExitStack() as CP:
                S_ps = [CP.enter_context(nc.psum_tensor(f"S_ps{i}", [P, CH], F32))
                        for i in range(2)]
                PT_ps = [CP.enter_context(nc.psum_tensor(f"PT_ps{i}", [P, CH], BF16))
                         for i in range(2)]
                AV_ps = CP.enter_context(nc.psum_tensor("AV_ps", [P, D], F32))

                def p6_sync(g):
                    for gg in range(NQB * NKC):
                        cp = gg % NKC
                        if gg >= 2:
                            g.wait("pe", ("p6", "av", gg * 4 - 5))
                        g.dma(lambda e, gg=gg, cp=cp: e.dma_start(
                            out=Kc[gg % 2][:], in_=k_d[cp]))
                        g.inc(f"k6{gg % 2}", 16, ("p6", "k", gg))
                        g.dma(lambda e, gg=gg, cp=cp: e.dma_start(
                            out=Vc[gg % 2][:], in_=v_d[cp]))
                        g.inc(f"v6{gg % 2}", 16, ("p6", "v", gg))

                def p6_gp(g):
                    def part_load(b, qt):
                        if qt >= 2:
                            g.wait(f"ot6{qt % 2}", ("p6", "out", b, qt - 2))
                        elif b >= 1:
                            g.wait(f"ot6{qt % 2}", ("p6", "out", b - 1, qt + 2))
                        g.dma(lambda e, b=b, qt=qt: e.dma_start(
                            out=part6[qt % 2][:], in_=part_d[b * 4 + qt]))
                        g.inc(f"pt6{qt % 2}", 16, ("p6", "part", b, qt))

                    def out_store(b, qt):
                        g.wait("vec", ("p6", "fin", b, qt))
                        g.dma(lambda e, b=b, qt=qt: e.dma_start(
                            out=out_e[(b * 4 + qt) * P:(b * 4 + qt + 1) * P, :],
                            in_=part6[qt % 2][:]))
                        g.inc(f"ot6{qt % 2}", 16, ("p6", "out", b, qt))

                    def fin_block(b):
                        out_store(b, 0)
                        part_load(b, 2)
                        out_store(b, 1)
                        part_load(b, 3)
                        out_store(b, 2)
                        out_store(b, 3)

                    for b in range(NQB):
                        if b >= 1:
                            g.wait("pe", ("p6", "sc", b * NKC * 4 - 1))
                        g.dma(lambda e, b=b: e.dma_start(
                            out=QTb[:], in_=qt_d[b]))
                        g.inc(f"qt6{b % 2}", 16, ("p6", "qt", b))
                        if b >= 1:
                            fin_block(b - 1)
                        part_load(b, 0)
                        part_load(b, 1)
                    fin_block(NQB - 1)
                    if debug:
                        for nm_, (dap, sap) in dbg.items():
                            g.dma(lambda e, dap=dap, sap=sap: e.dma_start(
                                out=dap[:], in_=sap[:]))
                            g.inc("dgp_misc", 16, ("p6", "dbg", nm_))

                def p6_pe(g):
                    g.wait("gp", ("p0", "const"))
                    g.wait("vec", ("p5", "pev", NBLK - 1, 3))

                    def sc2(j):
                        b, cp, qt = pcoord(j)
                        gg = b * NKC + cp
                        g.wait(f"k6{gg % 2}", ("p6", "k", gg))
                        if qt == 0 and cp == 0:
                            g.wait(f"qt6{b % 2}", ("p6", "qt", b))
                        if j >= 1:
                            g.wait("act", ("p6", "expb", j - 1))
                        for ot in range(HT):
                            for e2 in range(2):
                                g.do(lambda e, gg=gg, qt=qt, ot=ot, e2=e2:
                                     e.matmul(
                                         S_ps[e2][:],
                                         QTb[:, ot * 512 + qt * P:
                                             ot * 512 + (qt + 1) * P],
                                         Kc[gg % 2][:, ot * KCH + e2 * 512:
                                                    ot * KCH + (e2 + 1) * 512],
                                         start=(ot == 0), stop=(ot == HT - 1)))
                        g.inc("pe", 1, ("p6", "sc", j))

                    def trans(j, e2):
                        g.wait("act", ("p6", "expa" if e2 == 0 else "expb", j))
                        if j >= 1:
                            g.wait("act", ("p6", "ptva" if e2 == 0 else "ptvb",
                                           j - 1))
                        for j4 in range(4):
                            g.do(lambda e, e2=e2, j=j, j4=j4: e.transpose(
                                PT_ps[e2][:, j4 * P:(j4 + 1) * P],
                                P_sb[e2][:, j4 * P:(j4 + 1) * P],
                                ident[:]))
                        g.inc("pe", 1, ("p6", "tra" if e2 == 0 else "trb", j))

                    def av(j, e2):
                        b, cp, qt = pcoord(j)
                        gg = b * NKC + cp
                        g.wait("act", ("p6", "ptva" if e2 == 0 else "ptvb", j))
                        if e2 == 0:
                            g.wait(f"v6{gg % 2}", ("p6", "v", gg))
                            if j >= 1:
                                g.wait("vec", ("p6", "fl", j - 1))
                        for kk in range(4):
                            kv = e2 * 4 + kk
                            for j4 in range(4):
                                g.do(lambda e, e2=e2, gg=gg, kv=kv, kk=kk, j4=j4:
                                     e.matmul(
                                         AV_ps[:, j4 * 512:(j4 + 1) * 512],
                                         PT_sb[e2][:, kk * P:(kk + 1) * P],
                                         Vc[gg % 2][:, kv * D + j4 * 512:
                                                    kv * D + (j4 + 1) * 512],
                                         start=(e2 == 0 and kk == 0),
                                         stop=(e2 == 1 and kk == 3)))
                        if e2 == 1:
                            g.inc("pe", 1, ("p6", "av", j))
                        else:
                            g.inc("pe", 1, ("p6", "ava", j))

                    sc2(0)
                    for j in range(NPAIR):
                        if j + 1 < NPAIR:
                            sc2(j + 1)
                        trans(j, 0)
                        trans(j, 1)
                        av(j, 0)
                        av(j, 1)

                def p6_vec(g):
                    def nm_ops(j):
                        b, cp, qt = pcoord(j)
                        g.wait("pe", ("p6", "sc", j))
                        g.do(lambda e, j=j: e.reduce_max(
                            cnm[0][:], S_ps[0][:], axis=AX.X, negate=True))
                        g.do(lambda e, j=j: e.reduce_max(
                            cnm[1][:], S_ps[1][:], axis=AX.X, negate=True))
                        g.do(lambda e: e.tensor_tensor(
                            cnm[0][:], cnm[0][:], cnm[1][:], ALU.min))
                        if cp == 0:
                            g.do(lambda e, qt=qt: e.tensor_copy(
                                nm[qt][0][:], cnm[0][:]))
                        else:
                            g.do(lambda e, qt=qt, cp=cp: e.tensor_tensor(
                                nm[qt][cp % 2][:], nm[qt][(cp + 1) % 2][:],
                                cnm[0][:], ALU.min))
                            g.do(lambda e, qt=qt, cp=cp, j=j: e.tensor_sub(
                                delta[j % 2][:], nm[qt][cp % 2][:],
                                nm[qt][(cp + 1) % 2][:]))
                        g.mark("vec", ("p6", "nm", j))

                    def fl_ops(j):
                        b, cp, qt = pcoord(j)
                        aq = acc[:, qt * D:(qt + 1) * D]
                        g.wait("pe", ("p6", "av", j))
                        g.wait("act", ("p6", "expb", j))
                        if cp == 0:
                            g.do(lambda e, aq=aq: e.tensor_copy(aq, AV_ps[:]))
                            g.do(lambda e, qt=qt, j=j: e.tensor_add(
                                dsum[qt][:], csum_a[j % 2][:], csum_b[j % 2][:]))
                        else:
                            g.do(lambda e, aq=aq, j=j: e.scalar_tensor_tensor(
                                aq, aq, alpha[j % 2][:, 0:1], AV_ps[:],
                                ALU.mult, ALU.add))
                            g.do(lambda e, qt=qt, j=j: e.scalar_tensor_tensor(
                                dsum[qt][:], dsum[qt][:], alpha[j % 2][:, 0:1],
                                csum_a[j % 2][:], ALU.mult, ALU.add))
                            g.do(lambda e, qt=qt, j=j: e.tensor_add(
                                dsum[qt][:], dsum[qt][:], csum_b[j % 2][:]))
                        g.mark("vec", ("p6", "fl", j))
                        if cp == NKC - 1 and qt == 3:
                            for q2 in range(4):
                                g.do(lambda e, q2=q2: e.reciprocal(
                                    rcp[:], dsum[q2][:]))
                                g.wait(f"pt6{q2 % 2}", ("p6", "part", b, q2))
                                g.do(lambda e, q2=q2: e.scalar_tensor_tensor(
                                    part6[q2 % 2][:],
                                    acc[:, q2 * D:(q2 + 1) * D],
                                    rcp[:, 0:1], part6[q2 % 2][:],
                                    ALU.mult, ALU.add))
                                g.mark("vec", ("p6", "fin", b, q2))

                    nm_ops(0)
                    for j in range(NPAIR):
                        if j + 1 < NPAIR:
                            nm_ops(j + 1)
                        fl_ops(j)

                def p6_act(g):
                    for j in range(NPAIR):
                        b, cp, qt = pcoord(j)
                        g.wait("vec", ("p6", "nm", j))
                        if cp > 0:
                            g.do(lambda e, j=j: e.activation(
                                alpha[j % 2][:], delta[j % 2][:], AF.Exp))
                        g.do(lambda e, j=j, qt=qt, cp=cp: e.activation(
                            P_sb[0][:], S_ps[0][:], AF.Exp,
                            bias=nm[qt][cp % 2][:, 0:1],
                            accum_out=csum_a[j % 2][:, 0:1]))
                        g.mark("act", ("p6", "expa", j))
                        g.do(lambda e, j=j, qt=qt, cp=cp: e.activation(
                            P_sb[1][:], S_ps[1][:], AF.Exp,
                            bias=nm[qt][cp % 2][:, 0:1],
                            accum_out=csum_b[j % 2][:, 0:1]))
                        g.mark("act", ("p6", "expb", j))
                        g.wait("pe", ("p6", "tra", j))
                        g.do(lambda e, j=j: e.copy(PT_sb[0][:], PT_ps[0][:]))
                        g.mark("act", ("p6", "ptva", j))
                        g.wait("pe", ("p6", "trb", j))
                        g.do(lambda e, j=j: e.copy(PT_sb[1][:], PT_ps[1][:]))
                        g.mark("act", ("p6", "ptvb", j))

                run_phase(block, ph, {"sync": p6_sync, "gpsimd": p6_gp,
                                      "tensor": p6_pe, "vector": p6_vec,
                                      "scalar": p6_act})

    return nc


_N_CORES = 8


def kernel(L0_attn_input, L0_attn_Wq, L0_attn_bq, L0_attn_Wk, L0_attn_bk,
           L0_attn_Wv, L0_attn_bv, L0_ffn_gate_w, L0_ffn_up_w, L0_ffn_down_w,
           L0_ffn_gate_s, L0_ffn_up_s, L0_ffn_down_s):
    x = np.ascontiguousarray(np.asarray(L0_attn_input, dtype=np.float32))
    weights = {
        "wq": np.ascontiguousarray(np.asarray(L0_attn_Wq, np.float32)),
        "wk": np.ascontiguousarray(np.asarray(L0_attn_Wk, np.float32)),
        "wv": np.ascontiguousarray(np.asarray(L0_attn_Wv, np.float32)),
        "bq": np.ascontiguousarray(np.asarray(L0_attn_bq, np.float32)),
        "bk": np.ascontiguousarray(np.asarray(L0_attn_bk, np.float32)),
        "bv": np.ascontiguousarray(np.asarray(L0_attn_bv, np.float32)),
        "gw": np.ascontiguousarray(np.asarray(L0_ffn_gate_w, np.float32)),
        "uw": np.ascontiguousarray(np.asarray(L0_ffn_up_w, np.float32)),
        "dw": np.ascontiguousarray(np.asarray(L0_ffn_down_w, np.float32)),
        "gs": np.ascontiguousarray(np.asarray(L0_ffn_gate_s, np.float32)),
        "us": np.ascontiguousarray(np.asarray(L0_ffn_up_s, np.float32)),
        "ds": np.ascontiguousarray(np.asarray(L0_ffn_down_s, np.float32)),
    }
    in_maps = []
    for c in range(_N_CORES):
        b, s = c // 4, c % 4
        xb = x[b]
        xr = np.ascontiguousarray(
            np.concatenate([xb[s * TC:], xb[:s * TC]], axis=0))
        m = {"x_full": xr}
        m.update(weights)
        in_maps.append(m)

    nc = build_program()
    res = run_bass_kernel_spmd(nc, in_maps, core_ids=list(range(_N_CORES)))

    out = np.empty((2, SEQ, D), dtype=np.float32)
    for c in range(_N_CORES):
        b, s = c // 4, c % 4
        out[b, s * TC:(s + 1) * TC, :] = res.results[c]["out"]
    return out
